# version 02
# baseline (speedup 1.0000x reference)
from contextlib import ExitStack

import numpy as np

import concourse.bass as bass
import concourse.mybir as mybir
from concourse.bass_types import AP
from concourse.bass_core_rust import IndirectOffsetOnAxis
from concourse.bass_utils import run_bass_kernel_spmd

F32 = mybir.dt.float32
BF16 = mybir.dt.bfloat16
I32 = mybir.dt.int32
U32 = mybir.dt.uint32
AF = mybir.ActivationFunctionType
ALU = mybir.AluOpType
AXX = mybir.AxisListType.X

D = 1024
T = 4096
NDT = D // 128

ENGINES = ("sync", "tensor", "scalar", "vector", "gpsimd")
ENG_SEM = {"sync": "sE", "tensor": "tE", "scalar": "aE", "vector": "vE",
           "gpsimd": "gE"}

C_EXPB, C_ONE, C_QS, C_ZERO, C_EPS = range(5)
CONST_VALS = [-20.0, 1.0, 0.125, 0.0, 1e-5]

DEV = {}
SELF_EDGE = True


class SemPool:

    def __init__(self, nc, stack):
        self.nc = nc
        self.stack = stack
        self.handles = {}

    def get(self, name):
        if name not in self.handles:
            self.handles[name] = self.stack.enter_context(self.nc.semaphore(name))
        return self.handles[name]


class Ctx:

    def __init__(self, active, eng, pool, base):
        self.active = active
        self.eng = eng
        self.pool = pool
        self.cnt = dict(base)

    def val(self, s):
        return self.cnt.get(s, 0)

    def op(self, engname, fn, w=(), inc=None, n=1):
        tgt = None
        if inc is not None:
            self.cnt[inc] = self.cnt.get(inc, 0) + n
            tgt = self.cnt[inc]
        if engname == self.active:
            for s, v in w:
                if v > 0:
                    self.eng.wait_ge(self.pool.get(s), v)
            ins = fn(self.eng)
            if inc is not None:
                ins.then_inc(self.pool.get(inc), n)
        return tgt

    def t(self, fn, w=(), inc=None):
        return self.op("tensor", fn, w, inc)

    def a(self, fn, w=(), inc="aE"):
        if SELF_EDGE:
            w = list(w) + [("aE", self.val("aE"))]
        return self.op("scalar", fn, w, inc)

    def v(self, fn, w=(), inc="vE"):
        if SELF_EDGE:
            w = list(w) + [("vE", self.val("vE"))]
        return self.op("vector", fn, w, inc)

    def g(self, fn, w=(), inc="gE"):
        if SELF_EDGE:
            w = list(w) + [("gE", self.val("gE"))]
        return self.op("gpsimd", fn, w, inc)

    def dma(self, q, sem, out_ap, in_ap, w=(), **kw):
        return self.op(q, lambda e: e.dma_start(out_ap, in_ap, **kw), w, sem, 16)

    def idma(self, sem, out_ap, in_ap, in_off, w=()):
        return self.op("gpsimd",
                       lambda e: e.indirect_dma_start(out_ap, None, in_ap,
                                                      in_off),
                       w, sem, 16)

    def bar(self):
        bname = f"bar{self.cnt.get('_barid', 0)}"
        self.cnt["_barid"] = self.cnt.get("_barid", 0) + 1
        for e in ENGINES:
            self.op(e, lambda x: x.nop(), (), bname, 1)
        waits = [(bname, 5)]
        waits += [(s, v) for s, v in sorted(self.cnt.items())
                  if s not in ENG_SEM.values() and not s.startswith("bar")
                  and s != "_barid" and v > 0]
        for e in ENGINES:
            self.op(e, lambda x: x.nop(), waits)


def run_phase(block, pool, base_cnt, phase_logic):
    finals = []
    for engname in ENGINES:
        holder = {}

        def section(eng, engname=engname, holder=holder):
            c = Ctx(engname, eng, pool, base_cnt)
            phase_logic(c)
            holder["cnt"] = c.cnt

        getattr(block, engname)(section)
        finals.append(holder["cnt"])
    for f in finals[1:]:
        assert f == finals[0], "phase emission diverged between engines"
    return finals[0]



def g_of_d(d):
    idx = d + 2047
    idx = np.where(idx < 0, idx + 4095, idx)
    return np.clip(idx, 0, 4094)


def host_prep(inputs):
    f = np.float32

    def t(a):
        return np.ascontiguousarray(np.asarray(a).astype(f).T)

    sh = {}
    sh["WqT"] = t(inputs["L0_attn_Wq"])
    sh["WkT"] = t(inputs["L0_attn_Wk"])
    sh["WvT"] = t(inputs["L0_attn_Wv"])
    sh["WoT"] = t(inputs["L0_attn_Wo"])
    sh["bqkvo"] = np.stack([np.asarray(inputs["L0_attn_bq"]),
                            np.asarray(inputs["L0_attn_bk"]),
                            np.asarray(inputs["L0_attn_bv"]),
                            np.asarray(inputs["L0_attn_bo"])]).astype(f)
    RT = np.asarray(inputs["L0_attn_rel_emb"])[g_of_d(np.arange(-4095, 4096))]
    sh["RT"] = np.ascontiguousarray(RT.T.astype(f))
    sh["Wg"] = np.asarray(inputs["L0_ffn_Wg"]).astype(f)
    sh["bg"] = np.asarray(inputs["L0_ffn_bg"]).astype(f).reshape(1, 8)
    sh["W1"] = np.asarray(inputs["L0_ffn_W1"]).astype(f)
    sh["b1r"] = np.ascontiguousarray(np.asarray(inputs["L0_ffn_b1"]).astype(f)
                                     .reshape(8, 32, 128).transpose(2, 0, 1)
                                     .reshape(128, 256))
    sh["W2"] = np.asarray(inputs["L0_ffn_W2"]).astype(f)
    sh["b2r"] = np.ascontiguousarray(np.asarray(inputs["L0_ffn_b2"]).astype(f)
                                     .reshape(8, 8, 128).transpose(2, 0, 1)
                                     .reshape(128, 64))
    sh["wT"] = t(inputs["L1_attn_w"])
    sh["w_out"] = np.asarray(inputs["L1_attn_w_out"]).astype(f)
    sh["WupT"] = t(inputs["L1_ffn_W_up"])
    sh["WdnT"] = t(inputs["L1_ffn_W_down"])
    sh["Wq2T"] = t(inputs["L1_ffn_W_q"])
    sh["keysT"] = np.ascontiguousarray(
        np.asarray(inputs["L1_ffn_keys"]).astype(f).transpose(2, 0, 3, 1))
    sh["de"] = np.asarray(inputs["L1_ffn_down_embed"]).astype(f)
    sh["ue"] = np.asarray(inputs["L1_ffn_up_embed"]).astype(f)
    sh["consts"] = np.tile(np.array(CONST_VALS, f)[None, :], (128, 1))
    return sh


def declare_inputs(nc):
    d = {}
    for name, shape in [
        ("xT", [D, T]),
        ("WqT", [D, D]), ("WkT", [D, D]), ("WvT", [D, D]), ("WoT", [D, D]),
        ("bqkvo", [4, D]), ("RT", [64, 8191]),
        ("Wg", [D, 8]), ("bg", [1, 8]), ("W1", [8, D, 4096]),
        ("b1r", [128, 256]), ("W2", [8, 4096, D]), ("b2r", [128, 64]),
        ("wT", [D, 3 * D]), ("w_out", [D, D]),
        ("WupT", [D, 4096]), ("WdnT", [4096, D]), ("Wq2T", [D, 512]),
        ("keysT", [2, 4, 64, 64]), ("de", [4096, D]), ("ue", [4096, D]),
        ("consts", [128, 5]),
    ]:
        d[name] = nc.dram_tensor(name, shape, F32, kind="ExternalInput")
    return d



class Common:
    def __init__(self, nc, E):
        self.consts = E(nc.sbuf_tensor("cc_consts", [128, 5], F32))
        self.ones_col_f = E(nc.sbuf_tensor("cc_onc_f", [128, 1], F32))
        self.ones_col_b = E(nc.sbuf_tensor("cc_onc_b", [128, 1], BF16))
        self.ones_row_f = E(nc.sbuf_tensor("cc_onr_f", [1, 128], F32))
        self.ones_row_f512 = E(nc.sbuf_tensor("cc_onr_f5", [1, 512], F32))
        self.ident_b = E(nc.sbuf_tensor("cc_id_b", [128, 128], BF16))
        self.ident_f = E(nc.sbuf_tensor("cc_id_f", [128, 128], F32))
        self.iota_d = E(nc.sbuf_tensor("cc_iod", [128, 128], F32))
        self.iota8 = E(nc.sbuf_tensor("cc_io8", [128, 8], F32))
        self.iota64 = E(nc.sbuf_tensor("cc_io64", [128, 64], F32))
        self.tril_b = E(nc.sbuf_tensor("cc_tril", [128, 128], BF16))
        self.ones2 = E(nc.sbuf_tensor("cc_on2", [128, 2], BF16))
        self.sel2T = E(nc.sbuf_tensor("cc_s2t", [2, 128], F32))
        self.ones_t128 = E(nc.sbuf_tensor("cc_o128", [128, 128], F32))

    def c(self, col):
        return self.consts[:, col:col + 1]

    def c1(self, col):
        return self.consts[0:1, col:col + 1]


def emit_common(c, cm, din):
    c.dma("sync", "cs", cm.consts[:, :], din["consts"].ap())
    c.op("vector", lambda e: e.nop(), [("cs", c.val("cs"))])
    c.v(lambda e: e.memset(cm.ones_col_f[:, :], 1.0))
    c.v(lambda e: e.memset(cm.ones_col_b[:, :], 1.0))
    c.v(lambda e: e.memset(cm.ones_row_f[:, :], 1.0))
    c.v(lambda e: e.memset(cm.ones_row_f512[:, :], 1.0))
    c.v(lambda e: e.memset(cm.ones2[:, :], 0.0))
    c.v(lambda e: e.memset(cm.ones2[0:64, 0:1], 1.0))
    c.v(lambda e: e.memset(cm.ones2[64:128, 1:2], 1.0))
    c.g(lambda e: e.iota(cm.iota_d[:, :], pattern=[[1, 128]], base=0,
                         channel_multiplier=-1,
                         allow_small_or_imprecise_dtypes=True))
    io_t = c.val("gE")
    c.g(lambda e: e.iota(cm.iota8[:, :], pattern=[[1, 8]], base=0,
                         channel_multiplier=0,
                         allow_small_or_imprecise_dtypes=True))
    c.g(lambda e: e.iota(cm.iota64[:, :], pattern=[[1, 64]], base=0,
                         channel_multiplier=0,
                         allow_small_or_imprecise_dtypes=True))
    c.g(lambda e: e.iota(cm.ones_t128[0:2, 0:128], pattern=[[1, 128]], base=0,
                         channel_multiplier=-64,
                         allow_small_or_imprecise_dtypes=True))
    s2t = c.val("gE")
    c.v(lambda e: e.tensor_scalar(cm.sel2T[:, :], cm.ones_t128[0:2, 0:128],
                                  63.0, None, ALU.is_le), w=[("gE", s2t)])
    c.v(lambda e: e.tensor_scalar(cm.ones_t128[0:2, 0:128],
                                  cm.ones_t128[0:2, 0:128], 0.0, None,
                                  ALU.is_ge))
    c.v(lambda e: e.tensor_tensor(cm.sel2T[:, :], cm.sel2T[:, :],
                                  cm.ones_t128[0:2, 0:128], ALU.mult))
    c.v(lambda e: e.memset(cm.ones_t128[:, :], 1.0))
    c.v(lambda e: e.tensor_scalar(cm.ident_b[:, :], cm.iota_d[:, :], 0.0, None,
                                  ALU.is_equal), w=[("gE", io_t)])
    c.v(lambda e: e.tensor_scalar(cm.ident_f[:, :], cm.iota_d[:, :], 0.0, None,
                                  ALU.is_equal))
    c.v(lambda e: e.tensor_scalar(cm.tril_b[:, :], cm.iota_d[:, :], 0.0, None,
                                  ALU.is_ge))
    c.bar()



def phase_prep(nc, cm, din, src_f32, dst_bf, tdu, E):
    tf = [E(nc.sbuf_tensor(f"pp_tf{i}", [128, 4096], F32)) for i in range(2)]
    tb = [E(nc.sbuf_tensor(f"pp_tb{i}", [128, 4096], BF16)) for i in range(2)]

    def logic(c):
        if DEV.get("skip_prep"):
            c.bar()
            return
        for dt in range(NDT):
            c.dma("gpsimd", "pxc", dst_bf[dt * 128:(dt + 1) * 128, :],
                  src_f32[dt * 128:(dt + 1) * 128, :])
        for half, tab in ((0, din["de"]), (1, din["ue"])):
            for rt in range(8):
                b = rt % 2
                ls, ss = f"ppt{b}", f"ppu{b}"
                c.dma("sync", ls, tf[b][:, :].rearrange("p (a d) -> p a d",
                                                        d=1024),
                      tab.ap()[rt * 512:(rt + 1) * 512, :]
                      .rearrange("(a p) d -> p a d", p=128),
                      w=[("vE", c.val("vE"))])
                c.v(lambda e, b=b: e.tensor_copy(tb[b][:, :], tf[b][:, :]),
                    w=[(ls, c.val(ls)), (ss, c.val(ss))])
                ct = c.val("vE")
                c.dma("sync", ss,
                      tdu.ap()[rt * 512:(rt + 1) * 512,
                               half * 1024:(half + 1) * 1024]
                      .rearrange("(a p) d -> p a d", p=128),
                      tb[b][:, :].rearrange("p (a d) -> p a d", d=1024),
                      w=[("vE", ct)])
        c.bar()

    return logic



_EP_UNIQ = [0]


def alloc_epilogue(nc, E, ch=512):
    u = _EP_UNIQ[0]
    _EP_UNIQ[0] += 1
    ep = {"CH": ch}
    ep["z"] = [E(nc.sbuf_tensor(f"ep{u}_z{i}", [128, ch], F32))
               for i in range(NDT)]
    ep["sq"] = [E(nc.sbuf_tensor(f"ep{u}_sq{i}", [128, ch], F32))
                for i in range(2)]
    ep["ybf"] = [E(nc.sbuf_tensor(f"ep{u}_yb{i}", [128, ch], BF16))
                 for i in range(2)]
    for nm in ("m", "m2", "var", "sd", "rstd", "mr"):
        ep[nm] = E(nc.sbuf_tensor(f"ep{u}_{nm}", [1, ch], F32))
    ep["ps_s"] = E(nc.psum_tensor(f"ep{u}_ps_s", [1, ch], F32))
    ep["ps_q"] = E(nc.psum_tensor(f"ep{u}_ps_q", [1, ch], F32))
    ep["ps_r"] = E(nc.psum_tensor(f"ep{u}_ps_r", [128, ch], F32))
    ep["ps_m"] = E(nc.psum_tensor(f"ep{u}_ps_m", [128, ch], F32))
    return ep


def emit_epilogue_chunk(c, cm, ep, ch_idx, make_z, dst_f32, dst_bf, state):
    CH = ep["CH"]
    c0 = ch_idx * CH
    zrdy = []
    for dt in range(NDT):
        zrdy.append(make_z(c, dt, ep["z"][dt]))
    if DEV.get("epi_mode", 3) == 1:
        for dt in range(NDT):
            c.dma("sync", "ezs", dst_f32[dt * 128:(dt + 1) * 128, c0:c0 + CH],
                  ep["z"][dt][:, :], w=list(zrdy[dt]))
        state["z_free"] = [("ezs", c.val("ezs"))]
        state["bc_free"] = []
        return
    for dt in range(NDT):
        b = dt % 2
        c.a(lambda e, dt=dt, b=b: e.activation(ep["sq"][b][:, :],
                                               ep["z"][dt][:, :], AF.Square,
                                               bias=cm.c(C_ZERO),
                                               scale=cm.c(C_ONE)),
            w=list(zrdy[dt]) + (state.get(f"sqf{b}") or []))
        sqt = c.val("aE")
        stw = list(zrdy[dt])
        if dt == 0 and state.get("stat_free"):
            stw += state["stat_free"]
        c.t(lambda e, dt=dt: e.matmul(ep["ps_s"][0:1, :], cm.ones_col_f[:, :],
                                      ep["z"][dt][:, :], start=(dt == 0),
                                      stop=(dt == NDT - 1)), w=stw)
        c.t(lambda e, b=b: e.matmul(ep["ps_q"][0:1, :], cm.ones_col_f[:, :],
                                    ep["sq"][b][:, :], start=(dt == 0),
                                    stop=(dt == NDT - 1)),
            w=[("aE", sqt)], inc="tE")
        state[f"sqf{b}"] = [("tE", c.val("tE"))]
    mmt = c.val("tE")
    if DEV.get("epi_mode", 3) == 2:
        for dt in range(NDT):
            c.dma("sync", "ezs", dst_f32[dt * 128:(dt + 1) * 128, c0:c0 + CH],
                  ep["z"][dt][:, :], w=list(zrdy[dt]) + [("tE", mmt)])
        state["z_free"] = [("ezs", c.val("ezs"))]
        state["bc_free"] = []
        return
    c.v(lambda e: e.tensor_scalar(ep["m"][:, :], ep["ps_s"][0:1, :], 1.0 / D,
                                  None, ALU.mult), w=[("tE", mmt)])
    c.v(lambda e: e.tensor_scalar(ep["var"][:, :], ep["ps_q"][0:1, :], 1.0 / D,
                                  None, ALU.mult))
    c.v(lambda e: e.tensor_tensor(ep["m2"][:, :], ep["m"][:, :], ep["m"][:, :],
                                  ALU.mult))
    c.v(lambda e: e.tensor_tensor(ep["var"][:, :], ep["var"][:, :],
                                  ep["m2"][:, :], ALU.subtract))
    c.v(lambda e: e.tensor_scalar(ep["var"][:, :], ep["var"][:, :], 1e-5, None,
                                  ALU.add))
    vt = c.val("vE")
    state["stat_free"] = [("vE", vt)]
    c.a(lambda e: e.activation(ep["sd"][:, :], ep["var"][:, :], AF.Sqrt,
                               bias=cm.c1(C_ZERO), scale=cm.c1(C_ONE)),
        w=[("vE", vt)])
    at = c.val("aE")
    c.v(lambda e: e.reciprocal(ep["rstd"][:, :], ep["sd"][:, :]),
        w=[("aE", at)])
    rt = c.val("vE")
    c.v(lambda e: e.tensor_tensor(ep["mr"][:, :], ep["m"][:, :],
                                  ep["rstd"][:, :], ALU.mult))
    mrt = c.val("vE")
    bw = state.get("bc_free") or []
    c.t(lambda e: e.matmul(ep["ps_r"][:, :], cm.ones_row_f[:, :],
                           ep["rstd"][:, :], start=True, stop=True),
        w=[("vE", rt)] + bw)
    c.t(lambda e: e.matmul(ep["ps_m"][:, :], cm.ones_row_f[:, :],
                           ep["mr"][:, :], start=True, stop=True),
        w=[("vE", mrt)], inc="tE")
    bct = c.val("tE")
    if DEV.get("epi_mode", 3) == 25:
        for dt in range(NDT):
            c.dma("sync", "ezs", dst_f32[dt * 128:(dt + 1) * 128, c0:c0 + CH],
                  ep["z"][dt][:, :], w=list(zrdy[dt]) + [("tE", bct)])
        state["z_free"] = [("ezs", c.val("ezs"))]
        state["bc_free"] = [("tE", bct)]
        return
    if DEV.get("epi_mode", 3) == 31:
        for dt in range(NDT):
            c.v(lambda e, dt=dt: e.tensor_tensor(ep["z"][dt][:, :],
                                                 ep["z"][dt][:, :],
                                                 ep["ps_r"][:, :], ALU.mult),
                w=[("tE", bct)])
            yt = c.val("vE")
            c.dma("sync", "ezs", dst_f32[dt * 128:(dt + 1) * 128, c0:c0 + CH],
                  ep["z"][dt][:, :], w=[("vE", yt)])
        state["z_free"] = [("ezs", c.val("ezs"))]
        state["bc_free"] = [("vE", c.val("vE"))]
        return
    if DEV.get("epi_mode", 3) == 37:
        c.t(lambda e: e.matmul(ep["ps_r"][:, :], cm.ident_f[:, :],
                               ep["z"][0][:, :], start=True, stop=True),
            w=[("tE", bct)], inc="tE")
        k128t = c.val("tE")
        c.a(lambda e: e.activation(ep["sq"][0][:, :], ep["ps_r"][:, :],
                                   AF.Identity, bias=cm.c(C_ZERO),
                                   scale=cm.c(C_ONE)), w=[("tE", k128t)])
        cpt = c.val("aE")
        c.dma("sync", "ezs", dst_f32[0:128, c0:c0 + CH], ep["sq"][0][:, :],
              w=[("aE", cpt)])
        for dt in range(1, NDT):
            c.dma("sync", "ezs", dst_f32[dt * 128:(dt + 1) * 128, c0:c0 + CH],
                  ep["z"][dt][:, :], w=list(zrdy[dt]) + [("tE", bct)])
        state["z_free"] = [("ezs", c.val("ezs"))]
        state["bc_free"] = [("aE", cpt)]
        return
    if DEV.get("epi_mode", 3) == 34:
        c.a(lambda e: e.activation(ep["sq"][0][:, :], ep["ps_r"][:, :],
                                   AF.Identity, bias=cm.c(C_ZERO),
                                   scale=cm.c(C_ONE)), w=[("tE", bct)])
        cpt = c.val("aE")
        c.dma("sync", "ezs", dst_f32[0:128, c0:c0 + CH], ep["sq"][0][:, :],
              w=[("aE", cpt)])
        for dt in range(1, NDT):
            c.dma("sync", "ezs", dst_f32[dt * 128:(dt + 1) * 128, c0:c0 + CH],
                  ep["z"][dt][:, :], w=list(zrdy[dt]) + [("tE", bct)])
        state["z_free"] = [("ezs", c.val("ezs"))]
        state["bc_free"] = [("aE", cpt)]
        return
    if DEV.get("epi_mode", 3) == 33:
        for dt in range(NDT):
            b = dt % 2
            c.v(lambda e, dt=dt, b=b: e.tensor_tensor(ep["sq"][b][:, :],
                                                      ep["z"][dt][:, :],
                                                      ep["ps_r"][:, :],
                                                      ALU.mult),
                w=[("tE", bct), (f"ez{b}", c.val(f"ez{b}"))])
            yt = c.val("vE")
            c.dma("sync", f"ez{b}",
                  dst_f32[dt * 128:(dt + 1) * 128, c0:c0 + CH],
                  ep["sq"][b][:, :], w=[("vE", yt)])
        state["z_free"] = [("ez0", c.val("ez0")), ("ez1", c.val("ez1"))]
        state["bc_free"] = [("vE", c.val("vE"))]
        return
    if DEV.get("epi_mode", 3) == 32:
        c.a(lambda e: e.activation(ep["sq"][0][:, :], ep["ps_r"][:, :],
                                   AF.Identity, bias=cm.c(C_ZERO),
                                   scale=cm.c(C_ONE)), w=[("tE", bct)])
        c.a(lambda e: e.activation(ep["sq"][1][:, :], ep["ps_m"][:, :],
                                   AF.Identity, bias=cm.c(C_ZERO),
                                   scale=cm.c(C_ONE)))
        cpt = c.val("aE")
        for dt in range(NDT):
            c.v(lambda e, dt=dt: e.tensor_tensor(ep["z"][dt][:, :],
                                                 ep["z"][dt][:, :],
                                                 ep["sq"][0][:, :], ALU.mult),
                w=[("aE", cpt)])
            c.v(lambda e, dt=dt: e.tensor_tensor(ep["z"][dt][:, :],
                                                 ep["z"][dt][:, :],
                                                 ep["sq"][1][:, :],
                                                 ALU.subtract))
            yt = c.val("vE")
            c.dma("sync", "ezs", dst_f32[dt * 128:(dt + 1) * 128, c0:c0 + CH],
                  ep["z"][dt][:, :], w=[("vE", yt)])
        state["z_free"] = [("ezs", c.val("ezs"))]
        state["bc_free"] = [("aE", cpt)]
        return
    for dt in range(NDT):
        c.v(lambda e, dt=dt: e.tensor_tensor(ep["z"][dt][:, :],
                                             ep["z"][dt][:, :],
                                             ep["ps_r"][:, :], ALU.mult),
            w=[("tE", bct)])
        c.v(lambda e, dt=dt: e.tensor_tensor(ep["z"][dt][:, :],
                                             ep["z"][dt][:, :],
                                             ep["ps_m"][:, :], ALU.subtract))
        yt = c.val("vE")
        c.dma("sync", "ezs", dst_f32[dt * 128:(dt + 1) * 128, c0:c0 + CH],
              ep["z"][dt][:, :], w=[("vE", yt)])
        if dst_bf is not None:
            b = dt % 2
            yb = f"eyb{b}"
            c.a(lambda e, dt=dt, b=b: e.activation(ep["ybf"][b][:, :],
                                                   ep["z"][dt][:, :],
                                                   AF.Identity,
                                                   bias=cm.c(C_ZERO),
                                                   scale=cm.c(C_ONE)),
                w=[("vE", yt), (yb, c.val(yb))])
            ybt = c.val("aE")
            c.dma("sync", yb, dst_bf[dt * 128:(dt + 1) * 128, c0:c0 + CH],
                  ep["ybf"][b][:, :], w=[("aE", ybt)])
    state["bc_free"] = [("vE", c.val("vE"))]
    state["z_free"] = [("ezs", c.val("ezs"))]
    if dst_bf is not None:
        state["z_free"] += [("eyb0", c.val("eyb0")), ("eyb1", c.val("eyb1"))]



def phase_l0ffn_gate(nc, cm, din, src_f32, selT_dram, E):
    Wg_sb = E(nc.sbuf_tensor("mg_wg", [128, NDT * 8], F32))
    bg_sb = E(nc.sbuf_tensor("mg_bg", [1, 8], F32))
    xf = [E(nc.sbuf_tensor(f"mg_xf{i}", [128, 512], F32)) for i in range(2)]
    lg = [E(nc.sbuf_tensor(f"mg_lg{i}", [8, 512], F32)) for i in range(2)]
    lgn = [E(nc.sbuf_tensor(f"mg_lgn{i}", [128, 8], F32)) for i in range(2)]
    selst = [E(nc.sbuf_tensor(f"mg_sest{i}", [8, 128], F32)) for i in range(2)]
    scr8 = E(nc.sbuf_tensor("mg_scr8", [128, 8], F32))
    mx8 = E(nc.sbuf_tensor("mg_mx8", [128, 8], F32))
    nmx = E(nc.sbuf_tensor("mg_nmx", [128, 1], F32))
    idx8 = E(nc.sbuf_tensor("mg_idx8", [128, 8], U32))
    idxf = E(nc.sbuf_tensor("mg_idxf", [128, 1], F32))
    s8 = E(nc.sbuf_tensor("mg_s8", [128, 1], F32))
    wrec = E(nc.sbuf_tensor("mg_wrec", [128, 1], F32))
    onehot = E(nc.sbuf_tensor("mg_oh", [128, 8], F32))
    selw = E(nc.sbuf_tensor("mg_selw", [128, 8], F32))
    I8f = E(nc.sbuf_tensor("mg_i8f", [8, 8], F32))
    psG = E(nc.psum_tensor("mg_psG", [8, 512], F32))
    psTr = E(nc.psum_tensor("mg_psTr", [128, 8], F32))
    psS = E(nc.psum_tensor("mg_psS", [8, 128], F32))

    def logic(c):
        c.dma("sync", "cs", Wg_sb[:, :].rearrange("p (dt e) -> p dt e", e=8),
              din["Wg"].ap().rearrange("(dt p) e -> p dt e", p=128))
        c.dma("sync", "cs", bg_sb[:, :], din["bg"].ap())
        smt = c.val("cs")
        c.v(lambda e: e.memset(I8f[:, :], 0.0))
        c.v(lambda e: e.tensor_scalar(I8f[:, :], cm.iota_d[0:8, 0:8], 0.0,
                                      None, ALU.is_equal))
        i8t = c.val("vE")
        xf_free = [None, None]
        lg_free = [None, None]
        sel_free = [None, None]
        psG_free = None
        psTr_free = None
        for chk in range(T // 512 if DEV.get("gate", True) else 0):
            for dt in range(NDT):
                b = dt % 2
                xs = f"ggx{b}"
                c.dma("sync", xs, xf[b][:, :],
                      src_f32[dt * 128:(dt + 1) * 128,
                              chk * 512:(chk + 1) * 512],
                      w=(xf_free[b] or []))
                w = [(xs, c.val(xs)), ("cs", smt)]
                if dt == 0 and psG_free:
                    w += psG_free
                c.t(lambda e, dt=dt, b=b: e.matmul(
                    psG[:, :], Wg_sb[:, dt * 8:(dt + 1) * 8], xf[b][:, :],
                    start=(dt == 0), stop=False), w=w, inc="tE")
                xf_free[b] = [("tE", c.val("tE"))]
            c.t(lambda e: e.matmul(psG[:, :], bg_sb[:, :],
                                   cm.ones_row_f512[:, :],
                                   start=False, stop=True),
                w=[("cs", smt)], inc="tE")
            gt = c.val("tE")
            b = chk % 2
            c.v(lambda e, b=b: e.tensor_copy(lg[b][:, :], psG[:, :]),
                w=[("tE", gt)] + (lg_free[b] or []))
            lgt = c.val("vE")
            psG_free = [("vE", lgt)]
            for sub in range(4):
                w = [("vE", lgt), ("vE", i8t)]
                if psTr_free:
                    w += psTr_free
                c.t(lambda e, b=b, sub=sub: e.transpose(
                    psTr[:, :], lg[b][:, sub * 128:(sub + 1) * 128],
                    I8f[:, :]), w=w, inc="tE")
                trt = c.val("tE")
                bb = sub % 2
                c.v(lambda e, bb=bb: e.tensor_copy(lgn[bb][:, :], psTr[:, :]),
                    w=[("tE", trt)])
                psTr_free = [("vE", c.val("vE"))]
                c.v(lambda e, bb=bb: e.max(mx8[:, :], lgn[bb][:, :]))
                c.v(lambda e, bb=bb: e.max_index(idx8[:, :], mx8[:, :],
                                                 lgn[bb][:, :]))
                c.v(lambda e: e.tensor_copy(idxf[:, :], idx8[:, 0:1]))
                c.v(lambda e: e.tensor_scalar(nmx[:, :], mx8[:, 0:1], -1.0,
                                              None, ALU.mult))
                nt = c.val("vE")
                c.a(lambda e, bb=bb: e.activation(scr8[:, :], lgn[bb][:, :],
                                                  AF.Exp, bias=nmx[:, 0:1],
                                                  scale=cm.c(C_ONE)),
                    w=[("vE", nt)])
                et = c.val("aE")
                c.v(lambda e: e.reduce_sum(s8[:, :], scr8[:, :], axis=AXX),
                    w=[("aE", et)])
                c.v(lambda e: e.reciprocal(wrec[:, :], s8[:, :]))
                c.v(lambda e: e.tensor_scalar(onehot[:, :], cm.iota8[:, :],
                                              idxf[:, 0:1], None,
                                              ALU.is_equal))
                c.v(lambda e: e.tensor_scalar(selw[:, :], onehot[:, :],
                                              wrec[:, 0:1], None, ALU.mult))
                swt = c.val("vE")
                sb = sub % 2
                ssm = f"gsel{sb}"
                c.t(lambda e: e.transpose(psS[:, :], selw[:, :],
                                          cm.ident_f[:, :]),
                    w=[("vE", swt)] + (sel_free[sb] or []), inc="tE")
                stt = c.val("tE")
                c.v(lambda e, sb=sb: e.tensor_copy(selst[sb][:, :], psS[:, :]),
                    w=[("tE", stt), (ssm, c.val(ssm))])
                svt = c.val("vE")
                c.dma("sync", ssm,
                      selT_dram.ap()[:, chk * 512 + sub * 128:
                                     chk * 512 + (sub + 1) * 128],
                      selst[sb][:, :], w=[("vE", svt)])
                sel_free[sb] = [(ssm, c.val(ssm))]
            lg_free[b] = [("tE", c.val("tE"))]
        c.bar()

    return logic



def phase_l0ffn_experts(nc, cm, din, src_bf, ymoe, selT_dram, E):
    MC = 256
    NMC = T // MC
    W1sb = [E(nc.sbuf_tensor(f"mo_w1_{i}", [128, 4096], BF16))
            for i in range(NDT)]
    W2sb = [E(nc.sbuf_tensor(f"mo_w2_{i}", [128, 4096], BF16))
            for i in range(8)]
    Hsb = E(nc.sbuf_tensor("mo_h", [128, 32 * MC], BF16))
    xch = [E(nc.sbuf_tensor(f"mo_x{i}", [128, NDT * MC], BF16))
           for i in range(2)]
    b1_sb = E(nc.sbuf_tensor("mo_b1", [128, 256], F32))
    b2_sb = E(nc.sbuf_tensor("mo_b2", [128, 64], F32))
    selrow = [E(nc.sbuf_tensor(f"mo_serow{i}", [1, MC], F32)) for i in range(2)]
    selwb = [E(nc.sbuf_tensor(f"mo_swb{i}", [128, MC], F32)) for i in range(2)]
    otile = [E(nc.sbuf_tensor(f"mo_ot{i}", [128, MC], F32)) for i in range(2)]
    psH = [E(nc.psum_tensor(f"mo_psH{i}", [128, MC], F32)) for i in range(2)]
    psO = [E(nc.psum_tensor(f"mo_psO{i}", [128, MC], F32)) for i in range(2)]
    psB = [E(nc.psum_tensor(f"mo_psB{i}", [128, MC], F32)) for i in range(2)]

    def logic(c):
        c.dma("sync", "cs", b1_sb[:, :], din["b1r"].ap())
        c.dma("sync", "cs", b2_sb[:, :], din["b2r"].ap())
        smt = c.val("cs")
        prev = {}
        xch_free = [None, None]
        for e_i in range(DEV.get("moe_e", 8)):
            h_done_prev = c.val("tE")
            for dt in range(NDT):
                c.dma("gpsimd", "mew", W1sb[dt][:, :],
                      din["W1"].ap()[e_i, dt * 128:(dt + 1) * 128, :],
                      w=[("tE", h_done_prev)] if dt == 0 else [])
            for g in range(8):
                c.dma("gpsimd", "mew",
                      W2sb[g][:, :].rearrange("p (j f) -> p j f", f=1024),
                      din["W2"].ap()[e_i, g * 512:(g + 1) * 512, :]
                      .rearrange("(j p) f -> p j f", p=128))
            w_rdy = c.val("mew")
            for chk in range(DEV.get("moe_chk", NMC)):
                b = chk % 2
                xs, ss = f"mex{b}", f"mes{b}"
                c.dma("sync", xs,
                      xch[b][:, :].rearrange("p (dt c) -> p dt c", c=MC),
                      src_bf.rearrange("(dt p) t -> p dt t", p=128)[
                          :, :, chk * MC:(chk + 1) * MC],
                      w=(xch_free[b] or []))
                xt = c.val(xs)
                c.dma("sync", ss, selrow[b][:, :],
                      selT_dram.ap()[e_i:e_i + 1, chk * MC:(chk + 1) * MC],
                      w=(prev.get(("selrow", b)) or []))
                c.t(lambda e, b=b: e.matmul(psB[b][:, :], cm.ones_row_f[:, :],
                                            selrow[b][:, :], start=True,
                                            stop=True),
                    w=[(ss, c.val(ss))] + (prev.get(("psB", b)) or []),
                    inc="tE")
                psb_t = c.val("tE")
                prev[("selrow", b)] = [("tE", psb_t)]
                c.v(lambda e, b=b: e.tensor_copy(selwb[b][:, :], psB[b][:, :]),
                    w=[("tE", psb_t)] + (prev.get(("selwb", b)) or []))
                swb_t = c.val("vE")
                for jt in range(32):
                    hb = jt % 2
                    for dt in range(NDT):
                        w = []
                        if jt == 0 and dt == 0:
                            w = [("mew", w_rdy), (xs, xt), ("aE", c.val("aE"))]
                        elif dt == 0:
                            w = [("aE", c.val("aE"))]
                        c.t(lambda e, jt=jt, dt=dt, hb=hb, b=b: e.matmul(
                            psH[hb][:, :],
                            W1sb[dt][:, jt * 128:(jt + 1) * 128],
                            xch[b][:, dt * MC:(dt + 1) * MC],
                            start=(dt == 0), stop=(dt == NDT - 1)), w=w,
                            inc=("tE" if dt == NDT - 1 else None))
                    ht = c.val("tE")
                    c.a(lambda e, jt=jt, hb=hb: e.activation(
                        Hsb[:, jt * MC:(jt + 1) * MC], psH[hb][:, :], AF.Silu,
                        bias=b1_sb[:, e_i * 32 + jt:e_i * 32 + jt + 1],
                        scale=cm.c(C_ONE)), w=[("tE", ht), ("cs", smt)])
                h_all = c.val("aE")
                xch_free[b] = [("tE", c.val("tE"))]
                for ft in range(8):
                    ob = ft % 2
                    for jt in range(32):
                        w = []
                        if jt == 0:
                            w = [("aE", h_all), ("vE", c.val("vE"))]
                        c.t(lambda e, ft=ft, jt=jt, ob=ob: e.matmul(
                            psO[ob][:, :],
                            W2sb[jt // 4][:, (jt % 4) * 1024 + ft * 128:
                                          (jt % 4) * 1024 + (ft + 1) * 128],
                            Hsb[:, jt * MC:(jt + 1) * MC],
                            start=(jt == 0), stop=(jt == 31)), w=w,
                            inc=("tE" if jt == 31 else None))
                    ot = c.val("tE")
                    c.v(lambda e, ob=ob, ft=ft, b=b: e.scalar_tensor_tensor(
                        otile[ob][:, :], psO[ob][:, :],
                        b2_sb[:, e_i * 8 + ft:e_i * 8 + ft + 1],
                        selwb[b][:, :], ALU.add, ALU.mult),
                        w=[("tE", ot), ("vE", swb_t), ("cs", smt)])
                    ovt = c.val("vE")
                    c.dma("gpsimd", "mey",
                          ymoe.ap()[ft * 128:(ft + 1) * 128,
                                    chk * MC:(chk + 1) * MC],
                          otile[ob][:, :],
                          w=[("vE", ovt)] + (prev.get("mey_bar") or []),
                          accum_op=(ALU.bypass if e_i == 0 else ALU.add))
                prev[("psB", b)] = [("vE", swb_t)]
                prev[("selwb", b)] = [("vE", c.val("vE"))]
            prev["mey_bar"] = [("mey", c.val("mey"))]
        c.bar()

    return logic


def phase_l0ffn_epi(nc, cm, din, src_f32, ymoe, dst_f32, dst_bf, E):
    ep = alloc_epilogue(nc, E, 512)
    ymt = [E(nc.sbuf_tensor(f"me_ym{i}", [128, 512], F32)) for i in range(2)]
    xft = [E(nc.sbuf_tensor(f"me_xf{i}", [128, 512], F32)) for i in range(2)]

    def logic(c):
        state = {}
        buf_free = [None, None]
        for chk in range(DEV.get("epi_chk", T // 512)):
            def make_z(c, dt, ztile, chk=chk):
                b = dt % 2
                ys, xs = f"eym{b}", f"exf{b}"
                c.dma("sync", ys, ymt[b][:, :],
                      ymoe.ap()[dt * 128:(dt + 1) * 128,
                                chk * 512:(chk + 1) * 512],
                      w=(buf_free[b] or []))
                c.dma("sync", xs, xft[b][:, :],
                      src_f32[dt * 128:(dt + 1) * 128,
                              chk * 512:(chk + 1) * 512])
                zf = state.get("z_free") if dt == 0 else None
                c.v(lambda e, dt=dt, b=b: e.tensor_tensor(
                    ztile[:, :], ymt[b][:, :], xft[b][:, :], ALU.add),
                    w=[(ys, c.val(ys)), (xs, c.val(xs))] + (zf or []))
                buf_free[b] = [("vE", c.val("vE"))]
                return [("vE", c.val("vE"))]

            emit_epilogue_chunk(c, cm, ep, chk, make_z, dst_f32, dst_bf, state)
        c.bar()

    return logic



def phase_l1attn_main(nc, cm, din, src_bf, attO, E):
    CC = 128
    NCC = T // CC
    wTsb = [E(nc.sbuf_tensor(f"la_wT{i}", [128, 3 * D], BF16))
            for i in range(NDT)]
    xb = [E(nc.sbuf_tensor(f"la_xb{i}", [128, NDT * CC], BF16))
          for i in range(2)]
    qexp = E(nc.sbuf_tensor("la_qexp", [128, 8 * CC], BF16))
    kexpf = E(nc.sbuf_tensor("la_kexpf", [128, 8 * CC], F32))
    kexpb = E(nc.sbuf_tensor("la_kexpb", [128, 8 * CC], BF16))
    knat = E(nc.sbuf_tensor("la_knat", [128, 1024], BF16))
    vnat = E(nc.sbuf_tensor("la_vnat", [128, 1024], BF16))
    rsum2 = E(nc.sbuf_tensor("la_rsum2", [2, 8 * CC], F32))
    knorm = [E(nc.sbuf_tensor(f"la_kn{i}", [128, CC], F32)) for i in range(2)]
    rkn = [E(nc.sbuf_tensor(f"la_rkn{i}", [128, CC], F32)) for i in range(2)]
    qs1 = [E(nc.sbuf_tensor(f"la_qs1_{i}", [128, CC], F32)) for i in range(2)]
    qsT = E(nc.sbuf_tensor("la_qsT", [128, 8 * CC], BF16))
    carry = E(nc.sbuf_tensor("la_carry", [128, 8], F32))
    Sf = E(nc.sbuf_tensor("la_Sf", [128, 8 * 64], F32))
    Sb = E(nc.sbuf_tensor("la_Sb", [128, 8 * 64], BF16))
    Abf = [E(nc.sbuf_tensor(f"la_A{i}", [128, CC], BF16)) for i in range(2)]
    yT = [E(nc.sbuf_tensor(f"la_yT{i}", [128, 8 * 512], BF16))
          for i in range(2)]
    psP = [E(nc.psum_tensor(f"la_psP{i}", [128, 512], F32)) for i in range(2)]
    psQs = E(nc.psum_tensor("la_psQs", [2, CC], F32))
    psRB = E(nc.psum_tensor("la_psRB", [128, CC], F32))
    psA = E(nc.psum_tensor("la_psA", [128, CC], F32))
    psY = E(nc.psum_tensor("la_psY", [64, CC], F32))
    psSU = E(nc.psum_tensor("la_psSU", [64, 64], F32))

    def logic(c):
        for dt in range(NDT):
            c.dma("gpsimd", "law", wTsb[dt][:, :],
                  din["wT"].ap()[dt * 128:(dt + 1) * 128, :])
        wt = c.val("law")
        c.v(lambda e: e.memset(Sf[:, :], 0.0))
        c.v(lambda e: e.memset(Sb[:, :], 0.0))
        c.v(lambda e: e.memset(carry[:, :], 0.0))
        init_t = c.val("vE")
        st = {"xb": [None, None], "psP": [None, None],
              "psQs": None, "psRB": None, "kn": [None, None],
              "qs1": [None, None], "A": [None, None], "psA": None,
              "psY": None, "psSU": None, "qexp": None, "kexpf": None,
              "kexpb": None, "knat": None, "vnat": None, "qsT": None,
              "Sv": [init_t] * 16, "Sbv": [init_t] * 16, "yT": [None, None]}
        for cc in range(DEV.get("l1a_cc", NCC)):
            b = cc % 2
            yb = (cc // 4) % 2
            xs = f"lax{b}"
            c.dma("sync", xs,
                  xb[b][:, :].rearrange("p (dt c) -> p dt c", c=CC),
                  src_bf.rearrange("(dt p) t -> p dt t", p=128)[
                      :, :, cc * CC:(cc + 1) * CC],
                  w=(st["xb"][b] or []))
            xt = c.val(xs)
            for which in ("q", "k"):
                for et in range(NDT):
                    pb = et % 2
                    ecol = (0 if which == "q" else D) + et * 128
                    for dt in range(NDT):
                        w = []
                        if dt == 0:
                            w = [(xs, xt), ("law", wt)] + (st["psP"][pb] or [])
                        c.t(lambda e, dt=dt, pb=pb, ecol=ecol, b=b:
                            e.matmul(psP[pb][:, 0:CC],
                                     wTsb[dt][:, ecol:ecol + 128],
                                     xb[b][:, dt * CC:(dt + 1) * CC],
                                     start=(dt == 0), stop=(dt == NDT - 1)),
                            w=w, inc=("tE" if dt == NDT - 1 else None))
                    mt = c.val("tE")
                    dst = qexp if which == "q" else kexpf
                    wx = (st["qexp"] if which == "q" else st["kexpf"]) or []
                    c.a(lambda e, et=et, pb=pb, dst=dst: e.activation(
                        dst[:, et * CC:(et + 1) * CC], psP[pb][:, 0:CC],
                        AF.Exp, bias=cm.c(C_ZERO), scale=cm.c(C_QS)),
                        w=[("tE", mt)] + (wx if et < 2 else []))
                    at = c.val("aE")
                    st["psP"][pb] = [("aE", at)]
                    if which == "k":
                        c.v(lambda e, et=et: e.tensor_scalar(
                            kexpf[:, et * CC:(et + 1) * CC],
                            kexpf[:, et * CC:(et + 1) * CC], 1e-6, None,
                            ALU.add),
                            w=[("aE", at)] + ((st["kexpb"] or [])
                                              if et < 2 else []))
                        c.v(lambda e, et=et: e.tensor_copy(
                            kexpb[:, et * CC:(et + 1) * CC],
                            kexpf[:, et * CC:(et + 1) * CC]))
            kexp_done = c.val("vE")
            for which, col0 in (("k", D), ("v", 2 * D)):
                for ech in range(2):
                    pb = ech % 2
                    for dt in range(NDT):
                        w = []
                        if dt == 0:
                            w = [(xs, xt)] + (st["psP"][pb] or [])
                        c.t(lambda e, ech=ech, dt=dt, pb=pb, col0=col0, b=b:
                            e.matmul(psP[pb][:, :],
                                     xb[b][:, dt * CC:(dt + 1) * CC],
                                     wTsb[dt][:, col0 + ech * 512:
                                              col0 + (ech + 1) * 512],
                                     start=(dt == 0), stop=(dt == NDT - 1)),
                            w=w, inc=("tE" if dt == NDT - 1 else None))
                    mt = c.val("tE")
                    if which == "k":
                        c.a(lambda e, ech=ech, pb=pb: e.activation(
                            knat[:, ech * 512:(ech + 1) * 512], psP[pb][:, :],
                            AF.Exp, bias=cm.c(C_ZERO), scale=cm.c(C_QS)),
                            w=[("tE", mt)] + ((st["knat"] or [])
                                              if ech == 0 else []))
                    else:
                        c.a(lambda e, ech=ech, pb=pb: e.activation(
                            vnat[:, ech * 512:(ech + 1) * 512], psP[pb][:, :],
                            AF.Identity, bias=cm.c(C_ZERO),
                            scale=cm.c(C_ONE)),
                            w=[("tE", mt)] + ((st["vnat"] or [])
                                              if ech == 0 else []))
                    st["psP"][pb] = [("aE", c.val("aE"))]
            knv_done = c.val("aE")
            for et in range(NDT):
                kb = et % 2
                c.t(lambda e, et=et: e.matmul(psQs[:, :], cm.ones2[:, :],
                                              qexp[:, et * CC:(et + 1) * CC],
                                              start=True, stop=True),
                    w=[("aE", knv_done)] + (st["psQs"] or []), inc="tE")
                qst = c.val("tE")
                c.v(lambda e, et=et: e.reciprocal(
                    rsum2[:, et * CC:(et + 1) * CC], psQs[:, :]),
                    w=[("tE", qst)])
                st["psQs"] = [("vE", c.val("vE"))]
                rst = c.val("vE")
                c.t(lambda e, et=et: e.matmul(psRB[:, :], cm.sel2T[:, :],
                                              rsum2[:, et * CC:(et + 1) * CC],
                                              start=True, stop=True),
                    w=[("vE", rst)] + (st["psRB"] or []), inc="tE")
                rbt = c.val("tE")
                c.v(lambda e, et=et, kb=kb: e.tensor_tensor_scan(
                    knorm[kb][:, :], kexpf[:, et * CC:(et + 1) * CC],
                    cm.ones_t128[:, :], carry[:, et:et + 1], ALU.add,
                    ALU.mult),
                    w=[("vE", kexp_done)] + (st["kn"][kb] or []))
                c.v(lambda e, et=et, kb=kb: e.tensor_copy(
                    carry[:, et:et + 1], knorm[kb][:, CC - 1:CC]))
                c.v(lambda e, kb=kb: e.reciprocal(rkn[kb][:, :],
                                                  knorm[kb][:, :]))
                c.v(lambda e, et=et, kb=kb: e.tensor_tensor(
                    qs1[kb][:, :], qexp[:, et * CC:(et + 1) * CC],
                    psRB[:, :], ALU.mult),
                    w=[("tE", rbt)] + (st["qs1"][kb] or []))
                st["psRB"] = [("vE", c.val("vE"))]
                c.v(lambda e, et=et, kb=kb: e.tensor_tensor(
                    qsT[:, et * CC:(et + 1) * CC], qs1[kb][:, :],
                    rkn[kb][:, :], ALU.mult),
                    w=(st["qsT"] or []) if et == 0 else [])
                st["kn"][kb] = [("vE", c.val("vE"))]
                st["qs1"][kb] = st["kn"][kb]
            qs_done = c.val("vE")
            for h in range(16):
                et, hh = h // 2, h % 2
                ab = h % 2
                lo = hh * 64
                c.t(lambda e, et=et, lo=lo: e.matmul(
                    psA[:, :], kexpb[lo:lo + 64, et * CC:(et + 1) * CC],
                    qsT[lo:lo + 64, et * CC:(et + 1) * CC],
                    start=True, stop=True),
                    w=[("vE", qs_done)] + (st["psA"] or []), inc="tE")
                at_ = c.val("tE")
                c.v(lambda e, ab=ab: e.tensor_tensor(Abf[ab][:, :], psA[:, :],
                                                     cm.tril_b[:, :],
                                                     ALU.mult),
                    w=[("tE", at_)] + (st["A"][ab] or []))
                st["psA"] = [("vE", c.val("vE"))]
                amt = c.val("vE")
                c.t(lambda e, h=h, ab=ab: e.matmul(
                    psY[:, :], vnat[:, h * 64:(h + 1) * 64], Abf[ab][:, :],
                    start=True, stop=False),
                    w=[("vE", amt), ("aE", knv_done)] + (st["psY"] or []))
                c.t(lambda e, h=h, lo=lo: e.matmul(
                    psY[:, :],
                    Sb[lo:lo + 64, (h // 2) * 64:(h // 2) * 64 + 64],
                    qsT[lo:lo + 64, (h // 2) * CC:(h // 2) * CC + CC],
                    start=False, stop=True),
                    w=[("vE", st["Sbv"][h])], inc="tE")
                yt_ = c.val("tE")
                st["A"][ab] = [("tE", yt_)]
                c.a(lambda e, h=h, hh=hh, cc=cc, yb=yb: e.activation(
                    yT[yb][hh * 64:hh * 64 + 64,
                           (h // 2) * 512 + (cc % 4) * CC:
                           (h // 2) * 512 + (cc % 4) * CC + CC],
                    psY[:, :], AF.Identity,
                    bias=cm.consts[0:64, C_ZERO:C_ZERO + 1],
                    scale=cm.consts[0:64, C_ONE:C_ONE + 1]),
                    w=[("tE", yt_)] + (st["yT"][yb] or []))
                st["psY"] = [("aE", c.val("aE"))]
                c.t(lambda e, h=h: e.matmul(
                    psSU[:, :], knat[:, h * 64:(h + 1) * 64],
                    vnat[:, h * 64:(h + 1) * 64], start=True, stop=True),
                    w=(st["psSU"] or []) + [("aE", knv_done)], inc="tE")
                sut = c.val("tE")
                c.v(lambda e, h=h, lo=lo: e.tensor_tensor(
                    Sf[lo:lo + 64, (h // 2) * 64:(h // 2) * 64 + 64],
                    Sf[lo:lo + 64, (h // 2) * 64:(h // 2) * 64 + 64],
                    psSU[:, :], ALU.add),
                    w=[("tE", sut), ("vE", st["Sv"][h])])
                st["Sv"][h] = c.val("vE")
                c.v(lambda e, h=h, lo=lo: e.tensor_copy(
                    Sb[lo:lo + 64, (h // 2) * 64:(h // 2) * 64 + 64],
                    Sf[lo:lo + 64, (h // 2) * 64:(h // 2) * 64 + 64]))
                st["Sbv"][h] = c.val("vE")
                st["psSU"] = [("vE", st["Sv"][h])]
            end_t = c.val("tE")
            st["xb"][b] = [("tE", end_t)]
            st["qexp"] = [("tE", end_t), ("vE", qs_done)]
            st["kexpf"] = [("vE", qs_done)]
            st["kexpb"] = [("tE", end_t)]
            st["knat"] = [("tE", end_t)]
            st["vnat"] = [("tE", end_t)]
            st["qsT"] = [("tE", end_t)]
            if cc % 4 == 3:
                ys = f"lay{yb}"
                ya = c.val("aE")
                c.dma("sync", ys,
                      attO.rearrange("(et p) t -> p et t", p=128)[
                          :, :, (cc // 4) * 512:(cc // 4) * 512 + 512],
                      yT[yb][:, :].rearrange("p (et t) -> p et t", t=512),
                      w=[("aE", ya)])
                st["yT"][yb] = [(ys, c.val(ys))]
        c.bar()

    return logic




MPITCH = 4224
MSLAB = 17340672


def phase_l0attn(nc, cm, din, src_f32, src_bf, mscr, attO, E):
    RT = E(nc.sbuf_tensor("at_rt", [128, 8192], BF16))
    QT = E(nc.sbuf_tensor("at_qt", [128, T], BF16))
    KT = E(nc.sbuf_tensor("at_kt", [128, T], BF16))
    Vx = E(nc.sbuf_tensor("at_vx", [128, 64 * 66], BF16))
    wq = [E(nc.sbuf_tensor(f"at_wq{i}", [128, 8 * 128], BF16)) for i in range(2)]
    wk = [E(nc.sbuf_tensor(f"at_wk{i}", [128, 8 * 128], BF16)) for i in range(2)]
    wv = [E(nc.sbuf_tensor(f"at_wv{i}", [128, 8 * 128], BF16)) for i in range(2)]
    xc = [E(nc.sbuf_tensor(f"at_xc{i}", [128, NDT * 512], BF16)) for i in range(2)]
    bq = E(nc.sbuf_tensor("at_bq", [1, D], F32))
    bk = E(nc.sbuf_tensor("at_bk", [1, D], F32))
    bv = E(nc.sbuf_tensor("at_bv", [1, D], F32))
    Msb = [E(nc.sbuf_tensor(f"at_m{i}", [128, MPITCH], BF16)) for i in range(2)]
    bias = [E(nc.sbuf_tensor(f"at_bi{i}", [128, 512], BF16)) for i in range(3)]
    expS = [E(nc.sbuf_tensor(f"at_ex{i}", [128, 512], BF16)) for i in range(3)]
    rsum = E(nc.sbuf_tensor("at_rs", [1, 512], F32))
    rb = E(nc.sbuf_tensor("at_rb", [64, 512], F32))
    ao = [E(nc.sbuf_tensor(f"at_ao{i}", [64, 512], BF16)) for i in range(2)]
    psA = [E(nc.psum_tensor(f"at_psA{i}", [128, 512], F32)) for i in range(2)]
    psV = [E(nc.psum_tensor(f"at_psV{i}", [128, 128], F32)) for i in range(2)]
    psS = [E(nc.psum_tensor(f"at_psS{i}", [128, 512], F32)) for i in range(2)]
    psO = E(nc.psum_tensor("at_psO", [65, 512], F32))
    psR = E(nc.psum_tensor("at_psR", [64, 512], F32))

    def logic(c):
        c.dma("gpsimd", "arl", RT[0:64, 0:8191], din["RT"].ap())
        c.dma("gpsimd", "arl", RT[64:128, 0:8191], din["RT"].ap())
        c.dma("sync", "abl", bq[:, :], din["bqkvo"].ap()[0:1, :])
        c.dma("sync", "abl", bk[:, :], din["bqkvo"].ap()[1:2, :])
        c.dma("sync", "abl", bv[:, :], din["bqkvo"].ap()[2:3, :])
        rt_t = c.val("arl")
        bl_t = c.val("abl")
        c.v(lambda e: e.memset(RT[:, 8191:8192], 0.0))
        c.v(lambda e: e.memset(
            Vx[:, :].rearrange("p (g cc) -> p g cc", cc=66)[:, :, 64:65], 1.0))
        c.v(lambda e: e.memset(
            Vx[:, :].rearrange("p (g cc) -> p g cc", cc=66)[:, :, 65:66], 0.0))
        ones_t = c.val("vE")
        st = {"w": [None, None], "xc": [None, None], "psA": [None, None],
              "psV": [None, None], "psS": [None, None], "QT": None,
              "KT": None, "Vx": None, "Msb": [None, None],
              "bias": [None, None, None], "expS": [None, None, None],
              "psO": None, "psR": None, "rb": None, "ao": [None, None],
              "slab_free": [None, None, None, None]}
        for et in range(DEV.get("att_et", 8)):
            wb = et % 2
            for wbuf, wT in ((wq[wb], "WqT"), (wk[wb], "WkT"), (wv[wb], "WvT")):
                c.dma("gpsimd", f"aw{wb}",
                      wbuf[:, :].rearrange("p (dt e) -> p dt e", e=128),
                      din[wT].ap().rearrange("(dt p) e -> p dt e", p=128)[
                          :, :, et * 128:(et + 1) * 128],
                      w=(st["w"][wb] or []))
            w_t = c.val(f"aw{wb}")
            for tc in range(8):
                xb = tc % 2
                c.dma("sync", f"ax{xb}",
                      xc[xb][:, :].rearrange("p (dt cc) -> p dt cc", cc=512),
                      src_bf.rearrange("(dt p) t -> p dt t", p=128)[
                          :, :, tc * 512:(tc + 1) * 512],
                      w=(st["xc"][xb] or []))
                x_t = c.val(f"ax{xb}")
                for which, wbuf, brow, dst, scl in (
                        ("q", wq[wb], bq, QT, C_ONE), ("k", wk[wb], bk, KT, C_QS)):
                    pa = (tc * 2 + (0 if which == "q" else 1)) % 2
                    for dt in range(NDT):
                        w = []
                        if dt == 0:
                            w = [(f"ax{xb}", x_t), (f"aw{wb}", w_t),
                                 ("abl", bl_t)] + (st["psA"][pa] or [])
                        c.t(lambda e, dt=dt, pa=pa, xb=xb, wbuf=wbuf: e.matmul(
                            psA[pa][:, :], wbuf[:, dt * 128:(dt + 1) * 128],
                            xc[xb][:, dt * 512:(dt + 1) * 512],
                            start=(dt == 0), stop=False), w=w)
                    c.t(lambda e, pa=pa, brow=brow, et=et: e.matmul(
                        psA[pa][:, :], brow[:, et * 128:(et + 1) * 128],
                        cm.ones_row_f512[:, :], start=False, stop=True),
                        inc="tE")
                    mt = c.val("tE")
                    c.a(lambda e, pa=pa, tc=tc, dst=dst, scl=scl: e.activation(
                        dst[:, tc * 512:(tc + 1) * 512], psA[pa][:, :],
                        AF.Identity, bias=cm.c(C_ZERO), scale=cm.c(scl)),
                        w=[("tE", mt)] + ((st["QT"] or []) if tc == 0 and
                                          which == "q" else []))
                    st["psA"][pa] = [("aE", c.val("aE"))]
                for ttl in range(4):
                    pv = ttl % 2
                    for dt in range(NDT):
                        w = []
                        if dt == 0:
                            w = [(f"ax{xb}", x_t)] + (st["psV"][pv] or [])
                            if tc == 0 and ttl == 0:
                                w += (st["Vx"] or []) + [("vE", ones_t)]
                        c.t(lambda e, dt=dt, pv=pv, xb=xb, ttl=ttl, wb=wb:
                            e.matmul(psV[pv][:, :],
                                     xc[xb][:, dt * 512 + ttl * 128:
                                            dt * 512 + (ttl + 1) * 128],
                                     wv[wb][:, dt * 128:(dt + 1) * 128],
                                     start=(dt == 0), stop=False), w=w)
                    c.t(lambda e, pv=pv, et=et: e.matmul(
                        psV[pv][:, :], cm.ones_row_f[:, :],
                        bv[:, et * 128:(et + 1) * 128], start=False,
                        stop=True), inc="tE")
                    vt = c.val("tE")
                    jt = tc * 4 + ttl
                    for hh in range(2):
                        c.a(lambda e, pv=pv, jt=jt, hh=hh: e.activation(
                            Vx[:, (jt * 2 + hh) * 66:(jt * 2 + hh) * 66 + 64],
                            psV[pv][:, hh * 64:(hh + 1) * 64], AF.Identity,
                            bias=cm.consts[0:128, C_ZERO:C_ZERO + 1],
                            scale=cm.consts[0:128, C_ONE:C_ONE + 1]),
                            w=[("tE", vt)])
                    st["psV"][pv] = [("aE", c.val("aE"))]
                st["xc"][xb] = [("tE", c.val("tE"))]
            proj_t = c.val("aE")
            st["w"][wb] = [("tE", c.val("tE"))]
            for hh in range(2):
                slab = (et % 2) * 2 + hh
                ms = f"ams{slab}"
                lo = hh * 64
                for it in range(32):
                    mb = it % 2
                    for kc in range(9):
                        pa = kc % 2
                        ncol = 512 if kc < 8 else 128
                        cbase = 3968 - it * 128 + kc * 512
                        w = [("aE", proj_t), ("arl", rt_t)]
                        if kc == 0:
                            w += (st["Msb"][mb] or [])
                        w += (st["psA"][pa] or [])
                        c.t(lambda e, it=it, kc=kc, pa=pa, lo=lo, cbase=cbase,
                            ncol=ncol: e.matmul(
                                psA[pa][:, 0:ncol],
                                QT[lo:lo + 64, it * 128:(it + 1) * 128],
                                RT[lo:lo + 64, cbase:cbase + ncol],
                                start=True, stop=True), w=w, inc="tE")
                        mt = c.val("tE")
                        c.v(lambda e, kc=kc, pa=pa, mb=mb, ncol=ncol:
                            e.tensor_copy(Msb[mb][:, kc * 512:kc * 512 + ncol],
                                          psA[pa][:, 0:ncol]),
                            w=[("tE", mt)])
                        st["psA"][pa] = [("vE", c.val("vE"))]
                    mv = c.val("vE")
                    c.dma("sync", ms,
                          AP(mscr, slab * MSLAB + it * 128 * (MPITCH - 1),
                             [[MPITCH, 128], [1, MPITCH]]),
                          Msb[mb][:, :],
                          w=[("vE", mv)] + (st["slab_free"][slab] or []))
                    st["Msb"][mb] = [(ms, c.val(ms))]
                mstore_t = c.val(ms)
                sk = f"ask{slab}"
                for ic in range(8):
                    for jt in range(32):
                        bb = jt % 3
                        base = (slab * MSLAB + ic * 512 * (MPITCH - 1)
                                + jt * 128 + 127)
                        c.dma("sync", sk, bias[bb][:, :],
                              AP(mscr, base, [[MPITCH - 1, 512], [1, 128]]),
                              w=[(ms, mstore_t)] + (st["bias"][bb] or []),
                              transpose=True)
                        b_t = c.val(sk)
                        ps = jt % 2
                        w = [(sk, b_t)] + (st["psS"][ps] or [])
                        c.t(lambda e, bb=bb, ps=ps: e.matmul(
                            psS[ps][:, :], cm.ident_b[:, :], bias[bb][:, :],
                            start=True, stop=False), w=w)
                        c.t(lambda e, jt=jt, ic=ic, ps=ps, lo=lo: e.matmul(
                            psS[ps][:, :],
                            KT[lo:lo + 64, jt * 128:(jt + 1) * 128],
                            QT[lo:lo + 64, ic * 512:(ic + 1) * 512],
                            start=False, stop=True), inc="tE")
                        s_t = c.val("tE")
                        st["bias"][bb] = [("tE", s_t)]
                        eb = jt % 3
                        c.a(lambda e, ps=ps, eb=eb: e.activation(
                            expS[eb][:, :], psS[ps][:, :], AF.Exp,
                            bias=cm.c(C_EXPB), scale=cm.c(C_ONE)),
                            w=[("tE", s_t)] + (st["expS"][eb] or []))
                        e_t = c.val("aE")
                        st["psS"][ps] = [("aE", e_t)]
                        w = [("aE", e_t)]
                        if jt == 0:
                            w += (st["psO"] or [])
                        c.t(lambda e, jt=jt, eb=eb, hh=hh: e.matmul(
                            psO[:, :],
                            Vx[:, (jt * 2 + hh) * 66:(jt * 2 + hh) * 66 + 65],
                            expS[eb][:, :], start=(jt == 0), stop=(jt == 31)),
                            w=w, inc="tE")
                        st["expS"][eb] = [("tE", c.val("tE"))]
                    o_t = c.val("tE")
                    c.v(lambda e: e.reciprocal(rsum[:, :], psO[64:65, :]),
                        w=[("tE", o_t)] + (st["rb"] or []))
                    r_t = c.val("vE")
                    c.t(lambda e: e.matmul(psR[:, :], cm.ones_row_f[0:1, 0:64],
                                           rsum[:, :], start=True, stop=True),
                        w=[("vE", r_t)] + (st["psR"] or []), inc="tE")
                    rb_t = c.val("tE")
                    c.a(lambda e: e.activation(rb[:, :], psR[:, :],
                                               AF.Identity,
                                               bias=cm.consts[0:64,
                                                              C_ZERO:C_ZERO + 1],
                                               scale=cm.consts[0:64,
                                                               C_ONE:C_ONE + 1]),
                        w=[("tE", rb_t)])
                    rba_t = c.val("aE")
                    st["psR"] = [("aE", rba_t)]
                    ab = ic % 2
                    c.v(lambda e, ab=ab: e.tensor_tensor(
                        ao[ab][:, :], psO[0:64, :], rb[:, :], ALU.mult),
                        w=[("aE", rba_t)] + (st["ao"][ab] or []))
                    ao_t = c.val("vE")
                    st["psO"] = [("vE", ao_t)]
                    st["rb"] = [("vE", ao_t)]
                    c.dma("sync", f"aao{ab}",
                          attO[et * 128 + lo:et * 128 + lo + 64,
                               ic * 512:(ic + 1) * 512],
                          ao[ab][:, :], w=[("vE", ao_t)])
                    st["ao"][ab] = [(f"aao{ab}", c.val(f"aao{ab}"))]
                st["slab_free"][slab] = [(sk, c.val(sk))]
            st["QT"] = [("tE", c.val("tE"))]
            st["KT"] = st["QT"]
            st["Vx"] = st["QT"]
        c.bar()

    return logic



def phase_l1ffn_peer(nc, cm, din, src_f32, src_bf, tdu, expdT, E):
    xbt = E(nc.sbuf_tensor("pr_xbt", [128, NDT * 128], BF16))
    xft = [E(nc.sbuf_tensor(f"pr_xf{i}", [128, 128], F32)) for i in range(2)]
    y3n = E(nc.sbuf_tensor("pr_y3n", [128, 1024], BF16))
    Wq2 = E(nc.sbuf_tensor("pr_wq2", [128, NDT * 512], F32))
    keys = E(nc.sbuf_tensor("pr_keys", [128, 512], F32))
    qT = E(nc.sbuf_tensor("pr_qT", [128, 4 * 128], F32))
    sim = E(nc.sbuf_tensor("pr_sim", [128, 8 * 64], F32))
    sx = E(nc.sbuf_tensor("pr_sx", [128, 64], F32))
    ixf = E(nc.sbuf_tensor("pr_ixf", [128, 64], F32))
    idxu = E(nc.sbuf_tensor("pr_idxu", [128, 8], U32))
    allsc = E(nc.sbuf_tensor("pr_alls", [128, 64], F32))
    exy = E(nc.sbuf_tensor("pr_exy", [128, 64], F32))
    ix64 = E(nc.sbuf_tensor("pr_ix64", [128, 8], F32))
    s8f = E(nc.sbuf_tensor("pr_s8f", [128, 8], F32))
    pku = E(nc.sbuf_tensor("pr_pku", [128, 8], U32))
    pkf = E(nc.sbuf_tensor("pr_pkf", [128, 8], F32))
    oh64 = E(nc.sbuf_tensor("pr_oh64", [128, 64], F32))
    junk64 = E(nc.sbuf_tensor("pr_junk64", [128, 64], F32))
    ef8 = E(nc.sbuf_tensor("pr_ef8", [128, 8], F32))
    efi = E(nc.sbuf_tensor("pr_efi", [128, 8], I32))
    es8 = E(nc.sbuf_tensor("pr_es8", [128, 8], F32))
    p8 = E(nc.sbuf_tensor("pr_p8", [128, 8], F32))
    ssum = E(nc.sbuf_tensor("pr_ssum", [128, 1], F32))
    rs = E(nc.sbuf_tensor("pr_rs", [128, 1], F32))
    sd8 = E(nc.sbuf_tensor("pr_sd8", [128, 8], F32))
    w8 = E(nc.sbuf_tensor("pr_w8", [128, 8], F32))
    gjunk = E(nc.sbuf_tensor("pr_gjunk", [128, 1024], BF16))
    G = [E(nc.sbuf_tensor(f"pr_G{i}", [128, 2048], BF16)) for i in range(3)]
    expacc = E(nc.sbuf_tensor("pr_eacc", [128, 1024], F32))
    etile = [E(nc.sbuf_tensor(f"pr_et{i}", [128, 128], F32)) for i in range(2)]
    psB2 = [E(nc.psum_tensor(f"pr_psb{i}", [128, 128], BF16)) for i in range(2)]
    psF2 = [E(nc.psum_tensor(f"pr_psf{i}", [128, 128], F32)) for i in range(2)]

    def logic(c):
        c.dma("gpsimd", "prw",
              Wq2[:, :].rearrange("p (dt e) -> p dt e", e=512),
              din["Wq2T"].ap().rearrange("(dt p) e -> p dt e", p=128))
        c.dma("sync", "prk", keys[0:64, :].rearrange("p (g k) -> p g k", k=64),
              din["keysT"].ap().rearrange("two h n k -> n (two h) k"))
        c.dma("sync", "prk", keys[64:128, :].rearrange("p (g k) -> p g k", k=64),
              din["keysT"].ap().rearrange("two h n k -> n (two h) k"))
        wq_t = c.val("prw")
        k_t = c.val("prk")
        st = {"xbt": None, "xf": [None, None], "psB": [None, None],
              "psF": [None, None], "y3n": None, "qT": None, "sim": None,
              "G": [None, None, None], "et": [None, None], "eacc": None}
        for tt in range(DEV.get("peer_tt", 32)):
            t0 = tt * 128
            c.dma("sync", "prx",
                  xbt[:, :].rearrange("p (dt cc) -> p dt cc", cc=128),
                  src_bf.rearrange("(dt p) t -> p dt t", p=128)[
                      :, :, t0:t0 + 128], w=(st["xbt"] or []))
            xbt_t = c.val("prx")
            for dt in range(NDT):
                pb = dt % 2
                c.t(lambda e, dt=dt, pb=pb: e.transpose(
                    psB2[pb][:, :], xbt[:, dt * 128:(dt + 1) * 128],
                    cm.ident_b[:, :]),
                    w=[("prx", xbt_t)] + (st["psB"][pb] or []), inc="tE")
                trt = c.val("tE")
                c.a(lambda e, dt=dt, pb=pb: e.activation(
                    y3n[:, dt * 128:(dt + 1) * 128], psB2[pb][:, :],
                    AF.Identity, bias=cm.c(C_ZERO), scale=cm.c(C_ONE)),
                    w=[("tE", trt)] + ((st["y3n"] or []) if dt < 2 else []))
                st["psB"][pb] = [("aE", c.val("aE"))]
            y3_t = c.val("aE")
            for et in range(4):
                pb = et % 2
                for dt in range(NDT):
                    b = dt % 2
                    c.dma("sync", f"prf{b}", xft[b][:, :],
                          src_f32[dt * 128:(dt + 1) * 128, t0:t0 + 128],
                          w=(st["xf"][b] or []))
                    w = [(f"prf{b}", c.val(f"prf{b}")), ("prw", wq_t)]
                    if dt == 0:
                        w += (st["psF"][pb] or [])
                    c.t(lambda e, et=et, dt=dt, pb=pb, b=b: e.matmul(
                        psF2[pb][:, :],
                        Wq2[:, dt * 512 + et * 128:dt * 512 + (et + 1) * 128],
                        xft[b][:, :], start=(dt == 0), stop=(dt == NDT - 1)),
                        w=w, inc=("tE" if dt == NDT - 1 else None))
                    st["xf"][b] = [("tE", c.val("tE"))]
                qt_ = c.val("tE")
                c.v(lambda e, et=et, pb=pb: e.tensor_copy(
                    qT[:, et * 128:(et + 1) * 128], psF2[pb][:, :]),
                    w=[("tE", qt_)] + ((st["qT"] or []) if et < 2 else []))
                st["psF"][pb] = [("vE", c.val("vE"))]
            qT_t = c.val("vE")
            for ph in range(8):
                pb = ph % 2
                lo = (ph % 2) * 64
                etq = ph // 2
                c.t(lambda e, ph=ph, pb=pb, lo=lo, etq=etq: e.matmul(
                    psF2[pb][:, 0:64],
                    qT[lo:lo + 64, etq * 128:(etq + 1) * 128],
                    keys[lo:lo + 64, ph * 64:(ph + 1) * 64],
                    start=True, stop=True),
                    w=[("vE", qT_t), ("prk", k_t)] + (st["psF"][pb] or []),
                    inc="tE")
                smt = c.val("tE")
                c.v(lambda e, ph=ph, pb=pb: e.tensor_copy(
                    sim[:, ph * 64:(ph + 1) * 64], psF2[pb][:, 0:64]),
                    w=[("tE", smt)] + ((st["sim"] or []) if ph < 2 else []))
                st["psF"][pb] = [("vE", c.val("vE"))]
                c.v(lambda e, ph=ph: e.max(sx[:, ph * 8:(ph + 1) * 8],
                                           sim[:, ph * 64:(ph + 1) * 64]))
                c.v(lambda e, ph=ph: e.max_index(idxu[:, :],
                                                 sx[:, ph * 8:(ph + 1) * 8],
                                                 sim[:, ph * 64:(ph + 1) * 64]))
                c.v(lambda e, ph=ph: e.tensor_copy(ixf[:, ph * 8:(ph + 1) * 8],
                                                   idxu[:, :]))
            st["sim"] = None
            st["qT"] = None
            eacc_w = (st["eacc"] or [])
            c.v(lambda e: e.memset(expacc[:, :], 0.0), w=eacc_w)
            for h in range(4):
                for a in range(8):
                    c.v(lambda e, h=h, a=a: e.tensor_scalar(
                        allsc[:, a * 8:(a + 1) * 8],
                        sx[:, (4 + h) * 8:(5 + h) * 8],
                        sx[:, h * 8 + a:h * 8 + a + 1], None, ALU.add))
                c.v(lambda e, h=h: e.tensor_scalar(
                    ix64[:, :], ixf[:, h * 8:(h + 1) * 8], 64.0, None,
                    ALU.mult))
                for a in range(8):
                    c.v(lambda e, h=h, a=a: e.tensor_scalar(
                        exy[:, a * 8:(a + 1) * 8],
                        ixf[:, (4 + h) * 8:(5 + h) * 8],
                        ix64[:, a:a + 1], None, ALU.add))
                c.v(lambda e: e.max(s8f[:, :], allsc[:, :]))
                c.v(lambda e: e.max_index(pku[:, :], s8f[:, :], allsc[:, :]))
                c.v(lambda e: e.tensor_copy(pkf[:, :], pku[:, :]))
                c.v(lambda e: e.tensor_scalar(es8[:, :], s8f[:, :],
                                              s8f[:, 0:1], None, ALU.subtract))
                ev = c.val("vE")
                c.a(lambda e: e.activation(es8[:, :], es8[:, :], AF.Exp,
                                           bias=cm.c(C_ZERO),
                                           scale=cm.c(C_ONE)), w=[("vE", ev)])
                ea = c.val("aE")
                c.v(lambda e: e.reduce_sum(ssum[:, :], es8[:, :], axis=AXX),
                    w=[("aE", ea)])
                c.v(lambda e: e.reciprocal(rs[:, :], ssum[:, :]))
                c.v(lambda e: e.tensor_scalar(p8[:, :], es8[:, :], rs[:, 0:1],
                                              None, ALU.mult))
                for k in range(8):
                    c.v(lambda e, k=k: e.tensor_scalar(
                        oh64[:, :], cm.iota64[:, :], pkf[:, k:k + 1], None,
                        ALU.is_equal))
                    c.v(lambda e, k=k: e.scalar_tensor_tensor(
                        junk64[:, :], oh64[:, :], 1.0, exy[:, :],
                        ALU.mult, ALU.mult, accum_out=ef8[:, k:k + 1]))
                c.v(lambda e: e.tensor_copy(efi[:, :], ef8[:, :]))
                ef_t = c.val("vE")
                for s in range(8):
                    gb = s % 3
                    c.idma("prg", G[gb][:, :], tdu.ap(),
                           IndirectOffsetOnAxis(efi[:, s:s + 1], 0),
                           w=[("vE", ef_t)] + (st["G"][gb] or []))
                    g_t = c.val("prg")
                    c.v(lambda e, s=s, gb=gb: e.scalar_tensor_tensor(
                        gjunk[:, :], G[gb][:, 0:1024], 1.0, y3n[:, :],
                        ALU.mult, ALU.mult, accum_out=sd8[:, s:s + 1]),
                        w=[("prg", g_t), ("aE", y3_t)])
                    c.v(lambda e, s=s: e.tensor_tensor(
                        w8[:, s:s + 1], sd8[:, s:s + 1], p8[:, s:s + 1],
                        ALU.mult))
                    wv = c.val("vE")
                    c.a(lambda e, s=s: e.activation(w8[:, s:s + 1],
                                                    w8[:, s:s + 1], AF.Silu,
                                                    bias=cm.c(C_ZERO),
                                                    scale=cm.c(C_ONE)),
                        w=[("vE", wv)])
                    wa = c.val("aE")
                    c.v(lambda e, s=s, gb=gb: e.scalar_tensor_tensor(
                        expacc[:, :], G[gb][:, 1024:2048], w8[:, s:s + 1],
                        expacc[:, :], ALU.mult, ALU.add), w=[("aE", wa)])
                    st["G"][gb] = [("vE", c.val("vE"))]
            eacc_t = c.val("vE")
            for dt in range(NDT):
                pb = dt % 2
                eb = dt % 2
                c.t(lambda e, dt=dt, pb=pb: e.transpose(
                    psF2[pb][:, :], expacc[:, dt * 128:(dt + 1) * 128],
                    cm.ident_f[:, :]),
                    w=[("vE", eacc_t)] + (st["psF"][pb] or []), inc="tE")
                trt = c.val("tE")
                c.v(lambda e, pb=pb, eb=eb: e.tensor_copy(etile[eb][:, :],
                                                          psF2[pb][:, :]),
                    w=[("tE", trt)] + (st["et"][eb] or []))
                ev_ = c.val("vE")
                st["psF"][pb] = [("vE", ev_)]
                c.dma("sync", f"pre{eb}",
                      expdT[dt * 128:(dt + 1) * 128, t0:t0 + 128],
                      etile[eb][:, :], w=[("vE", ev_)])
                st["et"][eb] = [(f"pre{eb}", c.val(f"pre{eb}"))]
            st["eacc"] = [("tE", c.val("tE"))]
            st["xbt"] = [("tE", c.val("tE"))]
            st["y3n"] = [("vE", c.val("vE"))]
        c.bar()

    return logic



def phase_l1ffn_dense(nc, cm, din, src_f32, src_bf, expdT, dst_f32, dst_bf, E):
    ep = alloc_epilogue(nc, E, 512)
    Wdn = [E(nc.sbuf_tensor(f"du_wd{i}", [128, 4 * 1024], BF16))
           for i in range(8)]
    Wup = [E(nc.sbuf_tensor(f"du_wu{i}", [128, 8 * 1024], BF16))
           for i in range(2)]
    Usb = E(nc.sbuf_tensor("du_u", [128, 32 * 512], BF16))
    xb = E(nc.sbuf_tensor("du_xb", [128, NDT * 512], BF16))
    xft = [E(nc.sbuf_tensor(f"du_xf{i}", [128, 512], F32)) for i in range(2)]
    etl = [E(nc.sbuf_tensor(f"du_et{i}", [128, 512], F32)) for i in range(2)]
    psU = [E(nc.psum_tensor(f"du_psu{i}", [128, 512], F32)) for i in range(2)]
    psP = [E(nc.psum_tensor(f"du_psp{i}", [128, 512], F32)) for i in range(2)]

    def logic(c):
        for g in range(8):
            c.dma("gpsimd", "duwd",
                  Wdn[g][:, :].rearrange("p (it d) -> p it d", d=1024),
                  din["WdnT"].ap().rearrange("(it p) d -> p it d", p=128)[
                      :, 4 * g:4 * g + 4, :])
        wd_t = c.val("duwd")
        st = {"xb": None, "xf": [None, None], "et": [None, None],
              "psU": [None, None], "psP": [None, None], "U": None,
              "wu": [None, None]}
        state = {}
        for chk in range(T // 512):
            c0 = chk * 512
            c.dma("sync", "dux",
                  xb[:, :].rearrange("p (dt cc) -> p dt cc", cc=512),
                  src_bf.rearrange("(dt p) t -> p dt t", p=128)[
                      :, :, c0:c0 + 512], w=(st["xb"] or []))
            xb_t = c.val("dux")
            uw = (st["U"] or [])
            for q in range(4):
                wb = q % 2
                ws = f"duw{wb}"
                c.dma("gpsimd", ws,
                      Wup[wb][:, :].rearrange("p (dt i) -> p dt i", i=1024),
                      din["WupT"].ap().rearrange("(dt p) i -> p dt i", p=128)[
                          :, :, q * 1024:(q + 1) * 1024],
                      w=(st["wu"][wb] or []))
                w_t = c.val(ws)
                for itl in range(8):
                    ub = itl % 2
                    for dt in range(NDT):
                        w = []
                        if dt == 0:
                            w = [(ws, w_t), ("dux", xb_t),
                                 ("aE", c.val("aE"))] + \
                                (uw if q == 0 and itl == 0 else [])
                        c.t(lambda e, q=q, itl=itl, dt=dt, ub=ub, wb=wb:
                            e.matmul(psU[ub][:, :],
                                     Wup[wb][:, dt * 1024 + itl * 128:
                                             dt * 1024 + (itl + 1) * 128],
                                     xb[:, dt * 512:(dt + 1) * 512],
                                     start=(dt == 0), stop=(dt == NDT - 1)),
                            w=w, inc=("tE" if dt == NDT - 1 else None))
                    ut = c.val("tE")
                    c.a(lambda e, q=q, itl=itl, ub=ub: e.activation(
                        Usb[:, (8 * q + itl) * 512:(8 * q + itl + 1) * 512],
                        psU[ub][:, :], AF.Silu, bias=cm.c(C_ZERO),
                        scale=cm.c(C_ONE)), w=[("tE", ut)])
                st["wu"][wb] = [("tE", c.val("tE"))]
            u_all = c.val("aE")

            def make_z(c, dt, ztile, chk=chk, u_all=u_all):
                pb = dt % 2
                for it in range(32):
                    w = []
                    if it == 0:
                        w = [("aE", u_all), ("duwd", wd_t)] + \
                            (st["psP"][pb] or [])
                    c.t(lambda e, dt=dt, it=it, pb=pb: e.matmul(
                        psP[pb][:, :],
                        Wdn[it // 4][:, (it % 4) * 1024 + dt * 128:
                                     (it % 4) * 1024 + (dt + 1) * 128],
                        Usb[:, it * 512:(it + 1) * 512],
                        start=(it == 0), stop=(it == 31)), w=w,
                        inc=("tE" if it == 31 else None))
                pt = c.val("tE")
                b = dt % 2
                c.dma("sync", f"due{b}", etl[b][:, :],
                      expdT[dt * 128:(dt + 1) * 128, chk * 512:(chk + 1) * 512],
                      w=(st["et"][b] or []))
                c.dma("sync", f"duf{b}", xft[b][:, :],
                      src_f32[dt * 128:(dt + 1) * 128,
                              chk * 512:(chk + 1) * 512],
                      w=(st["xf"][b] or []))
                zf = state.get("z_free") if dt == 0 else None
                c.v(lambda e, dt=dt, pb=pb, b=b: e.tensor_tensor(
                    ztile[:, :], psP[pb][:, :], etl[b][:, :], ALU.add),
                    w=[("tE", pt), (f"due{b}", c.val(f"due{b}")),
                       (f"duf{b}", c.val(f"duf{b}"))] + (zf or []))
                c.v(lambda e, dt=dt, b=b: e.tensor_tensor(
                    ztile[:, :], ztile[:, :], xft[b][:, :], ALU.add))
                zt = c.val("vE")
                st["psP"][pb] = [("vE", zt)]
                st["et"][b] = [("vE", zt)]
                st["xf"][b] = [("vE", zt)]
                return [("vE", zt)]

            emit_epilogue_chunk(c, cm, ep, chk, make_z, dst_f32, dst_bf, state)
            st["U"] = [("tE", c.val("tE"))]
            st["xb"] = [("tE", c.val("tE"))]
        c.bar()

    return logic



def phase_projepi(nc, cm, din, wkey, brow, src_f32, actT, dst_f32, dst_bf, E):
    ep = alloc_epilogue(nc, E, 512)
    u = _EP_UNIQ[0]
    Wsb = [E(nc.sbuf_tensor(f"pe{u}_w{i}", [128, D], BF16)) for i in range(NDT)]
    ytl = [E(nc.sbuf_tensor(f"pe{u}_y{i}", [128, 512], BF16)) for i in range(8)]
    xft = [E(nc.sbuf_tensor(f"pe{u}_xf{i}", [128, 512], F32)) for i in range(2)]
    bo_sb = E(nc.sbuf_tensor(f"pe{u}_bo", [1, D], F32))
    psP = [E(nc.psum_tensor(f"pe{u}_psP{i}", [128, 512], F32)) for i in range(2)]

    def logic(c):
        for dt in range(NDT):
            c.dma("gpsimd", "pw", Wsb[dt][:, :],
                  din[wkey].ap()[dt * 128:(dt + 1) * 128, :])
        if brow is not None:
            c.dma("sync", "cs", bo_sb[:, :],
                  din["bqkvo"].ap()[brow:brow + 1, :])
        wt = c.val("pw")
        bt = c.val("cs")
        state = {}
        yfree = [None] * 8
        xfree = [None, None]
        pfree = [None, None]
        for chk in range(T // 512):
            ready = []
            for et in range(NDT):
                ys = f"pyl{et}"
                c.dma("sync", ys, ytl[et][:, :],
                      actT[et * 128:(et + 1) * 128,
                           chk * 512:(chk + 1) * 512],
                      w=(yfree[et] or []))
                ready.append((ys, c.val(ys)))

            def make_z(c, dt, ztile, chk=chk, ready=ready):
                pb = dt % 2
                for et in range(NDT):
                    w = [ready[et], ("pw", wt)]
                    if et == 0:
                        w += (pfree[pb] or [])
                    c.t(lambda e, dt=dt, et=et, pb=pb: e.matmul(
                        psP[pb][:, :], Wsb[et][:, dt * 128:(dt + 1) * 128],
                        ytl[et][:, :], start=(et == 0),
                        stop=(brow is None and et == NDT - 1)), w=w,
                        inc=("tE" if et == NDT - 1 and brow is None else None))
                if brow is not None:
                    c.t(lambda e, dt=dt, pb=pb: e.matmul(
                        psP[pb][:, :], bo_sb[:, dt * 128:(dt + 1) * 128],
                        cm.ones_row_f512[:, :], start=False, stop=True),
                        w=[("cs", bt)], inc="tE")
                pt = c.val("tE")
                xb_ = dt % 2
                xs = f"pxf{xb_}"
                c.dma("sync", xs, xft[xb_][:, :],
                      src_f32[dt * 128:(dt + 1) * 128,
                              chk * 512:(chk + 1) * 512],
                      w=(xfree[xb_] or []))
                zf = state.get("z_free") if dt == 0 else None
                c.v(lambda e, dt=dt, pb=pb, xb_=xb_: e.tensor_tensor(
                    ztile[:, :], psP[pb][:, :], xft[xb_][:, :], ALU.add),
                    w=[("tE", pt), (xs, c.val(xs))] + (zf or []))
                zt = c.val("vE")
                pfree[pb] = [("vE", zt)]
                xfree[xb_] = [("vE", zt)]
                return [("vE", zt)]

            emit_epilogue_chunk(c, cm, ep, chk, make_z, dst_f32, dst_bf, state)
            for et in range(NDT):
                yfree[et] = [("tE", c.val("tE"))]
        c.bar()

    return logic



PHASE_ORDER = ("l0attn", "l0ffn", "l1attn", "l1ffn")


def build_program(phases=PHASE_ORDER):
    nc = bass.Bass("TRN2", target_bir_lowering=False)
    din = declare_inputs(nc)
    zA = nc.dram_tensor("zA", [D, T], F32, kind="Internal")
    zB = nc.dram_tensor("zB", [D, T], F32, kind="Internal")
    zAb = nc.dram_tensor("zAb", [D, T], BF16, kind="Internal")
    zBb = nc.dram_tensor("zBb", [D, T], BF16, kind="Internal")
    x_bf = nc.dram_tensor("x_bf", [D, T], BF16, kind="Internal")
    ymoe = nc.dram_tensor("ymoe", [D, T], F32,
                          kind=("ExternalOutput" if DEV.get("dump_ymoe")
                                else "Internal"))
    selT_dram = nc.dram_tensor("selT", [8, T], F32,
                               kind=("ExternalOutput" if DEV.get("dump_sel")
                                     else "Internal"))
    tdu = nc.dram_tensor("tdu", [4096, 2048], BF16, kind="Internal")
    attO = nc.dram_tensor("attO", [D, T], BF16,
                          kind=("ExternalOutput" if DEV.get("dump_attO")
                                else "Internal"))
    expdT = nc.dram_tensor("expdT", [D, T], F32, kind="Internal")
    mscr = nc.dram_tensor("mscr", [4, 17340672], BF16, kind="Internal")
    outT = nc.dram_tensor("outT", [D, T], F32, kind="ExternalOutput")

    slots = [(zA.ap(), zAb.ap()), (zB.ap(), zBb.ap())]
    cnt = {}
    with ExitStack() as top:
        pool = SemPool(nc, top)
        cm = Common(nc, top.enter_context)
        block = top.enter_context(nc.Block())

        with ExitStack() as ph:
            plogic = phase_prep(nc, cm, din, din["xT"].ap(), x_bf.ap(), tdu,
                                ph.enter_context)

            def first(c):
                emit_common(c, cm, din)
                plogic(c)

            cnt = run_phase(block, pool, cnt, first)

        cur_f, cur_b = din["xT"].ap(), x_bf.ap()
        for pi, pname in enumerate(phases):
            last = pi == len(phases) - 1
            dst_f, dst_b = (outT.ap(), None) if last else slots[pi % 2]
            if pname == "l0attn":
                with ExitStack() as ph:
                    logic = phase_l0attn(nc, cm, din, cur_f, cur_b, mscr,
                                         attO.ap(), ph.enter_context)
                    cnt = run_phase(block, pool, cnt, logic)
                with ExitStack() as ph:
                    logic = phase_projepi(nc, cm, din, "WoT", 3, cur_f,
                                          attO.ap(), dst_f, dst_b,
                                          ph.enter_context)
                    cnt = run_phase(block, pool, cnt, logic)
            elif pname == "l0ffn":
                with ExitStack() as ph:
                    logic = phase_l0ffn_gate(nc, cm, din, cur_f, selT_dram,
                                             ph.enter_context)
                    cnt = run_phase(block, pool, cnt, logic)
                with ExitStack() as ph:
                    logic = phase_l0ffn_experts(nc, cm, din, cur_b, ymoe,
                                                selT_dram, ph.enter_context)
                    cnt = run_phase(block, pool, cnt, logic)
                with ExitStack() as ph:
                    logic = phase_l0ffn_epi(nc, cm, din, cur_f, ymoe, dst_f,
                                            dst_b, ph.enter_context)
                    cnt = run_phase(block, pool, cnt, logic)
            elif pname == "l1ffn":
                with ExitStack() as ph:
                    logic = phase_l1ffn_peer(nc, cm, din, cur_f, cur_b,
                                             tdu, expdT.ap(),
                                             ph.enter_context)
                    cnt = run_phase(block, pool, cnt, logic)
                with ExitStack() as ph:
                    logic = phase_l1ffn_dense(nc, cm, din, cur_f, cur_b,
                                              expdT.ap(), dst_f, dst_b,
                                              ph.enter_context)
                    cnt = run_phase(block, pool, cnt, logic)
            elif pname == "l1attn":
                with ExitStack() as ph:
                    logic = phase_l1attn_main(nc, cm, din, cur_b, attO.ap(),
                                              ph.enter_context)
                    cnt = run_phase(block, pool, cnt, logic)
                with ExitStack() as ph:
                    logic = phase_projepi(nc, cm, din, "w_out", None, cur_f,
                                          attO.ap(), dst_f, dst_b,
                                          ph.enter_context)
                    cnt = run_phase(block, pool, cnt, logic)
            else:
                raise NotImplementedError(pname)
            cur_f, cur_b = dst_f, dst_b

        def fin(c):
            c.bar()

        cnt = run_phase(block, pool, cnt, fin)
    return nc


def kernel(**inputs):
    sh = host_prep(inputs)
    x = np.asarray(inputs["L0_attn_x"])
    nc = build_program()
    in_maps = []
    for b in range(8):
        m = dict(sh)
        m["xT"] = np.ascontiguousarray(x[b].T.astype(np.float32))
        in_maps.append(m)
    res = run_bass_kernel_spmd(nc, in_maps, core_ids=list(range(8)))
    out = np.empty((8, T, D), np.float32)
    for b in range(8):
        out[b] = res.results[b]["outT"].T
    return out


# version 03
# speedup vs baseline: 1.2136x; 1.2136x over previous
from contextlib import ExitStack

import numpy as np

import concourse.bass as bass
import concourse.mybir as mybir
from concourse.bass_types import AP
from concourse.bass_core_rust import IndirectOffsetOnAxis
from concourse.bass_utils import run_bass_kernel_spmd

F32 = mybir.dt.float32
BF16 = mybir.dt.bfloat16
I32 = mybir.dt.int32
U32 = mybir.dt.uint32
AF = mybir.ActivationFunctionType
ALU = mybir.AluOpType
AXX = mybir.AxisListType.X

D = 1024
T = 4096
NDT = D // 128

ENGINES = ("sync", "tensor", "scalar", "vector", "gpsimd")
ENG_SEM = {"sync": "sE", "tensor": "tE", "scalar": "aE", "vector": "vE",
           "gpsimd": "gE"}

C_EXPB, C_ONE, C_QS, C_ZERO, C_EPS = range(5)
CONST_VALS = [-20.0, 1.0, 0.125, 0.0, 1e-5]

DEV = {}
SELF_EDGE = True


class SemPool:

    def __init__(self, nc, stack):
        self.nc = nc
        self.stack = stack
        self.handles = {}

    def get(self, name):
        if name not in self.handles:
            self.handles[name] = self.stack.enter_context(self.nc.semaphore(name))
        return self.handles[name]


class Ctx:

    def __init__(self, active, eng, pool, base):
        self.active = active
        self.eng = eng
        self.pool = pool
        self.cnt = dict(base)

    def val(self, s):
        return self.cnt.get(s, 0)

    def op(self, engname, fn, w=(), inc=None, n=1):
        tgt = None
        if inc is not None:
            self.cnt[inc] = self.cnt.get(inc, 0) + n
            tgt = self.cnt[inc]
        if engname == self.active:
            for s, v in w:
                if v > 0:
                    self.eng.wait_ge(self.pool.get(s), v)
            ins = fn(self.eng)
            if inc is not None:
                ins.then_inc(self.pool.get(inc), n)
        return tgt

    def t(self, fn, w=(), inc=None):
        return self.op("tensor", fn, w, inc)

    def a(self, fn, w=(), inc="aE"):
        if SELF_EDGE:
            w = list(w) + [("aE", self.val("aE"))]
        return self.op("scalar", fn, w, inc)

    def v(self, fn, w=(), inc="vE"):
        if SELF_EDGE:
            w = list(w) + [("vE", self.val("vE"))]
        return self.op("vector", fn, w, inc)

    def g(self, fn, w=(), inc="gE"):
        if SELF_EDGE:
            w = list(w) + [("gE", self.val("gE"))]
        return self.op("gpsimd", fn, w, inc)

    def dma(self, q, sem, out_ap, in_ap, w=(), **kw):
        return self.op(q, lambda e: e.dma_start(out_ap, in_ap, **kw), w, sem, 16)

    def idma(self, sem, out_ap, in_ap, in_off, w=()):
        return self.op("gpsimd",
                       lambda e: e.indirect_dma_start(out_ap, None, in_ap,
                                                      in_off),
                       w, sem, 16)

    def bar(self):
        bname = f"bar{self.cnt.get('_barid', 0)}"
        self.cnt["_barid"] = self.cnt.get("_barid", 0) + 1
        for e in ENGINES:
            self.op(e, lambda x: x.nop(), (), bname, 1)
        waits = [(bname, 5)]
        waits += [(s, v) for s, v in sorted(self.cnt.items())
                  if s not in ENG_SEM.values() and not s.startswith("bar")
                  and s != "_barid" and v > 0]
        for e in ENGINES:
            self.op(e, lambda x: x.nop(), waits)


def run_phase(block, pool, base_cnt, phase_logic):
    finals = []
    for engname in ENGINES:
        holder = {}

        def section(eng, engname=engname, holder=holder):
            c = Ctx(engname, eng, pool, base_cnt)
            phase_logic(c)
            holder["cnt"] = c.cnt

        getattr(block, engname)(section)
        finals.append(holder["cnt"])
    for f in finals[1:]:
        assert f == finals[0], "phase emission diverged between engines"
    return finals[0]



def g_of_d(d):
    idx = d + 2047
    idx = np.where(idx < 0, idx + 4095, idx)
    return np.clip(idx, 0, 4094)


def host_prep(inputs):
    f = np.float32

    def t(a):
        return np.ascontiguousarray(np.asarray(a).astype(f).T)

    sh = {}
    sh["WqT"] = t(inputs["L0_attn_Wq"])
    sh["WkT"] = t(inputs["L0_attn_Wk"])
    sh["WvT"] = t(inputs["L0_attn_Wv"])
    sh["WoT"] = t(inputs["L0_attn_Wo"])
    sh["bqkvo"] = np.stack([np.asarray(inputs["L0_attn_bq"]),
                            np.asarray(inputs["L0_attn_bk"]),
                            np.asarray(inputs["L0_attn_bv"]),
                            np.asarray(inputs["L0_attn_bo"])]).astype(f)
    RT = np.asarray(inputs["L0_attn_rel_emb"])[g_of_d(np.arange(-4095, 4096))]
    sh["RT"] = np.ascontiguousarray(RT.T.astype(f))
    sh["Wg"] = np.asarray(inputs["L0_ffn_Wg"]).astype(f)
    sh["bg"] = np.asarray(inputs["L0_ffn_bg"]).astype(f).reshape(1, 8)
    sh["W1"] = np.asarray(inputs["L0_ffn_W1"]).astype(f)
    sh["b1r"] = np.ascontiguousarray(np.asarray(inputs["L0_ffn_b1"]).astype(f)
                                     .reshape(8, 32, 128).transpose(2, 0, 1)
                                     .reshape(128, 256))
    sh["W2"] = np.asarray(inputs["L0_ffn_W2"]).astype(f)
    sh["b2r"] = np.ascontiguousarray(np.asarray(inputs["L0_ffn_b2"]).astype(f)
                                     .reshape(8, 8, 128).transpose(2, 0, 1)
                                     .reshape(128, 64))
    sh["wT"] = t(inputs["L1_attn_w"])
    sh["w_out"] = np.asarray(inputs["L1_attn_w_out"]).astype(f)
    sh["WupT"] = t(inputs["L1_ffn_W_up"])
    sh["WdnT"] = t(inputs["L1_ffn_W_down"])
    sh["Wq2T"] = t(inputs["L1_ffn_W_q"])
    sh["keysT"] = np.ascontiguousarray(
        np.asarray(inputs["L1_ffn_keys"]).astype(f).transpose(2, 0, 3, 1))
    sh["de"] = np.asarray(inputs["L1_ffn_down_embed"]).astype(f)
    sh["ue"] = np.asarray(inputs["L1_ffn_up_embed"]).astype(f)
    sh["consts"] = np.tile(np.array(CONST_VALS, f)[None, :], (128, 1))
    return sh


def declare_inputs(nc):
    d = {}
    for name, shape in [
        ("xT", [D, T]),
        ("WqT", [D, D]), ("WkT", [D, D]), ("WvT", [D, D]), ("WoT", [D, D]),
        ("bqkvo", [4, D]), ("RT", [64, 8191]),
        ("Wg", [D, 8]), ("bg", [1, 8]), ("W1", [8, D, 4096]),
        ("b1r", [128, 256]), ("W2", [8, 4096, D]), ("b2r", [128, 64]),
        ("wT", [D, 3 * D]), ("w_out", [D, D]),
        ("WupT", [D, 4096]), ("WdnT", [4096, D]), ("Wq2T", [D, 512]),
        ("keysT", [2, 4, 64, 64]), ("de", [4096, D]), ("ue", [4096, D]),
        ("consts", [128, 5]),
    ]:
        d[name] = nc.dram_tensor(name, shape, F32, kind="ExternalInput")
    return d



class Common:
    def __init__(self, nc, E):
        self.consts = E(nc.sbuf_tensor("cc_consts", [128, 5], F32))
        self.ones_col_f = E(nc.sbuf_tensor("cc_onc_f", [128, 1], F32))
        self.ones_col_b = E(nc.sbuf_tensor("cc_onc_b", [128, 1], BF16))
        self.ones_row_f = E(nc.sbuf_tensor("cc_onr_f", [1, 128], F32))
        self.ones_row_f512 = E(nc.sbuf_tensor("cc_onr_f5", [1, 512], F32))
        self.ident_b = E(nc.sbuf_tensor("cc_id_b", [128, 128], BF16))
        self.ident_f = E(nc.sbuf_tensor("cc_id_f", [128, 128], F32))
        self.iota_d = E(nc.sbuf_tensor("cc_iod", [128, 128], F32))
        self.iota8 = E(nc.sbuf_tensor("cc_io8", [128, 8], F32))
        self.iota64 = E(nc.sbuf_tensor("cc_io64", [128, 64], F32))
        self.tril_b = E(nc.sbuf_tensor("cc_tril", [128, 128], BF16))
        self.ones2 = E(nc.sbuf_tensor("cc_on2", [128, 2], BF16))
        self.sel2T = E(nc.sbuf_tensor("cc_s2t", [2, 128], F32))
        self.ones_t128 = E(nc.sbuf_tensor("cc_o128", [128, 128], F32))

    def c(self, col):
        return self.consts[:, col:col + 1]

    def c1(self, col):
        return self.consts[0:1, col:col + 1]


def emit_common(c, cm, din):
    c.dma("sync", "cs", cm.consts[:, :], din["consts"].ap())
    c.op("vector", lambda e: e.nop(), [("cs", c.val("cs"))])
    c.v(lambda e: e.memset(cm.ones_col_f[:, :], 1.0))
    c.v(lambda e: e.memset(cm.ones_col_b[:, :], 1.0))
    c.v(lambda e: e.memset(cm.ones_row_f[:, :], 1.0))
    c.v(lambda e: e.memset(cm.ones_row_f512[:, :], 1.0))
    c.v(lambda e: e.memset(cm.ones2[:, :], 0.0))
    c.v(lambda e: e.memset(cm.ones2[0:64, 0:1], 1.0))
    c.v(lambda e: e.memset(cm.ones2[64:128, 1:2], 1.0))
    c.g(lambda e: e.iota(cm.iota_d[:, :], pattern=[[1, 128]], base=0,
                         channel_multiplier=-1,
                         allow_small_or_imprecise_dtypes=True))
    io_t = c.val("gE")
    c.g(lambda e: e.iota(cm.iota8[:, :], pattern=[[1, 8]], base=0,
                         channel_multiplier=0,
                         allow_small_or_imprecise_dtypes=True))
    c.g(lambda e: e.iota(cm.iota64[:, :], pattern=[[1, 64]], base=0,
                         channel_multiplier=0,
                         allow_small_or_imprecise_dtypes=True))
    c.g(lambda e: e.iota(cm.ones_t128[0:2, 0:128], pattern=[[1, 128]], base=0,
                         channel_multiplier=-64,
                         allow_small_or_imprecise_dtypes=True))
    s2t = c.val("gE")
    c.v(lambda e: e.tensor_scalar(cm.sel2T[:, :], cm.ones_t128[0:2, 0:128],
                                  63.0, None, ALU.is_le), w=[("gE", s2t)])
    c.v(lambda e: e.tensor_scalar(cm.ones_t128[0:2, 0:128],
                                  cm.ones_t128[0:2, 0:128], 0.0, None,
                                  ALU.is_ge))
    c.v(lambda e: e.tensor_tensor(cm.sel2T[:, :], cm.sel2T[:, :],
                                  cm.ones_t128[0:2, 0:128], ALU.mult))
    c.v(lambda e: e.memset(cm.ones_t128[:, :], 1.0))
    c.v(lambda e: e.tensor_scalar(cm.ident_b[:, :], cm.iota_d[:, :], 0.0, None,
                                  ALU.is_equal), w=[("gE", io_t)])
    c.v(lambda e: e.tensor_scalar(cm.ident_f[:, :], cm.iota_d[:, :], 0.0, None,
                                  ALU.is_equal))
    c.v(lambda e: e.tensor_scalar(cm.tril_b[:, :], cm.iota_d[:, :], 0.0, None,
                                  ALU.is_ge))
    c.bar()



def phase_prep(nc, cm, din, src_f32, dst_bf, tdu, E):
    tf = [E(nc.sbuf_tensor(f"pp_tf{i}", [128, 4096], F32)) for i in range(2)]
    tb = [E(nc.sbuf_tensor(f"pp_tb{i}", [128, 4096], BF16)) for i in range(2)]

    def logic(c):
        if DEV.get("skip_prep"):
            c.bar()
            return
        for dt in range(NDT):
            c.dma("gpsimd", "pxc", dst_bf[dt * 128:(dt + 1) * 128, :],
                  src_f32[dt * 128:(dt + 1) * 128, :])
        for half, tab in ((0, din["de"]), (1, din["ue"])):
            for rt in range(8):
                b = rt % 2
                ls, ss = f"ppt{b}", f"ppu{b}"
                c.dma("sync", ls, tf[b][:, :].rearrange("p (a d) -> p a d",
                                                        d=1024),
                      tab.ap()[rt * 512:(rt + 1) * 512, :]
                      .rearrange("(a p) d -> p a d", p=128),
                      w=[("vE", c.val("vE"))])
                c.v(lambda e, b=b: e.tensor_copy(tb[b][:, :], tf[b][:, :]),
                    w=[(ls, c.val(ls)), (ss, c.val(ss))])
                ct = c.val("vE")
                c.dma("sync", ss,
                      tdu.ap()[rt * 512:(rt + 1) * 512,
                               half * 1024:(half + 1) * 1024]
                      .rearrange("(a p) d -> p a d", p=128),
                      tb[b][:, :].rearrange("p (a d) -> p a d", d=1024),
                      w=[("vE", ct)])
        c.bar()

    return logic



_EP_UNIQ = [0]


def alloc_epilogue(nc, E, ch=512):
    u = _EP_UNIQ[0]
    _EP_UNIQ[0] += 1
    ep = {"CH": ch}
    ep["z"] = [E(nc.sbuf_tensor(f"ep{u}_z{i}", [128, ch], F32))
               for i in range(NDT)]
    ep["sq"] = [E(nc.sbuf_tensor(f"ep{u}_sq{i}", [128, ch], F32))
                for i in range(2)]
    ep["ybf"] = [E(nc.sbuf_tensor(f"ep{u}_yb{i}", [128, ch], BF16))
                 for i in range(2)]
    for nm in ("m", "m2", "var", "sd", "rstd", "mr"):
        ep[nm] = E(nc.sbuf_tensor(f"ep{u}_{nm}", [1, ch], F32))
    ep["ps_s"] = E(nc.psum_tensor(f"ep{u}_ps_s", [1, ch], F32))
    ep["ps_q"] = E(nc.psum_tensor(f"ep{u}_ps_q", [1, ch], F32))
    ep["ps_r"] = E(nc.psum_tensor(f"ep{u}_ps_r", [128, ch], F32))
    ep["ps_m"] = E(nc.psum_tensor(f"ep{u}_ps_m", [128, ch], F32))
    return ep


def emit_epilogue_chunk(c, cm, ep, ch_idx, make_z, dst_f32, dst_bf, state):
    CH = ep["CH"]
    c0 = ch_idx * CH
    zrdy = []
    for dt in range(NDT):
        zrdy.append(make_z(c, dt, ep["z"][dt]))
    if DEV.get("epi_mode", 3) == 1:
        for dt in range(NDT):
            c.dma("sync", "ezs", dst_f32[dt * 128:(dt + 1) * 128, c0:c0 + CH],
                  ep["z"][dt][:, :], w=list(zrdy[dt]))
        state["z_free"] = [("ezs", c.val("ezs"))]
        state["bc_free"] = []
        return
    for dt in range(NDT):
        b = dt % 2
        c.a(lambda e, dt=dt, b=b: e.activation(ep["sq"][b][:, :],
                                               ep["z"][dt][:, :], AF.Square,
                                               bias=cm.c(C_ZERO),
                                               scale=cm.c(C_ONE)),
            w=list(zrdy[dt]) + (state.get(f"sqf{b}") or []))
        sqt = c.val("aE")
        stw = list(zrdy[dt])
        if dt == 0 and state.get("stat_free"):
            stw += state["stat_free"]
        c.t(lambda e, dt=dt: e.matmul(ep["ps_s"][0:1, :], cm.ones_col_f[:, :],
                                      ep["z"][dt][:, :], start=(dt == 0),
                                      stop=(dt == NDT - 1)), w=stw)
        c.t(lambda e, b=b: e.matmul(ep["ps_q"][0:1, :], cm.ones_col_f[:, :],
                                    ep["sq"][b][:, :], start=(dt == 0),
                                    stop=(dt == NDT - 1)),
            w=[("aE", sqt)], inc="tE")
        state[f"sqf{b}"] = [("tE", c.val("tE"))]
    mmt = c.val("tE")
    if DEV.get("epi_mode", 3) == 2:
        for dt in range(NDT):
            c.dma("sync", "ezs", dst_f32[dt * 128:(dt + 1) * 128, c0:c0 + CH],
                  ep["z"][dt][:, :], w=list(zrdy[dt]) + [("tE", mmt)])
        state["z_free"] = [("ezs", c.val("ezs"))]
        state["bc_free"] = []
        return
    c.v(lambda e: e.tensor_scalar(ep["m"][:, :], ep["ps_s"][0:1, :], 1.0 / D,
                                  None, ALU.mult), w=[("tE", mmt)])
    c.v(lambda e: e.tensor_scalar(ep["var"][:, :], ep["ps_q"][0:1, :], 1.0 / D,
                                  None, ALU.mult))
    c.v(lambda e: e.tensor_tensor(ep["m2"][:, :], ep["m"][:, :], ep["m"][:, :],
                                  ALU.mult))
    c.v(lambda e: e.tensor_tensor(ep["var"][:, :], ep["var"][:, :],
                                  ep["m2"][:, :], ALU.subtract))
    c.v(lambda e: e.tensor_scalar(ep["var"][:, :], ep["var"][:, :], 1e-5, None,
                                  ALU.add))
    vt = c.val("vE")
    state["stat_free"] = [("vE", vt)]
    c.a(lambda e: e.activation(ep["sd"][:, :], ep["var"][:, :], AF.Sqrt,
                               bias=cm.c1(C_ZERO), scale=cm.c1(C_ONE)),
        w=[("vE", vt)])
    at = c.val("aE")
    c.v(lambda e: e.reciprocal(ep["rstd"][:, :], ep["sd"][:, :]),
        w=[("aE", at)])
    rt = c.val("vE")
    c.v(lambda e: e.tensor_tensor(ep["mr"][:, :], ep["m"][:, :],
                                  ep["rstd"][:, :], ALU.mult))
    mrt = c.val("vE")
    bw = state.get("bc_free") or []
    c.t(lambda e: e.matmul(ep["ps_r"][:, :], cm.ones_row_f[:, :],
                           ep["rstd"][:, :], start=True, stop=True),
        w=[("vE", rt)] + bw)
    c.t(lambda e: e.matmul(ep["ps_m"][:, :], cm.ones_row_f[:, :],
                           ep["mr"][:, :], start=True, stop=True),
        w=[("vE", mrt)], inc="tE")
    bct = c.val("tE")
    if DEV.get("epi_mode", 3) == 25:
        for dt in range(NDT):
            c.dma("sync", "ezs", dst_f32[dt * 128:(dt + 1) * 128, c0:c0 + CH],
                  ep["z"][dt][:, :], w=list(zrdy[dt]) + [("tE", bct)])
        state["z_free"] = [("ezs", c.val("ezs"))]
        state["bc_free"] = [("tE", bct)]
        return
    if DEV.get("epi_mode", 3) == 31:
        for dt in range(NDT):
            c.v(lambda e, dt=dt: e.tensor_tensor(ep["z"][dt][:, :],
                                                 ep["z"][dt][:, :],
                                                 ep["ps_r"][:, :], ALU.mult),
                w=[("tE", bct)])
            yt = c.val("vE")
            c.dma("sync", "ezs", dst_f32[dt * 128:(dt + 1) * 128, c0:c0 + CH],
                  ep["z"][dt][:, :], w=[("vE", yt)])
        state["z_free"] = [("ezs", c.val("ezs"))]
        state["bc_free"] = [("vE", c.val("vE"))]
        return
    if DEV.get("epi_mode", 3) == 37:
        c.t(lambda e: e.matmul(ep["ps_r"][:, :], cm.ident_f[:, :],
                               ep["z"][0][:, :], start=True, stop=True),
            w=[("tE", bct)], inc="tE")
        k128t = c.val("tE")
        c.a(lambda e: e.activation(ep["sq"][0][:, :], ep["ps_r"][:, :],
                                   AF.Identity, bias=cm.c(C_ZERO),
                                   scale=cm.c(C_ONE)), w=[("tE", k128t)])
        cpt = c.val("aE")
        c.dma("sync", "ezs", dst_f32[0:128, c0:c0 + CH], ep["sq"][0][:, :],
              w=[("aE", cpt)])
        for dt in range(1, NDT):
            c.dma("sync", "ezs", dst_f32[dt * 128:(dt + 1) * 128, c0:c0 + CH],
                  ep["z"][dt][:, :], w=list(zrdy[dt]) + [("tE", bct)])
        state["z_free"] = [("ezs", c.val("ezs"))]
        state["bc_free"] = [("aE", cpt)]
        return
    if DEV.get("epi_mode", 3) == 34:
        c.a(lambda e: e.activation(ep["sq"][0][:, :], ep["ps_r"][:, :],
                                   AF.Identity, bias=cm.c(C_ZERO),
                                   scale=cm.c(C_ONE)), w=[("tE", bct)])
        cpt = c.val("aE")
        c.dma("sync", "ezs", dst_f32[0:128, c0:c0 + CH], ep["sq"][0][:, :],
              w=[("aE", cpt)])
        for dt in range(1, NDT):
            c.dma("sync", "ezs", dst_f32[dt * 128:(dt + 1) * 128, c0:c0 + CH],
                  ep["z"][dt][:, :], w=list(zrdy[dt]) + [("tE", bct)])
        state["z_free"] = [("ezs", c.val("ezs"))]
        state["bc_free"] = [("aE", cpt)]
        return
    if DEV.get("epi_mode", 3) == 33:
        for dt in range(NDT):
            b = dt % 2
            c.v(lambda e, dt=dt, b=b: e.tensor_tensor(ep["sq"][b][:, :],
                                                      ep["z"][dt][:, :],
                                                      ep["ps_r"][:, :],
                                                      ALU.mult),
                w=[("tE", bct), (f"ez{b}", c.val(f"ez{b}"))])
            yt = c.val("vE")
            c.dma("sync", f"ez{b}",
                  dst_f32[dt * 128:(dt + 1) * 128, c0:c0 + CH],
                  ep["sq"][b][:, :], w=[("vE", yt)])
        state["z_free"] = [("ez0", c.val("ez0")), ("ez1", c.val("ez1"))]
        state["bc_free"] = [("vE", c.val("vE"))]
        return
    if DEV.get("epi_mode", 3) == 32:
        c.a(lambda e: e.activation(ep["sq"][0][:, :], ep["ps_r"][:, :],
                                   AF.Identity, bias=cm.c(C_ZERO),
                                   scale=cm.c(C_ONE)), w=[("tE", bct)])
        c.a(lambda e: e.activation(ep["sq"][1][:, :], ep["ps_m"][:, :],
                                   AF.Identity, bias=cm.c(C_ZERO),
                                   scale=cm.c(C_ONE)))
        cpt = c.val("aE")
        for dt in range(NDT):
            c.v(lambda e, dt=dt: e.tensor_tensor(ep["z"][dt][:, :],
                                                 ep["z"][dt][:, :],
                                                 ep["sq"][0][:, :], ALU.mult),
                w=[("aE", cpt)])
            c.v(lambda e, dt=dt: e.tensor_tensor(ep["z"][dt][:, :],
                                                 ep["z"][dt][:, :],
                                                 ep["sq"][1][:, :],
                                                 ALU.subtract))
            yt = c.val("vE")
            c.dma("sync", "ezs", dst_f32[dt * 128:(dt + 1) * 128, c0:c0 + CH],
                  ep["z"][dt][:, :], w=[("vE", yt)])
        state["z_free"] = [("ezs", c.val("ezs"))]
        state["bc_free"] = [("aE", cpt)]
        return
    for dt in range(NDT):
        c.v(lambda e, dt=dt: e.tensor_tensor(ep["z"][dt][:, :],
                                             ep["z"][dt][:, :],
                                             ep["ps_r"][:, :], ALU.mult),
            w=[("tE", bct)])
        c.v(lambda e, dt=dt: e.tensor_tensor(ep["z"][dt][:, :],
                                             ep["z"][dt][:, :],
                                             ep["ps_m"][:, :], ALU.subtract))
        yt = c.val("vE")
        c.dma("sync", "ezs", dst_f32[dt * 128:(dt + 1) * 128, c0:c0 + CH],
              ep["z"][dt][:, :], w=[("vE", yt)])
        if dst_bf is not None:
            b = dt % 2
            yb = f"eyb{b}"
            c.a(lambda e, dt=dt, b=b: e.activation(ep["ybf"][b][:, :],
                                                   ep["z"][dt][:, :],
                                                   AF.Identity,
                                                   bias=cm.c(C_ZERO),
                                                   scale=cm.c(C_ONE)),
                w=[("vE", yt), (yb, c.val(yb))])
            ybt = c.val("aE")
            c.dma("sync", yb, dst_bf[dt * 128:(dt + 1) * 128, c0:c0 + CH],
                  ep["ybf"][b][:, :], w=[("aE", ybt)])
    state["bc_free"] = [("vE", c.val("vE"))]
    state["z_free"] = [("ezs", c.val("ezs"))]
    if dst_bf is not None:
        state["z_free"] += [("eyb0", c.val("eyb0")), ("eyb1", c.val("eyb1"))]



def phase_l0ffn_gate(nc, cm, din, src_f32, selT_dram, E):
    Wg_sb = E(nc.sbuf_tensor("mg_wg", [128, NDT * 8], F32))
    bg_sb = E(nc.sbuf_tensor("mg_bg", [1, 8], F32))
    xf = [E(nc.sbuf_tensor(f"mg_xf{i}", [128, 512], F32)) for i in range(2)]
    lg = [E(nc.sbuf_tensor(f"mg_lg{i}", [8, 512], F32)) for i in range(2)]
    lgn = [E(nc.sbuf_tensor(f"mg_lgn{i}", [128, 8], F32)) for i in range(2)]
    selst = [E(nc.sbuf_tensor(f"mg_sest{i}", [8, 128], F32)) for i in range(2)]
    scr8 = E(nc.sbuf_tensor("mg_scr8", [128, 8], F32))
    mx8 = E(nc.sbuf_tensor("mg_mx8", [128, 8], F32))
    nmx = E(nc.sbuf_tensor("mg_nmx", [128, 1], F32))
    idx8 = E(nc.sbuf_tensor("mg_idx8", [128, 8], U32))
    idxf = E(nc.sbuf_tensor("mg_idxf", [128, 1], F32))
    s8 = E(nc.sbuf_tensor("mg_s8", [128, 1], F32))
    wrec = E(nc.sbuf_tensor("mg_wrec", [128, 1], F32))
    onehot = E(nc.sbuf_tensor("mg_oh", [128, 8], F32))
    selw = E(nc.sbuf_tensor("mg_selw", [128, 8], F32))
    I8f = E(nc.sbuf_tensor("mg_i8f", [8, 8], F32))
    psG = E(nc.psum_tensor("mg_psG", [8, 512], F32))
    psTr = E(nc.psum_tensor("mg_psTr", [128, 8], F32))
    psS = E(nc.psum_tensor("mg_psS", [8, 128], F32))

    def logic(c):
        c.dma("sync", "cs", Wg_sb[:, :].rearrange("p (dt e) -> p dt e", e=8),
              din["Wg"].ap().rearrange("(dt p) e -> p dt e", p=128))
        c.dma("sync", "cs", bg_sb[:, :], din["bg"].ap())
        smt = c.val("cs")
        c.v(lambda e: e.memset(I8f[:, :], 0.0))
        c.v(lambda e: e.tensor_scalar(I8f[:, :], cm.iota_d[0:8, 0:8], 0.0,
                                      None, ALU.is_equal))
        i8t = c.val("vE")
        xf_free = [None, None]
        lg_free = [None, None]
        sel_free = [None, None]
        psG_free = None
        psTr_free = None
        for chk in range(T // 512 if DEV.get("gate", True) else 0):
            for dt in range(NDT):
                b = dt % 2
                xs = f"ggx{b}"
                c.dma("sync", xs, xf[b][:, :],
                      src_f32[dt * 128:(dt + 1) * 128,
                              chk * 512:(chk + 1) * 512],
                      w=(xf_free[b] or []))
                w = [(xs, c.val(xs)), ("cs", smt)]
                if dt == 0 and psG_free:
                    w += psG_free
                c.t(lambda e, dt=dt, b=b: e.matmul(
                    psG[:, :], Wg_sb[:, dt * 8:(dt + 1) * 8], xf[b][:, :],
                    start=(dt == 0), stop=False), w=w, inc="tE")
                xf_free[b] = [("tE", c.val("tE"))]
            c.t(lambda e: e.matmul(psG[:, :], bg_sb[:, :],
                                   cm.ones_row_f512[:, :],
                                   start=False, stop=True),
                w=[("cs", smt)], inc="tE")
            gt = c.val("tE")
            b = chk % 2
            c.v(lambda e, b=b: e.tensor_copy(lg[b][:, :], psG[:, :]),
                w=[("tE", gt)] + (lg_free[b] or []))
            lgt = c.val("vE")
            psG_free = [("vE", lgt)]
            for sub in range(4):
                w = [("vE", lgt), ("vE", i8t)]
                if psTr_free:
                    w += psTr_free
                c.t(lambda e, b=b, sub=sub: e.transpose(
                    psTr[:, :], lg[b][:, sub * 128:(sub + 1) * 128],
                    I8f[:, :]), w=w, inc="tE")
                trt = c.val("tE")
                bb = sub % 2
                c.v(lambda e, bb=bb: e.tensor_copy(lgn[bb][:, :], psTr[:, :]),
                    w=[("tE", trt)])
                psTr_free = [("vE", c.val("vE"))]
                c.v(lambda e, bb=bb: e.max(mx8[:, :], lgn[bb][:, :]))
                c.v(lambda e, bb=bb: e.max_index(idx8[:, :], mx8[:, :],
                                                 lgn[bb][:, :]))
                c.v(lambda e: e.tensor_copy(idxf[:, :], idx8[:, 0:1]))
                c.v(lambda e: e.tensor_scalar(nmx[:, :], mx8[:, 0:1], -1.0,
                                              None, ALU.mult))
                nt = c.val("vE")
                c.a(lambda e, bb=bb: e.activation(scr8[:, :], lgn[bb][:, :],
                                                  AF.Exp, bias=nmx[:, 0:1],
                                                  scale=cm.c(C_ONE)),
                    w=[("vE", nt)])
                et = c.val("aE")
                c.v(lambda e: e.reduce_sum(s8[:, :], scr8[:, :], axis=AXX),
                    w=[("aE", et)])
                c.v(lambda e: e.reciprocal(wrec[:, :], s8[:, :]))
                c.v(lambda e: e.tensor_scalar(onehot[:, :], cm.iota8[:, :],
                                              idxf[:, 0:1], None,
                                              ALU.is_equal))
                c.v(lambda e: e.tensor_scalar(selw[:, :], onehot[:, :],
                                              wrec[:, 0:1], None, ALU.mult))
                swt = c.val("vE")
                sb = sub % 2
                ssm = f"gsel{sb}"
                c.t(lambda e: e.transpose(psS[:, :], selw[:, :],
                                          cm.ident_f[:, :]),
                    w=[("vE", swt)] + (sel_free[sb] or []), inc="tE")
                stt = c.val("tE")
                c.v(lambda e, sb=sb: e.tensor_copy(selst[sb][:, :], psS[:, :]),
                    w=[("tE", stt), (ssm, c.val(ssm))])
                svt = c.val("vE")
                c.dma("sync", ssm,
                      selT_dram.ap()[:, chk * 512 + sub * 128:
                                     chk * 512 + (sub + 1) * 128],
                      selst[sb][:, :], w=[("vE", svt)])
                sel_free[sb] = [(ssm, c.val(ssm))]
            lg_free[b] = [("tE", c.val("tE"))]
        c.bar()

    return logic



def phase_l0ffn_experts(nc, cm, din, src_bf, ymoe, selT_dram, E):
    MC = 512
    NMC = T // MC
    W1sb = [E(nc.sbuf_tensor(f"mo_w1_{i}", [128, 4096], BF16))
            for i in range(NDT)]
    W2sb = [E(nc.sbuf_tensor(f"mo_w2_{i}", [128, 4096], BF16))
            for i in range(8)]
    Hsb = E(nc.sbuf_tensor("mo_h", [128, 32 * MC], BF16))
    xch = [E(nc.sbuf_tensor(f"mo_x{i}", [128, NDT * MC], BF16))
           for i in range(2)]
    b1_sb = E(nc.sbuf_tensor("mo_b1", [128, 256], F32))
    b2_sb = E(nc.sbuf_tensor("mo_b2", [128, 64], F32))
    selrow = [E(nc.sbuf_tensor(f"mo_serow{i}", [1, MC], F32)) for i in range(2)]
    selwb = [E(nc.sbuf_tensor(f"mo_swb{i}", [128, MC], F32)) for i in range(2)]
    otile = [E(nc.sbuf_tensor(f"mo_ot{i}", [128, MC], F32)) for i in range(2)]
    psH = [E(nc.psum_tensor(f"mo_psH{i}", [128, MC], F32)) for i in range(2)]
    psO = [E(nc.psum_tensor(f"mo_psO{i}", [128, MC], F32)) for i in range(2)]
    psB = [E(nc.psum_tensor(f"mo_psB{i}", [128, MC], F32)) for i in range(2)]

    def logic(c):
        c.dma("sync", "cs", b1_sb[:, :], din["b1r"].ap())
        c.dma("sync", "cs", b2_sb[:, :], din["b2r"].ap())
        smt = c.val("cs")
        prev = {}
        xch_free = [None, None]
        psH_free = [None, None]
        psO_free = [None, None]
        ot_free = [None, None]
        h_prev_last = None
        o_prev_last = None
        for e_i in range(DEV.get("moe_e", 8)):
            for dt in range(NDT):
                c.dma("gpsimd", "mew", W1sb[dt][:, :],
                      din["W1"].ap()[e_i, dt * 128:(dt + 1) * 128, :],
                      w=([("tE", h_prev_last)]
                         if dt == 0 and h_prev_last is not None else []))
            w1_rdy = c.val("mew")
            for g in range(8):
                c.dma("gpsimd", "mew",
                      W2sb[g][:, :].rearrange("p (j f) -> p j f", f=1024),
                      din["W2"].ap()[e_i, g * 512:(g + 1) * 512, :]
                      .rearrange("(j p) f -> p j f", p=128),
                      w=([("tE", o_prev_last)]
                         if g == 0 and o_prev_last is not None else []))
            w_rdy = c.val("mew")
            for chk in range(DEV.get("moe_chk", NMC)):
                b = chk % 2
                xs, ss = f"mex{b}", f"mes{b}"
                c.dma("sync", xs,
                      xch[b][:, :].rearrange("p (dt c) -> p dt c", c=MC),
                      src_bf.rearrange("(dt p) t -> p dt t", p=128)[
                          :, :, chk * MC:(chk + 1) * MC],
                      w=(xch_free[b] or []))
                xt = c.val(xs)
                c.dma("sync", ss, selrow[b][:, :],
                      selT_dram.ap()[e_i:e_i + 1, chk * MC:(chk + 1) * MC],
                      w=(prev.get(("selrow", b)) or []))
                c.t(lambda e, b=b: e.matmul(psB[b][:, :], cm.ones_row_f[:, :],
                                            selrow[b][:, :], start=True,
                                            stop=True),
                    w=[(ss, c.val(ss))] + (prev.get(("psB", b)) or []),
                    inc="tE")
                psb_t = c.val("tE")
                prev[("selrow", b)] = [("tE", psb_t)]
                c.v(lambda e, b=b: e.tensor_copy(selwb[b][:, :], psB[b][:, :]),
                    w=[("tE", psb_t)] + (prev.get(("selwb", b)) or []))
                swb_t = c.val("vE")
                for jt in range(32):
                    hb = jt % 2
                    for dt in range(NDT):
                        w = []
                        if dt == 0:
                            w = list(psH_free[hb] or [])
                            if jt == 0:
                                w += [("mew", w1_rdy), (xs, xt)]
                        c.t(lambda e, jt=jt, dt=dt, hb=hb, b=b: e.matmul(
                            psH[hb][:, :],
                            W1sb[dt][:, jt * 128:(jt + 1) * 128],
                            xch[b][:, dt * MC:(dt + 1) * MC],
                            start=(dt == 0), stop=(dt == NDT - 1)), w=w,
                            inc=("tE" if dt == NDT - 1 else None))
                    ht = c.val("tE")
                    c.a(lambda e, jt=jt, hb=hb: e.activation(
                        Hsb[:, jt * MC:(jt + 1) * MC], psH[hb][:, :], AF.Silu,
                        bias=b1_sb[:, e_i * 32 + jt:e_i * 32 + jt + 1],
                        scale=cm.c(C_ONE)), w=[("tE", ht), ("cs", smt)])
                    psH_free[hb] = [("aE", c.val("aE"))]
                h_all = c.val("aE")
                h_last = c.val("tE")
                xch_free[b] = [("tE", h_last)]
                for ft in range(8):
                    ob = ft % 2
                    for jt in range(32):
                        w = []
                        if jt == 0:
                            w = [("aE", h_all)] + (psO_free[ob] or [])
                            if ft == 0:
                                w += [("mew", w_rdy)]
                        c.t(lambda e, ft=ft, jt=jt, ob=ob: e.matmul(
                            psO[ob][:, :],
                            W2sb[jt // 4][:, (jt % 4) * 1024 + ft * 128:
                                          (jt % 4) * 1024 + (ft + 1) * 128],
                            Hsb[:, jt * MC:(jt + 1) * MC],
                            start=(jt == 0), stop=(jt == 31)), w=w,
                            inc=("tE" if jt == 31 else None))
                    ot = c.val("tE")
                    c.v(lambda e, ob=ob, ft=ft, b=b: e.scalar_tensor_tensor(
                        otile[ob][:, :], psO[ob][:, :],
                        b2_sb[:, e_i * 8 + ft:e_i * 8 + ft + 1],
                        selwb[b][:, :], ALU.add, ALU.mult),
                        w=[("tE", ot), ("vE", swb_t), ("cs", smt)]
                        + (ot_free[ob] or []))
                    ovt = c.val("vE")
                    psO_free[ob] = [("vE", ovt)]
                    ys = f"mey{ft % 2}"
                    c.dma("gpsimd", ys,
                          ymoe.ap()[ft * 128:(ft + 1) * 128,
                                    chk * MC:(chk + 1) * MC],
                          otile[ob][:, :],
                          w=[("vE", ovt)],
                          accum_op=(ALU.bypass if e_i == 0 else ALU.add))
                    ot_free[ob] = [(ys, c.val(ys))]
                prev[("psB", b)] = [("vE", swb_t)]
                prev[("selwb", b)] = [("vE", c.val("vE"))]
                if chk == NMC - 1:
                    h_prev_last = h_last
            o_prev_last = c.val("tE")
        c.bar()

    return logic


def phase_l0ffn_epi(nc, cm, din, src_f32, ymoe, dst_f32, dst_bf, E):
    ep = alloc_epilogue(nc, E, 512)
    ymt = [E(nc.sbuf_tensor(f"me_ym{i}", [128, 512], F32)) for i in range(2)]
    xft = [E(nc.sbuf_tensor(f"me_xf{i}", [128, 512], F32)) for i in range(2)]

    def logic(c):
        state = {}
        buf_free = [None, None]
        for chk in range(DEV.get("epi_chk", T // 512)):
            def make_z(c, dt, ztile, chk=chk):
                b = dt % 2
                ys, xs = f"eym{b}", f"exf{b}"
                c.dma("sync", ys, ymt[b][:, :],
                      ymoe.ap()[dt * 128:(dt + 1) * 128,
                                chk * 512:(chk + 1) * 512],
                      w=(buf_free[b] or []))
                c.dma("sync", xs, xft[b][:, :],
                      src_f32[dt * 128:(dt + 1) * 128,
                              chk * 512:(chk + 1) * 512])
                zf = state.get("z_free") if dt == 0 else None
                c.v(lambda e, dt=dt, b=b: e.tensor_tensor(
                    ztile[:, :], ymt[b][:, :], xft[b][:, :], ALU.add),
                    w=[(ys, c.val(ys)), (xs, c.val(xs))] + (zf or []))
                buf_free[b] = [("vE", c.val("vE"))]
                return [("vE", c.val("vE"))]

            emit_epilogue_chunk(c, cm, ep, chk, make_z, dst_f32, dst_bf, state)
        c.bar()

    return logic



def phase_l1attn_main(nc, cm, din, src_bf, attO, E):
    CC = 128
    NCC = T // CC
    wTsb = [E(nc.sbuf_tensor(f"la_wT{i}", [128, 3 * D], BF16))
            for i in range(NDT)]
    xb = [E(nc.sbuf_tensor(f"la_xb{i}", [128, NDT * CC], BF16))
          for i in range(2)]
    qexp = E(nc.sbuf_tensor("la_qexp", [128, 8 * CC], BF16))
    kexpf = E(nc.sbuf_tensor("la_kexpf", [128, 8 * CC], F32))
    kexpb = E(nc.sbuf_tensor("la_kexpb", [128, 8 * CC], BF16))
    knat = E(nc.sbuf_tensor("la_knat", [128, 1024], BF16))
    vnat = E(nc.sbuf_tensor("la_vnat", [128, 1024], BF16))
    rsum2 = E(nc.sbuf_tensor("la_rsum2", [2, 8 * CC], F32))
    knorm = [E(nc.sbuf_tensor(f"la_kn{i}", [128, CC], F32)) for i in range(2)]
    rkn = [E(nc.sbuf_tensor(f"la_rkn{i}", [128, CC], F32)) for i in range(2)]
    qs1 = [E(nc.sbuf_tensor(f"la_qs1_{i}", [128, CC], F32)) for i in range(2)]
    qsT = E(nc.sbuf_tensor("la_qsT", [128, 8 * CC], BF16))
    carry = E(nc.sbuf_tensor("la_carry", [128, 8], F32))
    Sf = E(nc.sbuf_tensor("la_Sf", [128, 8 * 64], F32))
    Sb = E(nc.sbuf_tensor("la_Sb", [128, 8 * 64], BF16))
    Abf = [E(nc.sbuf_tensor(f"la_A{i}", [128, CC], BF16)) for i in range(2)]
    yT = [E(nc.sbuf_tensor(f"la_yT{i}", [128, 8 * 512], BF16))
          for i in range(2)]
    psP = [E(nc.psum_tensor(f"la_psP{i}", [128, 512], F32)) for i in range(2)]
    psQs = E(nc.psum_tensor("la_psQs", [2, CC], F32))
    psRB = E(nc.psum_tensor("la_psRB", [128, CC], F32))
    psA = E(nc.psum_tensor("la_psA", [128, CC], F32))
    psY = E(nc.psum_tensor("la_psY", [64, CC], F32))
    psSU = E(nc.psum_tensor("la_psSU", [64, 64], F32))

    def logic(c):
        for dt in range(NDT):
            c.dma("gpsimd", "law", wTsb[dt][:, :],
                  din["wT"].ap()[dt * 128:(dt + 1) * 128, :])
        wt = c.val("law")
        c.v(lambda e: e.memset(Sf[:, :], 0.0))
        c.v(lambda e: e.memset(Sb[:, :], 0.0))
        c.v(lambda e: e.memset(carry[:, :], 0.0))
        init_t = c.val("vE")
        st = {"xb": [None, None], "psP": [None, None],
              "psQs": None, "psRB": None, "kn": [None, None],
              "qs1": [None, None], "A": [None, None], "psA": None,
              "psY": None, "psSU": None, "qexp": None, "kexpf": None,
              "kexpb": None, "knat": None, "vnat": None, "qsT": None,
              "Sv": [init_t] * 16, "Sbv": [init_t] * 16, "yT": [None, None]}
        for cc in range(DEV.get("l1a_cc", NCC)):
            b = cc % 2
            yb = (cc // 4) % 2
            xs = f"lax{b}"
            c.dma("sync", xs,
                  xb[b][:, :].rearrange("p (dt c) -> p dt c", c=CC),
                  src_bf.rearrange("(dt p) t -> p dt t", p=128)[
                      :, :, cc * CC:(cc + 1) * CC],
                  w=(st["xb"][b] or []))
            xt = c.val(xs)
            for which in ("q", "k"):
                for et in range(NDT):
                    pb = et % 2
                    ecol = (0 if which == "q" else D) + et * 128
                    for dt in range(NDT):
                        w = []
                        if dt == 0:
                            w = [(xs, xt), ("law", wt)] + (st["psP"][pb] or [])
                        c.t(lambda e, dt=dt, pb=pb, ecol=ecol, b=b:
                            e.matmul(psP[pb][:, 0:CC],
                                     wTsb[dt][:, ecol:ecol + 128],
                                     xb[b][:, dt * CC:(dt + 1) * CC],
                                     start=(dt == 0), stop=(dt == NDT - 1)),
                            w=w, inc=("tE" if dt == NDT - 1 else None))
                    mt = c.val("tE")
                    dst = qexp if which == "q" else kexpf
                    wx = (st["qexp"] if which == "q" else st["kexpf"]) or []
                    c.a(lambda e, et=et, pb=pb, dst=dst: e.activation(
                        dst[:, et * CC:(et + 1) * CC], psP[pb][:, 0:CC],
                        AF.Exp, bias=cm.c(C_ZERO), scale=cm.c(C_QS)),
                        w=[("tE", mt)] + (wx if et < 2 else []))
                    at = c.val("aE")
                    st["psP"][pb] = [("aE", at)]
                    if which == "k":
                        c.v(lambda e, et=et: e.tensor_scalar(
                            kexpf[:, et * CC:(et + 1) * CC],
                            kexpf[:, et * CC:(et + 1) * CC], 1e-6, None,
                            ALU.add),
                            w=[("aE", at)] + ((st["kexpb"] or [])
                                              if et < 2 else []))
                        c.v(lambda e, et=et: e.tensor_copy(
                            kexpb[:, et * CC:(et + 1) * CC],
                            kexpf[:, et * CC:(et + 1) * CC]))
            kexp_done = c.val("vE")
            for which, col0 in (("k", D), ("v", 2 * D)):
                for ech in range(2):
                    pb = ech % 2
                    for dt in range(NDT):
                        w = []
                        if dt == 0:
                            w = [(xs, xt)] + (st["psP"][pb] or [])
                        c.t(lambda e, ech=ech, dt=dt, pb=pb, col0=col0, b=b:
                            e.matmul(psP[pb][:, :],
                                     xb[b][:, dt * CC:(dt + 1) * CC],
                                     wTsb[dt][:, col0 + ech * 512:
                                              col0 + (ech + 1) * 512],
                                     start=(dt == 0), stop=(dt == NDT - 1)),
                            w=w, inc=("tE" if dt == NDT - 1 else None))
                    mt = c.val("tE")
                    if which == "k":
                        c.a(lambda e, ech=ech, pb=pb: e.activation(
                            knat[:, ech * 512:(ech + 1) * 512], psP[pb][:, :],
                            AF.Exp, bias=cm.c(C_ZERO), scale=cm.c(C_QS)),
                            w=[("tE", mt)] + ((st["knat"] or [])
                                              if ech == 0 else []))
                    else:
                        c.a(lambda e, ech=ech, pb=pb: e.activation(
                            vnat[:, ech * 512:(ech + 1) * 512], psP[pb][:, :],
                            AF.Identity, bias=cm.c(C_ZERO),
                            scale=cm.c(C_ONE)),
                            w=[("tE", mt)] + ((st["vnat"] or [])
                                              if ech == 0 else []))
                    st["psP"][pb] = [("aE", c.val("aE"))]
            knv_done = c.val("aE")
            for et in range(NDT):
                kb = et % 2
                c.t(lambda e, et=et: e.matmul(psQs[:, :], cm.ones2[:, :],
                                              qexp[:, et * CC:(et + 1) * CC],
                                              start=True, stop=True),
                    w=[("aE", knv_done)] + (st["psQs"] or []), inc="tE")
                qst = c.val("tE")
                c.v(lambda e, et=et: e.reciprocal(
                    rsum2[:, et * CC:(et + 1) * CC], psQs[:, :]),
                    w=[("tE", qst)])
                st["psQs"] = [("vE", c.val("vE"))]
                rst = c.val("vE")
                c.t(lambda e, et=et: e.matmul(psRB[:, :], cm.sel2T[:, :],
                                              rsum2[:, et * CC:(et + 1) * CC],
                                              start=True, stop=True),
                    w=[("vE", rst)] + (st["psRB"] or []), inc="tE")
                rbt = c.val("tE")
                c.v(lambda e, et=et, kb=kb: e.tensor_tensor_scan(
                    knorm[kb][:, :], kexpf[:, et * CC:(et + 1) * CC],
                    cm.ones_t128[:, :], carry[:, et:et + 1], ALU.add,
                    ALU.mult),
                    w=[("vE", kexp_done)] + (st["kn"][kb] or []))
                c.v(lambda e, et=et, kb=kb: e.tensor_copy(
                    carry[:, et:et + 1], knorm[kb][:, CC - 1:CC]))
                c.v(lambda e, kb=kb: e.reciprocal(rkn[kb][:, :],
                                                  knorm[kb][:, :]))
                c.v(lambda e, et=et, kb=kb: e.tensor_tensor(
                    qs1[kb][:, :], qexp[:, et * CC:(et + 1) * CC],
                    psRB[:, :], ALU.mult),
                    w=[("tE", rbt)] + (st["qs1"][kb] or []))
                st["psRB"] = [("vE", c.val("vE"))]
                c.v(lambda e, et=et, kb=kb: e.tensor_tensor(
                    qsT[:, et * CC:(et + 1) * CC], qs1[kb][:, :],
                    rkn[kb][:, :], ALU.mult),
                    w=(st["qsT"] or []) if et == 0 else [])
                st["kn"][kb] = [("vE", c.val("vE"))]
                st["qs1"][kb] = st["kn"][kb]
            qs_done = c.val("vE")
            for h in range(16):
                et, hh = h // 2, h % 2
                ab = h % 2
                lo = hh * 64
                c.t(lambda e, et=et, lo=lo: e.matmul(
                    psA[:, :], kexpb[lo:lo + 64, et * CC:(et + 1) * CC],
                    qsT[lo:lo + 64, et * CC:(et + 1) * CC],
                    start=True, stop=True),
                    w=[("vE", qs_done)] + (st["psA"] or []), inc="tE")
                at_ = c.val("tE")
                c.v(lambda e, ab=ab: e.tensor_tensor(Abf[ab][:, :], psA[:, :],
                                                     cm.tril_b[:, :],
                                                     ALU.mult),
                    w=[("tE", at_)] + (st["A"][ab] or []))
                st["psA"] = [("vE", c.val("vE"))]
                amt = c.val("vE")
                c.t(lambda e, h=h, ab=ab: e.matmul(
                    psY[:, :], vnat[:, h * 64:(h + 1) * 64], Abf[ab][:, :],
                    start=True, stop=False),
                    w=[("vE", amt), ("aE", knv_done)] + (st["psY"] or []))
                c.t(lambda e, h=h, lo=lo: e.matmul(
                    psY[:, :],
                    Sb[lo:lo + 64, (h // 2) * 64:(h // 2) * 64 + 64],
                    qsT[lo:lo + 64, (h // 2) * CC:(h // 2) * CC + CC],
                    start=False, stop=True),
                    w=[("vE", st["Sbv"][h])], inc="tE")
                yt_ = c.val("tE")
                st["A"][ab] = [("tE", yt_)]
                c.a(lambda e, h=h, hh=hh, cc=cc, yb=yb: e.activation(
                    yT[yb][hh * 64:hh * 64 + 64,
                           (h // 2) * 512 + (cc % 4) * CC:
                           (h // 2) * 512 + (cc % 4) * CC + CC],
                    psY[:, :], AF.Identity,
                    bias=cm.consts[0:64, C_ZERO:C_ZERO + 1],
                    scale=cm.consts[0:64, C_ONE:C_ONE + 1]),
                    w=[("tE", yt_)] + (st["yT"][yb] or []))
                st["psY"] = [("aE", c.val("aE"))]
                c.t(lambda e, h=h: e.matmul(
                    psSU[:, :], knat[:, h * 64:(h + 1) * 64],
                    vnat[:, h * 64:(h + 1) * 64], start=True, stop=True),
                    w=(st["psSU"] or []) + [("aE", knv_done)], inc="tE")
                sut = c.val("tE")
                c.v(lambda e, h=h, lo=lo: e.tensor_tensor(
                    Sf[lo:lo + 64, (h // 2) * 64:(h // 2) * 64 + 64],
                    Sf[lo:lo + 64, (h // 2) * 64:(h // 2) * 64 + 64],
                    psSU[:, :], ALU.add),
                    w=[("tE", sut), ("vE", st["Sv"][h])])
                st["Sv"][h] = c.val("vE")
                c.v(lambda e, h=h, lo=lo: e.tensor_copy(
                    Sb[lo:lo + 64, (h // 2) * 64:(h // 2) * 64 + 64],
                    Sf[lo:lo + 64, (h // 2) * 64:(h // 2) * 64 + 64]))
                st["Sbv"][h] = c.val("vE")
                st["psSU"] = [("vE", st["Sv"][h])]
            end_t = c.val("tE")
            st["xb"][b] = [("tE", end_t)]
            st["qexp"] = [("tE", end_t), ("vE", qs_done)]
            st["kexpf"] = [("vE", qs_done)]
            st["kexpb"] = [("tE", end_t)]
            st["knat"] = [("tE", end_t)]
            st["vnat"] = [("tE", end_t)]
            st["qsT"] = [("tE", end_t)]
            if cc % 4 == 3:
                ys = f"lay{yb}"
                ya = c.val("aE")
                c.dma("sync", ys,
                      attO.rearrange("(et p) t -> p et t", p=128)[
                          :, :, (cc // 4) * 512:(cc // 4) * 512 + 512],
                      yT[yb][:, :].rearrange("p (et t) -> p et t", t=512),
                      w=[("aE", ya)])
                st["yT"][yb] = [(ys, c.val(ys))]
        c.bar()

    return logic




MPITCH = 4224
MSLAB = 17340672


def phase_l0attn(nc, cm, din, src_f32, src_bf, mscr, attO, E):
    RT = E(nc.sbuf_tensor("at_rt", [128, 8192], BF16))
    QT = E(nc.sbuf_tensor("at_qt", [128, T], BF16))
    KT = E(nc.sbuf_tensor("at_kt", [128, T], BF16))
    Vx = E(nc.sbuf_tensor("at_vx", [128, 64 * 66], BF16))
    wq = [E(nc.sbuf_tensor(f"at_wq{i}", [128, 8 * 128], BF16)) for i in range(2)]
    wk = [E(nc.sbuf_tensor(f"at_wk{i}", [128, 8 * 128], BF16)) for i in range(2)]
    wv = [E(nc.sbuf_tensor(f"at_wv{i}", [128, 8 * 128], BF16)) for i in range(2)]
    xc = [E(nc.sbuf_tensor(f"at_xc{i}", [128, NDT * 512], BF16)) for i in range(2)]
    bq = E(nc.sbuf_tensor("at_bq", [1, D], F32))
    bk = E(nc.sbuf_tensor("at_bk", [1, D], F32))
    bv = E(nc.sbuf_tensor("at_bv", [1, D], F32))
    Msb = [E(nc.sbuf_tensor(f"at_m{i}", [128, MPITCH], BF16)) for i in range(2)]
    bias = [E(nc.sbuf_tensor(f"at_bi{i}", [128, 512], BF16)) for i in range(3)]
    expS = [E(nc.sbuf_tensor(f"at_ex{i}", [128, 512], BF16)) for i in range(3)]
    rsum = E(nc.sbuf_tensor("at_rs", [1, 512], F32))
    rb = E(nc.sbuf_tensor("at_rb", [64, 512], F32))
    ao = [E(nc.sbuf_tensor(f"at_ao{i}", [64, 512], BF16)) for i in range(2)]
    psA = [E(nc.psum_tensor(f"at_psA{i}", [128, 512], F32)) for i in range(2)]
    psV = [E(nc.psum_tensor(f"at_psV{i}", [128, 128], F32)) for i in range(2)]
    psS = [E(nc.psum_tensor(f"at_psS{i}", [128, 512], F32)) for i in range(2)]
    psO = E(nc.psum_tensor("at_psO", [65, 512], F32))
    psR = E(nc.psum_tensor("at_psR", [64, 512], F32))

    def logic(c):
        c.dma("gpsimd", "arl", RT[0:64, 0:8191], din["RT"].ap())
        c.dma("gpsimd", "arl", RT[64:128, 0:8191], din["RT"].ap())
        c.dma("sync", "abl", bq[:, :], din["bqkvo"].ap()[0:1, :])
        c.dma("sync", "abl", bk[:, :], din["bqkvo"].ap()[1:2, :])
        c.dma("sync", "abl", bv[:, :], din["bqkvo"].ap()[2:3, :])
        rt_t = c.val("arl")
        bl_t = c.val("abl")
        c.v(lambda e: e.memset(RT[:, 8191:8192], 0.0))
        c.v(lambda e: e.memset(
            Vx[:, :].rearrange("p (g cc) -> p g cc", cc=66)[:, :, 64:65], 1.0))
        c.v(lambda e: e.memset(
            Vx[:, :].rearrange("p (g cc) -> p g cc", cc=66)[:, :, 65:66], 0.0))
        ones_t = c.val("vE")
        st = {"w": [None, None], "xc": [None, None], "psA": [None, None],
              "psV": [None, None], "psS": [None, None], "QT": None,
              "KT": None, "Vx": None, "Msb": [None, None],
              "bias": [None, None, None], "expS": [None, None, None],
              "psO": None, "psR": None, "rb": None, "ao": [None, None],
              "slab_free": [None, None, None, None]}
        for et in range(DEV.get("att_et", 8)):
            wb = et % 2
            for wbuf, wT in ((wq[wb], "WqT"), (wk[wb], "WkT"), (wv[wb], "WvT")):
                c.dma("gpsimd", f"aw{wb}",
                      wbuf[:, :].rearrange("p (dt e) -> p dt e", e=128),
                      din[wT].ap().rearrange("(dt p) e -> p dt e", p=128)[
                          :, :, et * 128:(et + 1) * 128],
                      w=(st["w"][wb] or []))
            w_t = c.val(f"aw{wb}")
            for tc in range(8):
                xb = tc % 2
                c.dma("sync", f"ax{xb}",
                      xc[xb][:, :].rearrange("p (dt cc) -> p dt cc", cc=512),
                      src_bf.rearrange("(dt p) t -> p dt t", p=128)[
                          :, :, tc * 512:(tc + 1) * 512],
                      w=(st["xc"][xb] or []))
                x_t = c.val(f"ax{xb}")
                for which, wbuf, brow, dst, scl in (
                        ("q", wq[wb], bq, QT, C_ONE), ("k", wk[wb], bk, KT, C_QS)):
                    pa = (tc * 2 + (0 if which == "q" else 1)) % 2
                    for dt in range(NDT):
                        w = []
                        if dt == 0:
                            w = [(f"ax{xb}", x_t), (f"aw{wb}", w_t),
                                 ("abl", bl_t)] + (st["psA"][pa] or [])
                        c.t(lambda e, dt=dt, pa=pa, xb=xb, wbuf=wbuf: e.matmul(
                            psA[pa][:, :], wbuf[:, dt * 128:(dt + 1) * 128],
                            xc[xb][:, dt * 512:(dt + 1) * 512],
                            start=(dt == 0), stop=False), w=w)
                    c.t(lambda e, pa=pa, brow=brow, et=et: e.matmul(
                        psA[pa][:, :], brow[:, et * 128:(et + 1) * 128],
                        cm.ones_row_f512[:, :], start=False, stop=True),
                        inc="tE")
                    mt = c.val("tE")
                    c.a(lambda e, pa=pa, tc=tc, dst=dst, scl=scl: e.activation(
                        dst[:, tc * 512:(tc + 1) * 512], psA[pa][:, :],
                        AF.Identity, bias=cm.c(C_ZERO), scale=cm.c(scl)),
                        w=[("tE", mt)] + ((st["QT"] or []) if tc == 0 and
                                          which == "q" else []))
                    st["psA"][pa] = [("aE", c.val("aE"))]
                for ttl in range(4):
                    pv = ttl % 2
                    for dt in range(NDT):
                        w = []
                        if dt == 0:
                            w = [(f"ax{xb}", x_t)] + (st["psV"][pv] or [])
                            if tc == 0 and ttl == 0:
                                w += (st["Vx"] or []) + [("vE", ones_t)]
                        c.t(lambda e, dt=dt, pv=pv, xb=xb, ttl=ttl, wb=wb:
                            e.matmul(psV[pv][:, :],
                                     xc[xb][:, dt * 512 + ttl * 128:
                                            dt * 512 + (ttl + 1) * 128],
                                     wv[wb][:, dt * 128:(dt + 1) * 128],
                                     start=(dt == 0), stop=False), w=w)
                    c.t(lambda e, pv=pv, et=et: e.matmul(
                        psV[pv][:, :], cm.ones_row_f[:, :],
                        bv[:, et * 128:(et + 1) * 128], start=False,
                        stop=True), inc="tE")
                    vt = c.val("tE")
                    jt = tc * 4 + ttl
                    for hh in range(2):
                        c.a(lambda e, pv=pv, jt=jt, hh=hh: e.activation(
                            Vx[:, (jt * 2 + hh) * 66:(jt * 2 + hh) * 66 + 64],
                            psV[pv][:, hh * 64:(hh + 1) * 64], AF.Identity,
                            bias=cm.consts[0:128, C_ZERO:C_ZERO + 1],
                            scale=cm.consts[0:128, C_ONE:C_ONE + 1]),
                            w=[("tE", vt)])
                    st["psV"][pv] = [("aE", c.val("aE"))]
                st["xc"][xb] = [("tE", c.val("tE"))]
            proj_t = c.val("aE")
            st["w"][wb] = [("tE", c.val("tE"))]
            for hh in range(2):
                slab = (et % 2) * 2 + hh
                ms = f"ams{slab}"
                lo = hh * 64
                for it in range(32):
                    mb = it % 2
                    for kc in range(9):
                        pa = kc % 2
                        ncol = 512 if kc < 8 else 128
                        cbase = 3968 - it * 128 + kc * 512
                        w = [("aE", proj_t), ("arl", rt_t)]
                        if kc == 0:
                            w += (st["Msb"][mb] or [])
                        w += (st["psA"][pa] or [])
                        c.t(lambda e, it=it, kc=kc, pa=pa, lo=lo, cbase=cbase,
                            ncol=ncol: e.matmul(
                                psA[pa][:, 0:ncol],
                                QT[lo:lo + 64, it * 128:(it + 1) * 128],
                                RT[lo:lo + 64, cbase:cbase + ncol],
                                start=True, stop=True), w=w, inc="tE")
                        mt = c.val("tE")
                        c.v(lambda e, kc=kc, pa=pa, mb=mb, ncol=ncol:
                            e.tensor_copy(Msb[mb][:, kc * 512:kc * 512 + ncol],
                                          psA[pa][:, 0:ncol]),
                            w=[("tE", mt)])
                        st["psA"][pa] = [("vE", c.val("vE"))]
                    mv = c.val("vE")
                    c.dma("sync", ms,
                          AP(mscr, slab * MSLAB + it * 128 * (MPITCH - 1),
                             [[MPITCH, 128], [1, MPITCH]]),
                          Msb[mb][:, :],
                          w=[("vE", mv)] + (st["slab_free"][slab] or []))
                    st["Msb"][mb] = [(ms, c.val(ms))]
                mstore_t = c.val(ms)
                sk = f"ask{slab}"
                for ic in range(8):
                    for jt in range(32):
                        bb = jt % 3
                        base = (slab * MSLAB + ic * 512 * (MPITCH - 1)
                                + jt * 128 + 127)
                        c.dma("sync", sk, bias[bb][:, :],
                              AP(mscr, base, [[MPITCH - 1, 512], [1, 128]]),
                              w=[(ms, mstore_t)] + (st["bias"][bb] or []),
                              transpose=True)
                        b_t = c.val(sk)
                        ps = jt % 2
                        w = [(sk, b_t)] + (st["psS"][ps] or [])
                        c.t(lambda e, bb=bb, ps=ps: e.matmul(
                            psS[ps][:, :], cm.ident_b[:, :], bias[bb][:, :],
                            start=True, stop=False), w=w)
                        c.t(lambda e, jt=jt, ic=ic, ps=ps, lo=lo: e.matmul(
                            psS[ps][:, :],
                            KT[lo:lo + 64, jt * 128:(jt + 1) * 128],
                            QT[lo:lo + 64, ic * 512:(ic + 1) * 512],
                            start=False, stop=True), inc="tE")
                        s_t = c.val("tE")
                        st["bias"][bb] = [("tE", s_t)]
                        eb = jt % 3
                        c.a(lambda e, ps=ps, eb=eb: e.activation(
                            expS[eb][:, :], psS[ps][:, :], AF.Exp,
                            bias=cm.c(C_EXPB), scale=cm.c(C_ONE)),
                            w=[("tE", s_t)] + (st["expS"][eb] or []))
                        e_t = c.val("aE")
                        st["psS"][ps] = [("aE", e_t)]
                        w = [("aE", e_t)]
                        if jt == 0:
                            w += (st["psO"] or [])
                        c.t(lambda e, jt=jt, eb=eb, hh=hh: e.matmul(
                            psO[:, :],
                            Vx[:, (jt * 2 + hh) * 66:(jt * 2 + hh) * 66 + 65],
                            expS[eb][:, :], start=(jt == 0), stop=(jt == 31)),
                            w=w, inc="tE")
                        st["expS"][eb] = [("tE", c.val("tE"))]
                    o_t = c.val("tE")
                    c.v(lambda e: e.reciprocal(rsum[:, :], psO[64:65, :]),
                        w=[("tE", o_t)] + (st["rb"] or []))
                    r_t = c.val("vE")
                    c.t(lambda e: e.matmul(psR[:, :], cm.ones_row_f[0:1, 0:64],
                                           rsum[:, :], start=True, stop=True),
                        w=[("vE", r_t)] + (st["psR"] or []), inc="tE")
                    rb_t = c.val("tE")
                    c.a(lambda e: e.activation(rb[:, :], psR[:, :],
                                               AF.Identity,
                                               bias=cm.consts[0:64,
                                                              C_ZERO:C_ZERO + 1],
                                               scale=cm.consts[0:64,
                                                               C_ONE:C_ONE + 1]),
                        w=[("tE", rb_t)])
                    rba_t = c.val("aE")
                    st["psR"] = [("aE", rba_t)]
                    ab = ic % 2
                    c.v(lambda e, ab=ab: e.tensor_tensor(
                        ao[ab][:, :], psO[0:64, :], rb[:, :], ALU.mult),
                        w=[("aE", rba_t)] + (st["ao"][ab] or []))
                    ao_t = c.val("vE")
                    st["psO"] = [("vE", ao_t)]
                    st["rb"] = [("vE", ao_t)]
                    c.dma("sync", f"aao{ab}",
                          attO[et * 128 + lo:et * 128 + lo + 64,
                               ic * 512:(ic + 1) * 512],
                          ao[ab][:, :], w=[("vE", ao_t)])
                    st["ao"][ab] = [(f"aao{ab}", c.val(f"aao{ab}"))]
                st["slab_free"][slab] = [(sk, c.val(sk))]
            st["QT"] = [("tE", c.val("tE"))]
            st["KT"] = st["QT"]
            st["Vx"] = st["QT"]
        c.bar()

    return logic



def phase_l1ffn_peer(nc, cm, din, src_f32, src_bf, tdu, expdT, E):
    xbt = E(nc.sbuf_tensor("pr_xbt", [128, NDT * 128], BF16))
    xft = [E(nc.sbuf_tensor(f"pr_xf{i}", [128, 128], F32)) for i in range(2)]
    y3n = E(nc.sbuf_tensor("pr_y3n", [128, 1024], BF16))
    Wq2 = E(nc.sbuf_tensor("pr_wq2", [128, NDT * 512], F32))
    keys = E(nc.sbuf_tensor("pr_keys", [128, 512], F32))
    qT = E(nc.sbuf_tensor("pr_qT", [128, 4 * 128], F32))
    sim = E(nc.sbuf_tensor("pr_sim", [128, 8 * 64], F32))
    sx = E(nc.sbuf_tensor("pr_sx", [128, 64], F32))
    ixf = E(nc.sbuf_tensor("pr_ixf", [128, 64], F32))
    idxu = E(nc.sbuf_tensor("pr_idxu", [128, 8], U32))
    allsc = E(nc.sbuf_tensor("pr_alls", [128, 64], F32))
    exy = E(nc.sbuf_tensor("pr_exy", [128, 64], F32))
    ix64 = E(nc.sbuf_tensor("pr_ix64", [128, 8], F32))
    s8f = E(nc.sbuf_tensor("pr_s8f", [128, 8], F32))
    pku = E(nc.sbuf_tensor("pr_pku", [128, 8], U32))
    pkf = E(nc.sbuf_tensor("pr_pkf", [128, 8], F32))
    oh64 = E(nc.sbuf_tensor("pr_oh64", [128, 64], F32))
    junk64 = E(nc.sbuf_tensor("pr_junk64", [128, 64], F32))
    ef8 = E(nc.sbuf_tensor("pr_ef8", [128, 8], F32))
    efi = E(nc.sbuf_tensor("pr_efi", [128, 8], I32))
    es8 = E(nc.sbuf_tensor("pr_es8", [128, 8], F32))
    p8 = E(nc.sbuf_tensor("pr_p8", [128, 8], F32))
    ssum = E(nc.sbuf_tensor("pr_ssum", [128, 1], F32))
    rs = E(nc.sbuf_tensor("pr_rs", [128, 1], F32))
    sd8 = E(nc.sbuf_tensor("pr_sd8", [128, 8], F32))
    w8 = E(nc.sbuf_tensor("pr_w8", [128, 8], F32))
    gjunk = E(nc.sbuf_tensor("pr_gjunk", [128, 1024], BF16))
    G = [E(nc.sbuf_tensor(f"pr_G{i}", [128, 2048], BF16)) for i in range(3)]
    expacc = E(nc.sbuf_tensor("pr_eacc", [128, 1024], F32))
    etile = [E(nc.sbuf_tensor(f"pr_et{i}", [128, 128], F32)) for i in range(2)]
    psB2 = [E(nc.psum_tensor(f"pr_psb{i}", [128, 128], BF16)) for i in range(2)]
    psF2 = [E(nc.psum_tensor(f"pr_psf{i}", [128, 128], F32)) for i in range(2)]

    def logic(c):
        c.dma("gpsimd", "prw",
              Wq2[:, :].rearrange("p (dt e) -> p dt e", e=512),
              din["Wq2T"].ap().rearrange("(dt p) e -> p dt e", p=128))
        c.dma("sync", "prk", keys[0:64, :].rearrange("p (g k) -> p g k", k=64),
              din["keysT"].ap().rearrange("two h n k -> n (two h) k"))
        c.dma("sync", "prk", keys[64:128, :].rearrange("p (g k) -> p g k", k=64),
              din["keysT"].ap().rearrange("two h n k -> n (two h) k"))
        wq_t = c.val("prw")
        k_t = c.val("prk")
        st = {"xbt": None, "xf": [None, None], "psB": [None, None],
              "psF": [None, None], "y3n": None, "qT": None, "sim": None,
              "G": [None, None, None], "et": [None, None], "eacc": None}
        for tt in range(DEV.get("peer_tt", 32)):
            t0 = tt * 128
            c.dma("sync", "prx",
                  xbt[:, :].rearrange("p (dt cc) -> p dt cc", cc=128),
                  src_bf.rearrange("(dt p) t -> p dt t", p=128)[
                      :, :, t0:t0 + 128], w=(st["xbt"] or []))
            xbt_t = c.val("prx")
            for dt in range(NDT):
                pb = dt % 2
                c.t(lambda e, dt=dt, pb=pb: e.transpose(
                    psB2[pb][:, :], xbt[:, dt * 128:(dt + 1) * 128],
                    cm.ident_b[:, :]),
                    w=[("prx", xbt_t)] + (st["psB"][pb] or []), inc="tE")
                trt = c.val("tE")
                c.a(lambda e, dt=dt, pb=pb: e.activation(
                    y3n[:, dt * 128:(dt + 1) * 128], psB2[pb][:, :],
                    AF.Identity, bias=cm.c(C_ZERO), scale=cm.c(C_ONE)),
                    w=[("tE", trt)] + ((st["y3n"] or []) if dt < 2 else []))
                st["psB"][pb] = [("aE", c.val("aE"))]
            y3_t = c.val("aE")
            for et in range(4):
                pb = et % 2
                for dt in range(NDT):
                    b = dt % 2
                    c.dma("sync", f"prf{b}", xft[b][:, :],
                          src_f32[dt * 128:(dt + 1) * 128, t0:t0 + 128],
                          w=(st["xf"][b] or []))
                    w = [(f"prf{b}", c.val(f"prf{b}")), ("prw", wq_t)]
                    if dt == 0:
                        w += (st["psF"][pb] or [])
                    c.t(lambda e, et=et, dt=dt, pb=pb, b=b: e.matmul(
                        psF2[pb][:, :],
                        Wq2[:, dt * 512 + et * 128:dt * 512 + (et + 1) * 128],
                        xft[b][:, :], start=(dt == 0), stop=(dt == NDT - 1)),
                        w=w, inc=("tE" if dt == NDT - 1 else None))
                    st["xf"][b] = [("tE", c.val("tE"))]
                qt_ = c.val("tE")
                c.v(lambda e, et=et, pb=pb: e.tensor_copy(
                    qT[:, et * 128:(et + 1) * 128], psF2[pb][:, :]),
                    w=[("tE", qt_)] + ((st["qT"] or []) if et < 2 else []))
                st["psF"][pb] = [("vE", c.val("vE"))]
            qT_t = c.val("vE")
            for ph in range(8):
                pb = ph % 2
                lo = (ph % 2) * 64
                etq = ph // 2
                c.t(lambda e, ph=ph, pb=pb, lo=lo, etq=etq: e.matmul(
                    psF2[pb][:, 0:64],
                    qT[lo:lo + 64, etq * 128:(etq + 1) * 128],
                    keys[lo:lo + 64, ph * 64:(ph + 1) * 64],
                    start=True, stop=True),
                    w=[("vE", qT_t), ("prk", k_t)] + (st["psF"][pb] or []),
                    inc="tE")
                smt = c.val("tE")
                c.v(lambda e, ph=ph, pb=pb: e.tensor_copy(
                    sim[:, ph * 64:(ph + 1) * 64], psF2[pb][:, 0:64]),
                    w=[("tE", smt)] + ((st["sim"] or []) if ph < 2 else []))
                st["psF"][pb] = [("vE", c.val("vE"))]
                c.v(lambda e, ph=ph: e.max(sx[:, ph * 8:(ph + 1) * 8],
                                           sim[:, ph * 64:(ph + 1) * 64]))
                c.v(lambda e, ph=ph: e.max_index(idxu[:, :],
                                                 sx[:, ph * 8:(ph + 1) * 8],
                                                 sim[:, ph * 64:(ph + 1) * 64]))
                c.v(lambda e, ph=ph: e.tensor_copy(ixf[:, ph * 8:(ph + 1) * 8],
                                                   idxu[:, :]))
            st["sim"] = None
            st["qT"] = None
            eacc_w = (st["eacc"] or [])
            c.v(lambda e: e.memset(expacc[:, :], 0.0), w=eacc_w)
            for h in range(4):
                for a in range(8):
                    c.v(lambda e, h=h, a=a: e.tensor_scalar(
                        allsc[:, a * 8:(a + 1) * 8],
                        sx[:, (4 + h) * 8:(5 + h) * 8],
                        sx[:, h * 8 + a:h * 8 + a + 1], None, ALU.add))
                c.v(lambda e, h=h: e.tensor_scalar(
                    ix64[:, :], ixf[:, h * 8:(h + 1) * 8], 64.0, None,
                    ALU.mult))
                for a in range(8):
                    c.v(lambda e, h=h, a=a: e.tensor_scalar(
                        exy[:, a * 8:(a + 1) * 8],
                        ixf[:, (4 + h) * 8:(5 + h) * 8],
                        ix64[:, a:a + 1], None, ALU.add))
                c.v(lambda e: e.max(s8f[:, :], allsc[:, :]))
                c.v(lambda e: e.max_index(pku[:, :], s8f[:, :], allsc[:, :]))
                c.v(lambda e: e.tensor_copy(pkf[:, :], pku[:, :]))
                c.v(lambda e: e.tensor_scalar(es8[:, :], s8f[:, :],
                                              s8f[:, 0:1], None, ALU.subtract))
                ev = c.val("vE")
                c.a(lambda e: e.activation(es8[:, :], es8[:, :], AF.Exp,
                                           bias=cm.c(C_ZERO),
                                           scale=cm.c(C_ONE)), w=[("vE", ev)])
                ea = c.val("aE")
                c.v(lambda e: e.reduce_sum(ssum[:, :], es8[:, :], axis=AXX),
                    w=[("aE", ea)])
                c.v(lambda e: e.reciprocal(rs[:, :], ssum[:, :]))
                c.v(lambda e: e.tensor_scalar(p8[:, :], es8[:, :], rs[:, 0:1],
                                              None, ALU.mult))
                for k in range(8):
                    c.v(lambda e, k=k: e.tensor_scalar(
                        oh64[:, :], cm.iota64[:, :], pkf[:, k:k + 1], None,
                        ALU.is_equal))
                    c.v(lambda e, k=k: e.scalar_tensor_tensor(
                        junk64[:, :], oh64[:, :], 1.0, exy[:, :],
                        ALU.mult, ALU.mult, accum_out=ef8[:, k:k + 1]))
                c.v(lambda e: e.tensor_copy(efi[:, :], ef8[:, :]))
                ef_t = c.val("vE")
                for s in range(8):
                    gb = s % 3
                    c.idma("prg", G[gb][:, :], tdu.ap(),
                           IndirectOffsetOnAxis(efi[:, s:s + 1], 0),
                           w=[("vE", ef_t)] + (st["G"][gb] or []))
                    g_t = c.val("prg")
                    c.v(lambda e, s=s, gb=gb: e.scalar_tensor_tensor(
                        gjunk[:, :], G[gb][:, 0:1024], 1.0, y3n[:, :],
                        ALU.mult, ALU.mult, accum_out=sd8[:, s:s + 1]),
                        w=[("prg", g_t), ("aE", y3_t)])
                    c.v(lambda e, s=s: e.tensor_tensor(
                        w8[:, s:s + 1], sd8[:, s:s + 1], p8[:, s:s + 1],
                        ALU.mult))
                    wv = c.val("vE")
                    c.a(lambda e, s=s: e.activation(w8[:, s:s + 1],
                                                    w8[:, s:s + 1], AF.Silu,
                                                    bias=cm.c(C_ZERO),
                                                    scale=cm.c(C_ONE)),
                        w=[("vE", wv)])
                    wa = c.val("aE")
                    c.v(lambda e, s=s, gb=gb: e.scalar_tensor_tensor(
                        expacc[:, :], G[gb][:, 1024:2048], w8[:, s:s + 1],
                        expacc[:, :], ALU.mult, ALU.add), w=[("aE", wa)])
                    st["G"][gb] = [("vE", c.val("vE"))]
            eacc_t = c.val("vE")
            for dt in range(NDT):
                pb = dt % 2
                eb = dt % 2
                c.t(lambda e, dt=dt, pb=pb: e.transpose(
                    psF2[pb][:, :], expacc[:, dt * 128:(dt + 1) * 128],
                    cm.ident_f[:, :]),
                    w=[("vE", eacc_t)] + (st["psF"][pb] or []), inc="tE")
                trt = c.val("tE")
                c.v(lambda e, pb=pb, eb=eb: e.tensor_copy(etile[eb][:, :],
                                                          psF2[pb][:, :]),
                    w=[("tE", trt)] + (st["et"][eb] or []))
                ev_ = c.val("vE")
                st["psF"][pb] = [("vE", ev_)]
                c.dma("sync", f"pre{eb}",
                      expdT[dt * 128:(dt + 1) * 128, t0:t0 + 128],
                      etile[eb][:, :], w=[("vE", ev_)])
                st["et"][eb] = [(f"pre{eb}", c.val(f"pre{eb}"))]
            st["eacc"] = [("tE", c.val("tE"))]
            st["xbt"] = [("tE", c.val("tE"))]
            st["y3n"] = [("vE", c.val("vE"))]
        c.bar()

    return logic



def phase_l1ffn_dense(nc, cm, din, src_f32, src_bf, expdT, dst_f32, dst_bf, E):
    ep = alloc_epilogue(nc, E, 512)
    Wdn = [E(nc.sbuf_tensor(f"du_wd{i}", [128, 4 * 1024], BF16))
           for i in range(8)]
    Wup = [E(nc.sbuf_tensor(f"du_wu{i}", [128, 8 * 1024], BF16))
           for i in range(2)]
    Usb = E(nc.sbuf_tensor("du_u", [128, 32 * 512], BF16))
    xb = E(nc.sbuf_tensor("du_xb", [128, NDT * 512], BF16))
    xft = [E(nc.sbuf_tensor(f"du_xf{i}", [128, 512], F32)) for i in range(2)]
    etl = [E(nc.sbuf_tensor(f"du_et{i}", [128, 512], F32)) for i in range(2)]
    psU = [E(nc.psum_tensor(f"du_psu{i}", [128, 512], F32)) for i in range(2)]
    psP = [E(nc.psum_tensor(f"du_psp{i}", [128, 512], F32)) for i in range(2)]

    def logic(c):
        for g in range(8):
            c.dma("gpsimd", "duwd",
                  Wdn[g][:, :].rearrange("p (it d) -> p it d", d=1024),
                  din["WdnT"].ap().rearrange("(it p) d -> p it d", p=128)[
                      :, 4 * g:4 * g + 4, :])
        wd_t = c.val("duwd")
        st = {"xb": None, "xf": [None, None], "et": [None, None],
              "psU": [None, None], "psP": [None, None], "U": None,
              "wu": [None, None]}
        state = {}
        for chk in range(T // 512):
            c0 = chk * 512
            c.dma("sync", "dux",
                  xb[:, :].rearrange("p (dt cc) -> p dt cc", cc=512),
                  src_bf.rearrange("(dt p) t -> p dt t", p=128)[
                      :, :, c0:c0 + 512], w=(st["xb"] or []))
            xb_t = c.val("dux")
            uw = (st["U"] or [])
            for q in range(4):
                wb = q % 2
                ws = f"duw{wb}"
                c.dma("gpsimd", ws,
                      Wup[wb][:, :].rearrange("p (dt i) -> p dt i", i=1024),
                      din["WupT"].ap().rearrange("(dt p) i -> p dt i", p=128)[
                          :, :, q * 1024:(q + 1) * 1024],
                      w=(st["wu"][wb] or []))
                w_t = c.val(ws)
                for itl in range(8):
                    ub = itl % 2
                    for dt in range(NDT):
                        w = []
                        if dt == 0:
                            w = [(ws, w_t), ("dux", xb_t),
                                 ("aE", c.val("aE"))] + \
                                (uw if q == 0 and itl == 0 else [])
                        c.t(lambda e, q=q, itl=itl, dt=dt, ub=ub, wb=wb:
                            e.matmul(psU[ub][:, :],
                                     Wup[wb][:, dt * 1024 + itl * 128:
                                             dt * 1024 + (itl + 1) * 128],
                                     xb[:, dt * 512:(dt + 1) * 512],
                                     start=(dt == 0), stop=(dt == NDT - 1)),
                            w=w, inc=("tE" if dt == NDT - 1 else None))
                    ut = c.val("tE")
                    c.a(lambda e, q=q, itl=itl, ub=ub: e.activation(
                        Usb[:, (8 * q + itl) * 512:(8 * q + itl + 1) * 512],
                        psU[ub][:, :], AF.Silu, bias=cm.c(C_ZERO),
                        scale=cm.c(C_ONE)), w=[("tE", ut)])
                st["wu"][wb] = [("tE", c.val("tE"))]
            u_all = c.val("aE")

            def make_z(c, dt, ztile, chk=chk, u_all=u_all):
                pb = dt % 2
                for it in range(32):
                    w = []
                    if it == 0:
                        w = [("aE", u_all), ("duwd", wd_t)] + \
                            (st["psP"][pb] or [])
                    c.t(lambda e, dt=dt, it=it, pb=pb: e.matmul(
                        psP[pb][:, :],
                        Wdn[it // 4][:, (it % 4) * 1024 + dt * 128:
                                     (it % 4) * 1024 + (dt + 1) * 128],
                        Usb[:, it * 512:(it + 1) * 512],
                        start=(it == 0), stop=(it == 31)), w=w,
                        inc=("tE" if it == 31 else None))
                pt = c.val("tE")
                b = dt % 2
                c.dma("sync", f"due{b}", etl[b][:, :],
                      expdT[dt * 128:(dt + 1) * 128, chk * 512:(chk + 1) * 512],
                      w=(st["et"][b] or []))
                c.dma("sync", f"duf{b}", xft[b][:, :],
                      src_f32[dt * 128:(dt + 1) * 128,
                              chk * 512:(chk + 1) * 512],
                      w=(st["xf"][b] or []))
                zf = state.get("z_free") if dt == 0 else None
                c.v(lambda e, dt=dt, pb=pb, b=b: e.tensor_tensor(
                    ztile[:, :], psP[pb][:, :], etl[b][:, :], ALU.add),
                    w=[("tE", pt), (f"due{b}", c.val(f"due{b}")),
                       (f"duf{b}", c.val(f"duf{b}"))] + (zf or []))
                c.v(lambda e, dt=dt, b=b: e.tensor_tensor(
                    ztile[:, :], ztile[:, :], xft[b][:, :], ALU.add))
                zt = c.val("vE")
                st["psP"][pb] = [("vE", zt)]
                st["et"][b] = [("vE", zt)]
                st["xf"][b] = [("vE", zt)]
                return [("vE", zt)]

            emit_epilogue_chunk(c, cm, ep, chk, make_z, dst_f32, dst_bf, state)
            st["U"] = [("tE", c.val("tE"))]
            st["xb"] = [("tE", c.val("tE"))]
        c.bar()

    return logic



def phase_projepi(nc, cm, din, wkey, brow, src_f32, actT, dst_f32, dst_bf, E):
    ep = alloc_epilogue(nc, E, 512)
    u = _EP_UNIQ[0]
    Wsb = [E(nc.sbuf_tensor(f"pe{u}_w{i}", [128, D], BF16)) for i in range(NDT)]
    ytl = [E(nc.sbuf_tensor(f"pe{u}_y{i}", [128, 512], BF16)) for i in range(8)]
    xft = [E(nc.sbuf_tensor(f"pe{u}_xf{i}", [128, 512], F32)) for i in range(2)]
    bo_sb = E(nc.sbuf_tensor(f"pe{u}_bo", [1, D], F32))
    psP = [E(nc.psum_tensor(f"pe{u}_psP{i}", [128, 512], F32)) for i in range(2)]

    def logic(c):
        for dt in range(NDT):
            c.dma("gpsimd", "pw", Wsb[dt][:, :],
                  din[wkey].ap()[dt * 128:(dt + 1) * 128, :])
        if brow is not None:
            c.dma("sync", "cs", bo_sb[:, :],
                  din["bqkvo"].ap()[brow:brow + 1, :])
        wt = c.val("pw")
        bt = c.val("cs")
        state = {}
        yfree = [None] * 8
        xfree = [None, None]
        pfree = [None, None]
        for chk in range(T // 512):
            ready = []
            for et in range(NDT):
                ys = f"pyl{et}"
                c.dma("sync", ys, ytl[et][:, :],
                      actT[et * 128:(et + 1) * 128,
                           chk * 512:(chk + 1) * 512],
                      w=(yfree[et] or []))
                ready.append((ys, c.val(ys)))

            def make_z(c, dt, ztile, chk=chk, ready=ready):
                pb = dt % 2
                for et in range(NDT):
                    w = [ready[et], ("pw", wt)]
                    if et == 0:
                        w += (pfree[pb] or [])
                    c.t(lambda e, dt=dt, et=et, pb=pb: e.matmul(
                        psP[pb][:, :], Wsb[et][:, dt * 128:(dt + 1) * 128],
                        ytl[et][:, :], start=(et == 0),
                        stop=(brow is None and et == NDT - 1)), w=w,
                        inc=("tE" if et == NDT - 1 and brow is None else None))
                if brow is not None:
                    c.t(lambda e, dt=dt, pb=pb: e.matmul(
                        psP[pb][:, :], bo_sb[:, dt * 128:(dt + 1) * 128],
                        cm.ones_row_f512[:, :], start=False, stop=True),
                        w=[("cs", bt)], inc="tE")
                pt = c.val("tE")
                xb_ = dt % 2
                xs = f"pxf{xb_}"
                c.dma("sync", xs, xft[xb_][:, :],
                      src_f32[dt * 128:(dt + 1) * 128,
                              chk * 512:(chk + 1) * 512],
                      w=(xfree[xb_] or []))
                zf = state.get("z_free") if dt == 0 else None
                c.v(lambda e, dt=dt, pb=pb, xb_=xb_: e.tensor_tensor(
                    ztile[:, :], psP[pb][:, :], xft[xb_][:, :], ALU.add),
                    w=[("tE", pt), (xs, c.val(xs))] + (zf or []))
                zt = c.val("vE")
                pfree[pb] = [("vE", zt)]
                xfree[xb_] = [("vE", zt)]
                return [("vE", zt)]

            emit_epilogue_chunk(c, cm, ep, chk, make_z, dst_f32, dst_bf, state)
            for et in range(NDT):
                yfree[et] = [("tE", c.val("tE"))]
        c.bar()

    return logic



PHASE_ORDER = ("l0attn", "l0ffn", "l1attn", "l1ffn")


def build_program(phases=PHASE_ORDER):
    nc = bass.Bass("TRN2", target_bir_lowering=False)
    din = declare_inputs(nc)
    zA = nc.dram_tensor("zA", [D, T], F32, kind="Internal")
    zB = nc.dram_tensor("zB", [D, T], F32, kind="Internal")
    zAb = nc.dram_tensor("zAb", [D, T], BF16, kind="Internal")
    zBb = nc.dram_tensor("zBb", [D, T], BF16, kind="Internal")
    x_bf = nc.dram_tensor("x_bf", [D, T], BF16, kind="Internal")
    ymoe = nc.dram_tensor("ymoe", [D, T], F32,
                          kind=("ExternalOutput" if DEV.get("dump_ymoe")
                                else "Internal"))
    selT_dram = nc.dram_tensor("selT", [8, T], F32,
                               kind=("ExternalOutput" if DEV.get("dump_sel")
                                     else "Internal"))
    tdu = nc.dram_tensor("tdu", [4096, 2048], BF16, kind="Internal")
    attO = nc.dram_tensor("attO", [D, T], BF16,
                          kind=("ExternalOutput" if DEV.get("dump_attO")
                                else "Internal"))
    expdT = nc.dram_tensor("expdT", [D, T], F32, kind="Internal")
    mscr = nc.dram_tensor("mscr", [4, 17340672], BF16, kind="Internal")
    outT = nc.dram_tensor("outT", [D, T], F32, kind="ExternalOutput")

    slots = [(zA.ap(), zAb.ap()), (zB.ap(), zBb.ap())]
    cnt = {}
    with ExitStack() as top:
        pool = SemPool(nc, top)
        cm = Common(nc, top.enter_context)
        block = top.enter_context(nc.Block())

        with ExitStack() as ph:
            plogic = phase_prep(nc, cm, din, din["xT"].ap(), x_bf.ap(), tdu,
                                ph.enter_context)

            def first(c):
                emit_common(c, cm, din)
                plogic(c)

            cnt = run_phase(block, pool, cnt, first)

        cur_f, cur_b = din["xT"].ap(), x_bf.ap()
        for pi, pname in enumerate(phases):
            last = pi == len(phases) - 1
            dst_f, dst_b = (outT.ap(), None) if last else slots[pi % 2]
            if pname == "l0attn":
                with ExitStack() as ph:
                    logic = phase_l0attn(nc, cm, din, cur_f, cur_b, mscr,
                                         attO.ap(), ph.enter_context)
                    cnt = run_phase(block, pool, cnt, logic)
                with ExitStack() as ph:
                    logic = phase_projepi(nc, cm, din, "WoT", 3, cur_f,
                                          attO.ap(), dst_f, dst_b,
                                          ph.enter_context)
                    cnt = run_phase(block, pool, cnt, logic)
            elif pname == "l0ffn":
                with ExitStack() as ph:
                    logic = phase_l0ffn_gate(nc, cm, din, cur_f, selT_dram,
                                             ph.enter_context)
                    cnt = run_phase(block, pool, cnt, logic)
                with ExitStack() as ph:
                    logic = phase_l0ffn_experts(nc, cm, din, cur_b, ymoe,
                                                selT_dram, ph.enter_context)
                    cnt = run_phase(block, pool, cnt, logic)
                with ExitStack() as ph:
                    logic = phase_l0ffn_epi(nc, cm, din, cur_f, ymoe, dst_f,
                                            dst_b, ph.enter_context)
                    cnt = run_phase(block, pool, cnt, logic)
            elif pname == "l1ffn":
                with ExitStack() as ph:
                    logic = phase_l1ffn_peer(nc, cm, din, cur_f, cur_b,
                                             tdu, expdT.ap(),
                                             ph.enter_context)
                    cnt = run_phase(block, pool, cnt, logic)
                with ExitStack() as ph:
                    logic = phase_l1ffn_dense(nc, cm, din, cur_f, cur_b,
                                              expdT.ap(), dst_f, dst_b,
                                              ph.enter_context)
                    cnt = run_phase(block, pool, cnt, logic)
            elif pname == "l1attn":
                with ExitStack() as ph:
                    logic = phase_l1attn_main(nc, cm, din, cur_b, attO.ap(),
                                              ph.enter_context)
                    cnt = run_phase(block, pool, cnt, logic)
                with ExitStack() as ph:
                    logic = phase_projepi(nc, cm, din, "w_out", None, cur_f,
                                          attO.ap(), dst_f, dst_b,
                                          ph.enter_context)
                    cnt = run_phase(block, pool, cnt, logic)
            else:
                raise NotImplementedError(pname)
            cur_f, cur_b = dst_f, dst_b

        def fin(c):
            c.bar()

        cnt = run_phase(block, pool, cnt, fin)
    return nc


def kernel(**inputs):
    sh = host_prep(inputs)
    x = np.asarray(inputs["L0_attn_x"])
    nc = build_program()
    in_maps = []
    for b in range(8):
        m = dict(sh)
        m["xT"] = np.ascontiguousarray(x[b].T.astype(np.float32))
        in_maps.append(m)
    res = run_bass_kernel_spmd(nc, in_maps, core_ids=list(range(8)))
    out = np.empty((8, T, D), np.float32)
    for b in range(8):
        out[b] = res.results[b]["outT"].T
    return out


# version 04 (final)
# speedup vs baseline: 1.4629x; 1.2054x over previous
from contextlib import ExitStack

import numpy as np

import concourse.bass as bass
import concourse.mybir as mybir
from concourse.bass_types import AP
from concourse.bass_core_rust import IndirectOffsetOnAxis
from concourse.bass_utils import run_bass_kernel_spmd

F32 = mybir.dt.float32
BF16 = mybir.dt.bfloat16
I32 = mybir.dt.int32
U32 = mybir.dt.uint32
AF = mybir.ActivationFunctionType
ALU = mybir.AluOpType
AXX = mybir.AxisListType.X

D = 1024
T = 4096
NDT = D // 128

ENGINES = ("sync", "tensor", "scalar", "vector", "gpsimd")
ENG_SEM = {"sync": "sE", "tensor": "tE", "scalar": "aE", "vector": "vE",
           "gpsimd": "gE"}

C_EXPB, C_ONE, C_QS, C_ZERO, C_EPS = range(5)
CONST_VALS = [-20.0, 1.0, 0.125, 0.0, 1e-5]

DEV = {}
SELF_EDGE = True


class SemPool:

    def __init__(self, nc, stack):
        self.nc = nc
        self.stack = stack
        self.handles = {}

    def get(self, name):
        if name not in self.handles:
            self.handles[name] = self.stack.enter_context(self.nc.semaphore(name))
        return self.handles[name]


class Ctx:

    def __init__(self, active, eng, pool, base):
        self.active = active
        self.eng = eng
        self.pool = pool
        self.cnt = dict(base)

    def val(self, s):
        return self.cnt.get(s, 0)

    def op(self, engname, fn, w=(), inc=None, n=1):
        tgt = None
        if inc is not None:
            self.cnt[inc] = self.cnt.get(inc, 0) + n
            tgt = self.cnt[inc]
        if engname == self.active:
            for s, v in w:
                if v > 0:
                    self.eng.wait_ge(self.pool.get(s), v)
            ins = fn(self.eng)
            if inc is not None:
                ins.then_inc(self.pool.get(inc), n)
        return tgt

    def t(self, fn, w=(), inc=None):
        return self.op("tensor", fn, w, inc)

    def a(self, fn, w=(), inc="aE"):
        if SELF_EDGE:
            w = list(w) + [("aE", self.val("aE"))]
        return self.op("scalar", fn, w, inc)

    def v(self, fn, w=(), inc="vE"):
        if SELF_EDGE:
            w = list(w) + [("vE", self.val("vE"))]
        return self.op("vector", fn, w, inc)

    def g(self, fn, w=(), inc="gE"):
        if SELF_EDGE:
            w = list(w) + [("gE", self.val("gE"))]
        return self.op("gpsimd", fn, w, inc)

    def dma(self, q, sem, out_ap, in_ap, w=(), **kw):
        return self.op(q, lambda e: e.dma_start(out_ap, in_ap, **kw), w, sem, 16)

    def idma(self, sem, out_ap, in_ap, in_off, w=()):
        return self.op("gpsimd",
                       lambda e: e.indirect_dma_start(out_ap, None, in_ap,
                                                      in_off),
                       w, sem, 16)

    def bar(self):
        bname = f"bar{self.cnt.get('_barid', 0)}"
        self.cnt["_barid"] = self.cnt.get("_barid", 0) + 1
        for e in ENGINES:
            self.op(e, lambda x: x.nop(), (), bname, 1)
        waits = [(bname, 5)]
        waits += [(s, v) for s, v in sorted(self.cnt.items())
                  if s not in ENG_SEM.values() and not s.startswith("bar")
                  and s != "_barid" and v > 0]
        for e in ENGINES:
            self.op(e, lambda x: x.nop(), waits)


def run_phase(block, pool, base_cnt, phase_logic):
    finals = []
    for engname in ENGINES:
        holder = {}

        def section(eng, engname=engname, holder=holder):
            c = Ctx(engname, eng, pool, base_cnt)
            phase_logic(c)
            holder["cnt"] = c.cnt

        getattr(block, engname)(section)
        finals.append(holder["cnt"])
    for f in finals[1:]:
        assert f == finals[0], "phase emission diverged between engines"
    return finals[0]



def g_of_d(d):
    idx = d + 2047
    idx = np.where(idx < 0, idx + 4095, idx)
    return np.clip(idx, 0, 4094)


def host_prep(inputs):
    f = np.float32

    def t(a):
        return np.ascontiguousarray(np.asarray(a).astype(f).T)

    sh = {}
    sh["WqT"] = t(inputs["L0_attn_Wq"])
    sh["WkT"] = t(inputs["L0_attn_Wk"])
    sh["WvT"] = t(inputs["L0_attn_Wv"])
    sh["WoT"] = t(inputs["L0_attn_Wo"])
    sh["bqkvo"] = np.stack([np.asarray(inputs["L0_attn_bq"]),
                            np.asarray(inputs["L0_attn_bk"]),
                            np.asarray(inputs["L0_attn_bv"]),
                            np.asarray(inputs["L0_attn_bo"])]).astype(f)
    RT = np.asarray(inputs["L0_attn_rel_emb"])[g_of_d(np.arange(-4095, 4096))]
    sh["RT"] = np.ascontiguousarray(RT.T.astype(f))
    sh["Wg"] = np.asarray(inputs["L0_ffn_Wg"]).astype(f)
    sh["bg"] = np.asarray(inputs["L0_ffn_bg"]).astype(f).reshape(1, 8)
    sh["W1"] = np.asarray(inputs["L0_ffn_W1"]).astype(f)
    sh["b1r"] = np.ascontiguousarray(np.asarray(inputs["L0_ffn_b1"]).astype(f)
                                     .reshape(8, 32, 128).transpose(2, 0, 1)
                                     .reshape(128, 256))
    sh["W2"] = np.asarray(inputs["L0_ffn_W2"]).astype(f)
    sh["b2r"] = np.ascontiguousarray(np.asarray(inputs["L0_ffn_b2"]).astype(f)
                                     .reshape(8, 8, 128).transpose(2, 0, 1)
                                     .reshape(128, 64))
    sh["wT"] = t(inputs["L1_attn_w"])
    sh["w_out"] = np.asarray(inputs["L1_attn_w_out"]).astype(f)
    sh["WupT"] = t(inputs["L1_ffn_W_up"])
    sh["WdnT"] = t(inputs["L1_ffn_W_down"])
    sh["Wq2T"] = t(inputs["L1_ffn_W_q"])
    sh["keysT"] = np.ascontiguousarray(
        np.asarray(inputs["L1_ffn_keys"]).astype(f).transpose(2, 0, 3, 1))
    sh["de"] = np.asarray(inputs["L1_ffn_down_embed"]).astype(f)
    sh["ue"] = np.asarray(inputs["L1_ffn_up_embed"]).astype(f)
    sh["consts"] = np.tile(np.array(CONST_VALS, f)[None, :], (128, 1))
    return sh


def declare_inputs(nc):
    d = {}
    for name, shape in [
        ("xT", [D, T]),
        ("WqT", [D, D]), ("WkT", [D, D]), ("WvT", [D, D]), ("WoT", [D, D]),
        ("bqkvo", [4, D]), ("RT", [64, 8191]),
        ("Wg", [D, 8]), ("bg", [1, 8]), ("W1", [8, D, 4096]),
        ("b1r", [128, 256]), ("W2", [8, 4096, D]), ("b2r", [128, 64]),
        ("wT", [D, 3 * D]), ("w_out", [D, D]),
        ("WupT", [D, 4096]), ("WdnT", [4096, D]), ("Wq2T", [D, 512]),
        ("keysT", [2, 4, 64, 64]), ("de", [4096, D]), ("ue", [4096, D]),
        ("consts", [128, 5]),
    ]:
        d[name] = nc.dram_tensor(name, shape, F32, kind="ExternalInput")
    return d



class Common:
    def __init__(self, nc, E):
        self.consts = E(nc.sbuf_tensor("cc_consts", [128, 5], F32))
        self.ones_col_f = E(nc.sbuf_tensor("cc_onc_f", [128, 1], F32))
        self.ones_col_b = E(nc.sbuf_tensor("cc_onc_b", [128, 1], BF16))
        self.ones_row_f = E(nc.sbuf_tensor("cc_onr_f", [1, 128], F32))
        self.ones_row_f512 = E(nc.sbuf_tensor("cc_onr_f5", [1, 512], F32))
        self.ident_b = E(nc.sbuf_tensor("cc_id_b", [128, 128], BF16))
        self.ident_f = E(nc.sbuf_tensor("cc_id_f", [128, 128], F32))
        self.iota_d = E(nc.sbuf_tensor("cc_iod", [128, 128], F32))
        self.iota8 = E(nc.sbuf_tensor("cc_io8", [128, 8], F32))
        self.iota64 = E(nc.sbuf_tensor("cc_io64", [128, 64], F32))
        self.tril_b = E(nc.sbuf_tensor("cc_tril", [128, 128], BF16))
        self.ones2 = E(nc.sbuf_tensor("cc_on2", [128, 2], BF16))
        self.sel2T = E(nc.sbuf_tensor("cc_s2t", [2, 128], F32))
        self.ones_t128 = E(nc.sbuf_tensor("cc_o128", [128, 128], F32))

    def c(self, col):
        return self.consts[:, col:col + 1]

    def c1(self, col):
        return self.consts[0:1, col:col + 1]


def emit_common(c, cm, din):
    c.dma("sync", "cs", cm.consts[:, :], din["consts"].ap())
    c.op("vector", lambda e: e.nop(), [("cs", c.val("cs"))])
    c.v(lambda e: e.memset(cm.ones_col_f[:, :], 1.0))
    c.v(lambda e: e.memset(cm.ones_col_b[:, :], 1.0))
    c.v(lambda e: e.memset(cm.ones_row_f[:, :], 1.0))
    c.v(lambda e: e.memset(cm.ones_row_f512[:, :], 1.0))
    c.v(lambda e: e.memset(cm.ones2[:, :], 0.0))
    c.v(lambda e: e.memset(cm.ones2[0:64, 0:1], 1.0))
    c.v(lambda e: e.memset(cm.ones2[64:128, 1:2], 1.0))
    c.g(lambda e: e.iota(cm.iota_d[:, :], pattern=[[1, 128]], base=0,
                         channel_multiplier=-1,
                         allow_small_or_imprecise_dtypes=True))
    io_t = c.val("gE")
    c.g(lambda e: e.iota(cm.iota8[:, :], pattern=[[1, 8]], base=0,
                         channel_multiplier=0,
                         allow_small_or_imprecise_dtypes=True))
    c.g(lambda e: e.iota(cm.iota64[:, :], pattern=[[1, 64]], base=0,
                         channel_multiplier=0,
                         allow_small_or_imprecise_dtypes=True))
    c.g(lambda e: e.iota(cm.ones_t128[0:2, 0:128], pattern=[[1, 128]], base=0,
                         channel_multiplier=-64,
                         allow_small_or_imprecise_dtypes=True))
    s2t = c.val("gE")
    c.v(lambda e: e.tensor_scalar(cm.sel2T[:, :], cm.ones_t128[0:2, 0:128],
                                  63.0, None, ALU.is_le), w=[("gE", s2t)])
    c.v(lambda e: e.tensor_scalar(cm.ones_t128[0:2, 0:128],
                                  cm.ones_t128[0:2, 0:128], 0.0, None,
                                  ALU.is_ge))
    c.v(lambda e: e.tensor_tensor(cm.sel2T[:, :], cm.sel2T[:, :],
                                  cm.ones_t128[0:2, 0:128], ALU.mult))
    c.v(lambda e: e.memset(cm.ones_t128[:, :], 1.0))
    c.v(lambda e: e.tensor_scalar(cm.ident_b[:, :], cm.iota_d[:, :], 0.0, None,
                                  ALU.is_equal), w=[("gE", io_t)])
    c.v(lambda e: e.tensor_scalar(cm.ident_f[:, :], cm.iota_d[:, :], 0.0, None,
                                  ALU.is_equal))
    c.v(lambda e: e.tensor_scalar(cm.tril_b[:, :], cm.iota_d[:, :], 0.0, None,
                                  ALU.is_ge))
    c.bar()



def phase_prep(nc, cm, din, src_f32, dst_bf, tdu, E):
    tf = [E(nc.sbuf_tensor(f"pp_tf{i}", [128, 4096], F32)) for i in range(2)]
    tb = [E(nc.sbuf_tensor(f"pp_tb{i}", [128, 4096], BF16)) for i in range(2)]

    def logic(c):
        if DEV.get("skip_prep"):
            c.bar()
            return
        for dt in range(NDT):
            c.dma("gpsimd", "pxc", dst_bf[dt * 128:(dt + 1) * 128, :],
                  src_f32[dt * 128:(dt + 1) * 128, :])
        for half, tab in ((0, din["de"]), (1, din["ue"])):
            for rt in range(8):
                b = rt % 2
                ls, ss = f"ppt{b}", f"ppu{b}"
                c.dma("sync", ls, tf[b][:, :].rearrange("p (a d) -> p a d",
                                                        d=1024),
                      tab.ap()[rt * 512:(rt + 1) * 512, :]
                      .rearrange("(a p) d -> p a d", p=128),
                      w=[("vE", c.val("vE"))])
                c.v(lambda e, b=b: e.tensor_copy(tb[b][:, :], tf[b][:, :]),
                    w=[(ls, c.val(ls)), (ss, c.val(ss))])
                ct = c.val("vE")
                c.dma("sync", ss,
                      tdu.ap()[rt * 512:(rt + 1) * 512,
                               half * 1024:(half + 1) * 1024]
                      .rearrange("(a p) d -> p a d", p=128),
                      tb[b][:, :].rearrange("p (a d) -> p a d", d=1024),
                      w=[("vE", ct)])
        c.bar()

    return logic



_EP_UNIQ = [0]


def alloc_epilogue(nc, E, ch=512):
    u = _EP_UNIQ[0]
    _EP_UNIQ[0] += 1
    ep = {"CH": ch}
    ep["z"] = [E(nc.sbuf_tensor(f"ep{u}_z{i}", [128, ch], F32))
               for i in range(NDT)]
    ep["sq"] = [E(nc.sbuf_tensor(f"ep{u}_sq{i}", [128, ch], F32))
                for i in range(2)]
    ep["ybf"] = [E(nc.sbuf_tensor(f"ep{u}_yb{i}", [128, ch], BF16))
                 for i in range(2)]
    for nm in ("m", "m2", "var", "sd", "rstd", "mr"):
        ep[nm] = E(nc.sbuf_tensor(f"ep{u}_{nm}", [1, ch], F32))
    ep["ps_s"] = E(nc.psum_tensor(f"ep{u}_ps_s", [1, ch], F32))
    ep["ps_q"] = E(nc.psum_tensor(f"ep{u}_ps_q", [1, ch], F32))
    ep["ps_r"] = E(nc.psum_tensor(f"ep{u}_ps_r", [128, ch], F32))
    ep["ps_m"] = E(nc.psum_tensor(f"ep{u}_ps_m", [128, ch], F32))
    return ep


def emit_epilogue_chunk(c, cm, ep, ch_idx, make_z, dst_f32, dst_bf, state):
    CH = ep["CH"]
    c0 = ch_idx * CH
    zrdy = []
    for dt in range(NDT):
        zrdy.append(make_z(c, dt, ep["z"][dt]))
    if DEV.get("epi_mode", 3) == 1:
        for dt in range(NDT):
            c.dma("sync", "ezs", dst_f32[dt * 128:(dt + 1) * 128, c0:c0 + CH],
                  ep["z"][dt][:, :], w=list(zrdy[dt]))
        state["z_free"] = [("ezs", c.val("ezs"))]
        state["bc_free"] = []
        return
    for dt in range(NDT):
        b = dt % 2
        c.a(lambda e, dt=dt, b=b: e.activation(ep["sq"][b][:, :],
                                               ep["z"][dt][:, :], AF.Square,
                                               bias=cm.c(C_ZERO),
                                               scale=cm.c(C_ONE)),
            w=list(zrdy[dt]) + (state.get(f"sqf{b}") or []))
        sqt = c.val("aE")
        stw = list(zrdy[dt])
        if dt == 0 and state.get("stat_free"):
            stw += state["stat_free"]
        c.t(lambda e, dt=dt: e.matmul(ep["ps_s"][0:1, :], cm.ones_col_f[:, :],
                                      ep["z"][dt][:, :], start=(dt == 0),
                                      stop=(dt == NDT - 1)), w=stw)
        c.t(lambda e, b=b: e.matmul(ep["ps_q"][0:1, :], cm.ones_col_f[:, :],
                                    ep["sq"][b][:, :], start=(dt == 0),
                                    stop=(dt == NDT - 1)),
            w=[("aE", sqt)], inc="tE")
        state[f"sqf{b}"] = [("tE", c.val("tE"))]
    mmt = c.val("tE")
    if DEV.get("epi_mode", 3) == 2:
        for dt in range(NDT):
            c.dma("sync", "ezs", dst_f32[dt * 128:(dt + 1) * 128, c0:c0 + CH],
                  ep["z"][dt][:, :], w=list(zrdy[dt]) + [("tE", mmt)])
        state["z_free"] = [("ezs", c.val("ezs"))]
        state["bc_free"] = []
        return
    c.v(lambda e: e.tensor_scalar(ep["m"][:, :], ep["ps_s"][0:1, :], 1.0 / D,
                                  None, ALU.mult), w=[("tE", mmt)])
    c.v(lambda e: e.tensor_scalar(ep["var"][:, :], ep["ps_q"][0:1, :], 1.0 / D,
                                  None, ALU.mult))
    c.v(lambda e: e.tensor_tensor(ep["m2"][:, :], ep["m"][:, :], ep["m"][:, :],
                                  ALU.mult))
    c.v(lambda e: e.tensor_tensor(ep["var"][:, :], ep["var"][:, :],
                                  ep["m2"][:, :], ALU.subtract))
    c.v(lambda e: e.tensor_scalar(ep["var"][:, :], ep["var"][:, :], 1e-5, None,
                                  ALU.add))
    vt = c.val("vE")
    state["stat_free"] = [("vE", vt)]
    c.a(lambda e: e.activation(ep["sd"][:, :], ep["var"][:, :], AF.Sqrt,
                               bias=cm.c1(C_ZERO), scale=cm.c1(C_ONE)),
        w=[("vE", vt)])
    at = c.val("aE")
    c.v(lambda e: e.reciprocal(ep["rstd"][:, :], ep["sd"][:, :]),
        w=[("aE", at)])
    rt = c.val("vE")
    c.v(lambda e: e.tensor_tensor(ep["mr"][:, :], ep["m"][:, :],
                                  ep["rstd"][:, :], ALU.mult))
    mrt = c.val("vE")
    bw = state.get("bc_free") or []
    c.t(lambda e: e.matmul(ep["ps_r"][:, :], cm.ones_row_f[:, :],
                           ep["rstd"][:, :], start=True, stop=True),
        w=[("vE", rt)] + bw)
    c.t(lambda e: e.matmul(ep["ps_m"][:, :], cm.ones_row_f[:, :],
                           ep["mr"][:, :], start=True, stop=True),
        w=[("vE", mrt)], inc="tE")
    bct = c.val("tE")
    if DEV.get("epi_mode", 3) == 25:
        for dt in range(NDT):
            c.dma("sync", "ezs", dst_f32[dt * 128:(dt + 1) * 128, c0:c0 + CH],
                  ep["z"][dt][:, :], w=list(zrdy[dt]) + [("tE", bct)])
        state["z_free"] = [("ezs", c.val("ezs"))]
        state["bc_free"] = [("tE", bct)]
        return
    if DEV.get("epi_mode", 3) == 31:
        for dt in range(NDT):
            c.v(lambda e, dt=dt: e.tensor_tensor(ep["z"][dt][:, :],
                                                 ep["z"][dt][:, :],
                                                 ep["ps_r"][:, :], ALU.mult),
                w=[("tE", bct)])
            yt = c.val("vE")
            c.dma("sync", "ezs", dst_f32[dt * 128:(dt + 1) * 128, c0:c0 + CH],
                  ep["z"][dt][:, :], w=[("vE", yt)])
        state["z_free"] = [("ezs", c.val("ezs"))]
        state["bc_free"] = [("vE", c.val("vE"))]
        return
    if DEV.get("epi_mode", 3) == 37:
        c.t(lambda e: e.matmul(ep["ps_r"][:, :], cm.ident_f[:, :],
                               ep["z"][0][:, :], start=True, stop=True),
            w=[("tE", bct)], inc="tE")
        k128t = c.val("tE")
        c.a(lambda e: e.activation(ep["sq"][0][:, :], ep["ps_r"][:, :],
                                   AF.Identity, bias=cm.c(C_ZERO),
                                   scale=cm.c(C_ONE)), w=[("tE", k128t)])
        cpt = c.val("aE")
        c.dma("sync", "ezs", dst_f32[0:128, c0:c0 + CH], ep["sq"][0][:, :],
              w=[("aE", cpt)])
        for dt in range(1, NDT):
            c.dma("sync", "ezs", dst_f32[dt * 128:(dt + 1) * 128, c0:c0 + CH],
                  ep["z"][dt][:, :], w=list(zrdy[dt]) + [("tE", bct)])
        state["z_free"] = [("ezs", c.val("ezs"))]
        state["bc_free"] = [("aE", cpt)]
        return
    if DEV.get("epi_mode", 3) == 34:
        c.a(lambda e: e.activation(ep["sq"][0][:, :], ep["ps_r"][:, :],
                                   AF.Identity, bias=cm.c(C_ZERO),
                                   scale=cm.c(C_ONE)), w=[("tE", bct)])
        cpt = c.val("aE")
        c.dma("sync", "ezs", dst_f32[0:128, c0:c0 + CH], ep["sq"][0][:, :],
              w=[("aE", cpt)])
        for dt in range(1, NDT):
            c.dma("sync", "ezs", dst_f32[dt * 128:(dt + 1) * 128, c0:c0 + CH],
                  ep["z"][dt][:, :], w=list(zrdy[dt]) + [("tE", bct)])
        state["z_free"] = [("ezs", c.val("ezs"))]
        state["bc_free"] = [("aE", cpt)]
        return
    if DEV.get("epi_mode", 3) == 33:
        for dt in range(NDT):
            b = dt % 2
            c.v(lambda e, dt=dt, b=b: e.tensor_tensor(ep["sq"][b][:, :],
                                                      ep["z"][dt][:, :],
                                                      ep["ps_r"][:, :],
                                                      ALU.mult),
                w=[("tE", bct), (f"ez{b}", c.val(f"ez{b}"))])
            yt = c.val("vE")
            c.dma("sync", f"ez{b}",
                  dst_f32[dt * 128:(dt + 1) * 128, c0:c0 + CH],
                  ep["sq"][b][:, :], w=[("vE", yt)])
        state["z_free"] = [("ez0", c.val("ez0")), ("ez1", c.val("ez1"))]
        state["bc_free"] = [("vE", c.val("vE"))]
        return
    if DEV.get("epi_mode", 3) == 32:
        c.a(lambda e: e.activation(ep["sq"][0][:, :], ep["ps_r"][:, :],
                                   AF.Identity, bias=cm.c(C_ZERO),
                                   scale=cm.c(C_ONE)), w=[("tE", bct)])
        c.a(lambda e: e.activation(ep["sq"][1][:, :], ep["ps_m"][:, :],
                                   AF.Identity, bias=cm.c(C_ZERO),
                                   scale=cm.c(C_ONE)))
        cpt = c.val("aE")
        for dt in range(NDT):
            c.v(lambda e, dt=dt: e.tensor_tensor(ep["z"][dt][:, :],
                                                 ep["z"][dt][:, :],
                                                 ep["sq"][0][:, :], ALU.mult),
                w=[("aE", cpt)])
            c.v(lambda e, dt=dt: e.tensor_tensor(ep["z"][dt][:, :],
                                                 ep["z"][dt][:, :],
                                                 ep["sq"][1][:, :],
                                                 ALU.subtract))
            yt = c.val("vE")
            c.dma("sync", "ezs", dst_f32[dt * 128:(dt + 1) * 128, c0:c0 + CH],
                  ep["z"][dt][:, :], w=[("vE", yt)])
        state["z_free"] = [("ezs", c.val("ezs"))]
        state["bc_free"] = [("aE", cpt)]
        return
    for dt in range(NDT):
        c.v(lambda e, dt=dt: e.tensor_tensor(ep["z"][dt][:, :],
                                             ep["z"][dt][:, :],
                                             ep["ps_r"][:, :], ALU.mult),
            w=[("tE", bct)])
        c.v(lambda e, dt=dt: e.tensor_tensor(ep["z"][dt][:, :],
                                             ep["z"][dt][:, :],
                                             ep["ps_m"][:, :], ALU.subtract))
        yt = c.val("vE")
        c.dma("sync", "ezs", dst_f32[dt * 128:(dt + 1) * 128, c0:c0 + CH],
              ep["z"][dt][:, :], w=[("vE", yt)])
        if dst_bf is not None:
            b = dt % 2
            yb = f"eyb{b}"
            c.a(lambda e, dt=dt, b=b: e.activation(ep["ybf"][b][:, :],
                                                   ep["z"][dt][:, :],
                                                   AF.Identity,
                                                   bias=cm.c(C_ZERO),
                                                   scale=cm.c(C_ONE)),
                w=[("vE", yt), (yb, c.val(yb))])
            ybt = c.val("aE")
            c.dma("sync", yb, dst_bf[dt * 128:(dt + 1) * 128, c0:c0 + CH],
                  ep["ybf"][b][:, :], w=[("aE", ybt)])
    state["bc_free"] = [("vE", c.val("vE"))]
    state["z_free"] = [("ezs", c.val("ezs"))]
    if dst_bf is not None:
        state["z_free"] += [("eyb0", c.val("eyb0")), ("eyb1", c.val("eyb1"))]



def phase_l0ffn_gate(nc, cm, din, src_f32, selT_dram, E):
    Wg_sb = E(nc.sbuf_tensor("mg_wg", [128, NDT * 8], F32))
    bg_sb = E(nc.sbuf_tensor("mg_bg", [1, 8], F32))
    xf = [E(nc.sbuf_tensor(f"mg_xf{i}", [128, 512], F32)) for i in range(2)]
    lg = [E(nc.sbuf_tensor(f"mg_lg{i}", [8, 512], F32)) for i in range(2)]
    lgn = [E(nc.sbuf_tensor(f"mg_lgn{i}", [128, 8], F32)) for i in range(2)]
    selst = [E(nc.sbuf_tensor(f"mg_sest{i}", [8, 128], F32)) for i in range(2)]
    scr8 = E(nc.sbuf_tensor("mg_scr8", [128, 8], F32))
    mx8 = E(nc.sbuf_tensor("mg_mx8", [128, 8], F32))
    nmx = E(nc.sbuf_tensor("mg_nmx", [128, 1], F32))
    idx8 = E(nc.sbuf_tensor("mg_idx8", [128, 8], U32))
    idxf = E(nc.sbuf_tensor("mg_idxf", [128, 1], F32))
    s8 = E(nc.sbuf_tensor("mg_s8", [128, 1], F32))
    wrec = E(nc.sbuf_tensor("mg_wrec", [128, 1], F32))
    onehot = E(nc.sbuf_tensor("mg_oh", [128, 8], F32))
    selw = E(nc.sbuf_tensor("mg_selw", [128, 8], F32))
    I8f = E(nc.sbuf_tensor("mg_i8f", [8, 8], F32))
    psG = E(nc.psum_tensor("mg_psG", [8, 512], F32))
    psTr = E(nc.psum_tensor("mg_psTr", [128, 8], F32))
    psS = E(nc.psum_tensor("mg_psS", [8, 128], F32))

    def logic(c):
        c.dma("sync", "cs", Wg_sb[:, :].rearrange("p (dt e) -> p dt e", e=8),
              din["Wg"].ap().rearrange("(dt p) e -> p dt e", p=128))
        c.dma("sync", "cs", bg_sb[:, :], din["bg"].ap())
        smt = c.val("cs")
        c.v(lambda e: e.memset(I8f[:, :], 0.0))
        c.v(lambda e: e.tensor_scalar(I8f[:, :], cm.iota_d[0:8, 0:8], 0.0,
                                      None, ALU.is_equal))
        i8t = c.val("vE")
        xf_free = [None, None]
        lg_free = [None, None]
        sel_free = [None, None]
        psG_free = None
        psTr_free = None
        for chk in range(T // 512 if DEV.get("gate", True) else 0):
            for dt in range(NDT):
                b = dt % 2
                xs = f"ggx{b}"
                c.dma("sync", xs, xf[b][:, :],
                      src_f32[dt * 128:(dt + 1) * 128,
                              chk * 512:(chk + 1) * 512],
                      w=(xf_free[b] or []))
                w = [(xs, c.val(xs)), ("cs", smt)]
                if dt == 0 and psG_free:
                    w += psG_free
                c.t(lambda e, dt=dt, b=b: e.matmul(
                    psG[:, :], Wg_sb[:, dt * 8:(dt + 1) * 8], xf[b][:, :],
                    start=(dt == 0), stop=False), w=w, inc="tE")
                xf_free[b] = [("tE", c.val("tE"))]
            c.t(lambda e: e.matmul(psG[:, :], bg_sb[:, :],
                                   cm.ones_row_f512[:, :],
                                   start=False, stop=True),
                w=[("cs", smt)], inc="tE")
            gt = c.val("tE")
            b = chk % 2
            c.v(lambda e, b=b: e.tensor_copy(lg[b][:, :], psG[:, :]),
                w=[("tE", gt)] + (lg_free[b] or []))
            lgt = c.val("vE")
            psG_free = [("vE", lgt)]
            for sub in range(4):
                w = [("vE", lgt), ("vE", i8t)]
                if psTr_free:
                    w += psTr_free
                c.t(lambda e, b=b, sub=sub: e.transpose(
                    psTr[:, :], lg[b][:, sub * 128:(sub + 1) * 128],
                    I8f[:, :]), w=w, inc="tE")
                trt = c.val("tE")
                bb = sub % 2
                c.v(lambda e, bb=bb: e.tensor_copy(lgn[bb][:, :], psTr[:, :]),
                    w=[("tE", trt)])
                psTr_free = [("vE", c.val("vE"))]
                c.v(lambda e, bb=bb: e.max(mx8[:, :], lgn[bb][:, :]))
                c.v(lambda e, bb=bb: e.max_index(idx8[:, :], mx8[:, :],
                                                 lgn[bb][:, :]))
                c.v(lambda e: e.tensor_copy(idxf[:, :], idx8[:, 0:1]))
                c.v(lambda e: e.tensor_scalar(nmx[:, :], mx8[:, 0:1], -1.0,
                                              None, ALU.mult))
                nt = c.val("vE")
                c.a(lambda e, bb=bb: e.activation(scr8[:, :], lgn[bb][:, :],
                                                  AF.Exp, bias=nmx[:, 0:1],
                                                  scale=cm.c(C_ONE)),
                    w=[("vE", nt)])
                et = c.val("aE")
                c.v(lambda e: e.reduce_sum(s8[:, :], scr8[:, :], axis=AXX),
                    w=[("aE", et)])
                c.v(lambda e: e.reciprocal(wrec[:, :], s8[:, :]))
                c.v(lambda e: e.tensor_scalar(onehot[:, :], cm.iota8[:, :],
                                              idxf[:, 0:1], None,
                                              ALU.is_equal))
                c.v(lambda e: e.tensor_scalar(selw[:, :], onehot[:, :],
                                              wrec[:, 0:1], None, ALU.mult))
                swt = c.val("vE")
                sb = sub % 2
                ssm = f"gsel{sb}"
                c.t(lambda e: e.transpose(psS[:, :], selw[:, :],
                                          cm.ident_f[:, :]),
                    w=[("vE", swt)] + (sel_free[sb] or []), inc="tE")
                stt = c.val("tE")
                c.v(lambda e, sb=sb: e.tensor_copy(selst[sb][:, :], psS[:, :]),
                    w=[("tE", stt), (ssm, c.val(ssm))])
                svt = c.val("vE")
                c.dma("sync", ssm,
                      selT_dram.ap()[:, chk * 512 + sub * 128:
                                     chk * 512 + (sub + 1) * 128],
                      selst[sb][:, :], w=[("vE", svt)])
                sel_free[sb] = [(ssm, c.val(ssm))]
            lg_free[b] = [("tE", c.val("tE"))]
        c.bar()

    return logic



def phase_l0ffn_experts(nc, cm, din, src_bf, ymoe, selT_dram, E):
    MC = 512
    NMC = T // MC
    W1sb = [E(nc.sbuf_tensor(f"mo_w1_{i}", [128, 4096], BF16))
            for i in range(NDT)]
    W2sb = [E(nc.sbuf_tensor(f"mo_w2_{i}", [128, 4096], BF16))
            for i in range(8)]
    Hsb = E(nc.sbuf_tensor("mo_h", [128, 32 * MC], BF16))
    xch = [E(nc.sbuf_tensor(f"mo_x{i}", [128, NDT * MC], BF16))
           for i in range(2)]
    b1_sb = E(nc.sbuf_tensor("mo_b1", [128, 256], F32))
    b2_sb = E(nc.sbuf_tensor("mo_b2", [128, 64], F32))
    selrow = [E(nc.sbuf_tensor(f"mo_serow{i}", [1, MC], F32)) for i in range(2)]
    selwb = [E(nc.sbuf_tensor(f"mo_swb{i}", [128, MC], F32)) for i in range(2)]
    otile = [E(nc.sbuf_tensor(f"mo_ot{i}", [128, MC], F32)) for i in range(2)]
    psH = [E(nc.psum_tensor(f"mo_psH{i}", [128, MC], F32)) for i in range(2)]
    psO = [E(nc.psum_tensor(f"mo_psO{i}", [128, MC], F32)) for i in range(2)]
    psB = [E(nc.psum_tensor(f"mo_psB{i}", [128, MC], F32)) for i in range(2)]

    def logic(c):
        c.dma("sync", "cs", b1_sb[:, :], din["b1r"].ap())
        c.dma("sync", "cs", b2_sb[:, :], din["b2r"].ap())
        smt = c.val("cs")
        prev = {}
        xch_free = [None, None]
        psH_free = [None, None]
        psO_free = [None, None]
        ot_free = [None, None]
        h_prev_last = None
        o_prev_last = None
        for e_i in range(DEV.get("moe_e", 8)):
            for dt in range(NDT):
                c.dma("gpsimd", "mew", W1sb[dt][:, :],
                      din["W1"].ap()[e_i, dt * 128:(dt + 1) * 128, :],
                      w=([("tE", h_prev_last)]
                         if dt == 0 and h_prev_last is not None else []))
            w1_rdy = c.val("mew")
            for g in range(8):
                c.dma("gpsimd", "mew",
                      W2sb[g][:, :].rearrange("p (j f) -> p j f", f=1024),
                      din["W2"].ap()[e_i, g * 512:(g + 1) * 512, :]
                      .rearrange("(j p) f -> p j f", p=128),
                      w=([("tE", o_prev_last)]
                         if g == 0 and o_prev_last is not None else []))
            w_rdy = c.val("mew")
            for chk in range(DEV.get("moe_chk", NMC)):
                b = chk % 2
                xs, ss = f"mex{b}", f"mes{b}"
                c.dma("sync", xs,
                      xch[b][:, :].rearrange("p (dt c) -> p dt c", c=MC),
                      src_bf.rearrange("(dt p) t -> p dt t", p=128)[
                          :, :, chk * MC:(chk + 1) * MC],
                      w=(xch_free[b] or []))
                xt = c.val(xs)
                c.dma("sync", ss, selrow[b][:, :],
                      selT_dram.ap()[e_i:e_i + 1, chk * MC:(chk + 1) * MC],
                      w=(prev.get(("selrow", b)) or []))
                c.t(lambda e, b=b: e.matmul(psB[b][:, :], cm.ones_row_f[:, :],
                                            selrow[b][:, :], start=True,
                                            stop=True),
                    w=[(ss, c.val(ss))] + (prev.get(("psB", b)) or []),
                    inc="tE")
                psb_t = c.val("tE")
                prev[("selrow", b)] = [("tE", psb_t)]
                c.v(lambda e, b=b: e.tensor_copy(selwb[b][:, :], psB[b][:, :]),
                    w=[("tE", psb_t)] + (prev.get(("selwb", b)) or []))
                swb_t = c.val("vE")
                for jt in range(32):
                    hb = jt % 2
                    for dt in range(NDT):
                        w = []
                        if dt == 0:
                            w = list(psH_free[hb] or [])
                            if jt == 0:
                                w += [("mew", w1_rdy), (xs, xt)]
                        c.t(lambda e, jt=jt, dt=dt, hb=hb, b=b: e.matmul(
                            psH[hb][:, :],
                            W1sb[dt][:, jt * 128:(jt + 1) * 128],
                            xch[b][:, dt * MC:(dt + 1) * MC],
                            start=(dt == 0), stop=(dt == NDT - 1)), w=w,
                            inc=("tE" if dt == NDT - 1 else None))
                    ht = c.val("tE")
                    c.a(lambda e, jt=jt, hb=hb: e.activation(
                        Hsb[:, jt * MC:(jt + 1) * MC], psH[hb][:, :], AF.Silu,
                        bias=b1_sb[:, e_i * 32 + jt:e_i * 32 + jt + 1],
                        scale=cm.c(C_ONE)), w=[("tE", ht), ("cs", smt)])
                    psH_free[hb] = [("aE", c.val("aE"))]
                h_all = c.val("aE")
                h_last = c.val("tE")
                xch_free[b] = [("tE", h_last)]
                for ft in range(8):
                    ob = ft % 2
                    for jt in range(32):
                        w = []
                        if jt == 0:
                            w = [("aE", h_all)] + (psO_free[ob] or [])
                            if ft == 0:
                                w += [("mew", w_rdy)]
                        c.t(lambda e, ft=ft, jt=jt, ob=ob: e.matmul(
                            psO[ob][:, :],
                            W2sb[jt // 4][:, (jt % 4) * 1024 + ft * 128:
                                          (jt % 4) * 1024 + (ft + 1) * 128],
                            Hsb[:, jt * MC:(jt + 1) * MC],
                            start=(jt == 0), stop=(jt == 31)), w=w,
                            inc=("tE" if jt == 31 else None))
                    ot = c.val("tE")
                    c.v(lambda e, ob=ob, ft=ft, b=b: e.scalar_tensor_tensor(
                        otile[ob][:, :], psO[ob][:, :],
                        b2_sb[:, e_i * 8 + ft:e_i * 8 + ft + 1],
                        selwb[b][:, :], ALU.add, ALU.mult),
                        w=[("tE", ot), ("vE", swb_t), ("cs", smt)]
                        + (ot_free[ob] or []))
                    ovt = c.val("vE")
                    psO_free[ob] = [("vE", ovt)]
                    ys = f"mey{ft % 2}"
                    c.dma("gpsimd", ys,
                          ymoe.ap()[ft * 128:(ft + 1) * 128,
                                    chk * MC:(chk + 1) * MC],
                          otile[ob][:, :],
                          w=[("vE", ovt)],
                          accum_op=(ALU.bypass if e_i == 0 else ALU.add))
                    ot_free[ob] = [(ys, c.val(ys))]
                prev[("psB", b)] = [("vE", swb_t)]
                prev[("selwb", b)] = [("vE", c.val("vE"))]
                if chk == NMC - 1:
                    h_prev_last = h_last
            o_prev_last = c.val("tE")
        c.bar()

    return logic


def phase_l0ffn_epi(nc, cm, din, src_f32, ymoe, dst_f32, dst_bf, E):
    ep = alloc_epilogue(nc, E, 512)
    ymt = [E(nc.sbuf_tensor(f"me_ym{i}", [128, 512], F32)) for i in range(2)]
    xft = [E(nc.sbuf_tensor(f"me_xf{i}", [128, 512], F32)) for i in range(2)]

    def logic(c):
        state = {}
        buf_free = [None, None]
        for chk in range(DEV.get("epi_chk", T // 512)):
            def make_z(c, dt, ztile, chk=chk):
                b = dt % 2
                ys, xs = f"eym{b}", f"exf{b}"
                c.dma("sync", ys, ymt[b][:, :],
                      ymoe.ap()[dt * 128:(dt + 1) * 128,
                                chk * 512:(chk + 1) * 512],
                      w=(buf_free[b] or []))
                c.dma("sync", xs, xft[b][:, :],
                      src_f32[dt * 128:(dt + 1) * 128,
                              chk * 512:(chk + 1) * 512])
                zf = state.get("z_free") if dt == 0 else None
                c.v(lambda e, dt=dt, b=b: e.tensor_tensor(
                    ztile[:, :], ymt[b][:, :], xft[b][:, :], ALU.add),
                    w=[(ys, c.val(ys)), (xs, c.val(xs))] + (zf or []))
                buf_free[b] = [("vE", c.val("vE"))]
                return [("vE", c.val("vE"))]

            emit_epilogue_chunk(c, cm, ep, chk, make_z, dst_f32, dst_bf, state)
        c.bar()

    return logic



def phase_l1attn_main(nc, cm, din, src_bf, attO, E):
    CC = 128
    NCC = T // CC
    wTsb = [E(nc.sbuf_tensor(f"la_wT{i}", [128, 3 * D], BF16))
            for i in range(NDT)]
    xb = [E(nc.sbuf_tensor(f"la_xb{i}", [128, NDT * CC], BF16))
          for i in range(2)]
    qexp = E(nc.sbuf_tensor("la_qexp", [128, 8 * CC], BF16))
    kexpf = E(nc.sbuf_tensor("la_kexpf", [128, 8 * CC], F32))
    kexpb = E(nc.sbuf_tensor("la_kexpb", [128, 8 * CC], BF16))
    knat = E(nc.sbuf_tensor("la_knat", [128, 1024], BF16))
    vnat = E(nc.sbuf_tensor("la_vnat", [128, 1024], BF16))
    rsum2 = E(nc.sbuf_tensor("la_rsum2", [2, 8 * CC], F32))
    knorm = [E(nc.sbuf_tensor(f"la_kn{i}", [128, CC], F32)) for i in range(2)]
    rkn = [E(nc.sbuf_tensor(f"la_rkn{i}", [128, CC], F32)) for i in range(2)]
    qs1 = [E(nc.sbuf_tensor(f"la_qs1_{i}", [128, CC], F32)) for i in range(2)]
    qsT = E(nc.sbuf_tensor("la_qsT", [128, 8 * CC], BF16))
    carry = E(nc.sbuf_tensor("la_carry", [128, 8], F32))
    Sf = E(nc.sbuf_tensor("la_Sf", [128, 8 * 64], F32))
    Sb = E(nc.sbuf_tensor("la_Sb", [128, 8 * 64], BF16))
    Abf = [E(nc.sbuf_tensor(f"la_A{i}", [128, CC], BF16)) for i in range(2)]
    yT = [E(nc.sbuf_tensor(f"la_yT{i}", [128, 8 * 512], BF16))
          for i in range(2)]
    psP = [E(nc.psum_tensor(f"la_psP{i}", [128, 512], F32)) for i in range(2)]
    psQs = E(nc.psum_tensor("la_psQs", [2, CC], F32))
    psRB = E(nc.psum_tensor("la_psRB", [128, CC], F32))
    psA = E(nc.psum_tensor("la_psA", [128, CC], F32))
    psY = E(nc.psum_tensor("la_psY", [64, CC], F32))
    psSU = E(nc.psum_tensor("la_psSU", [64, 64], F32))

    def logic(c):
        for dt in range(NDT):
            c.dma("gpsimd", "law", wTsb[dt][:, :],
                  din["wT"].ap()[dt * 128:(dt + 1) * 128, :])
        wt = c.val("law")
        c.v(lambda e: e.memset(Sf[:, :], 0.0))
        c.v(lambda e: e.memset(Sb[:, :], 0.0))
        c.v(lambda e: e.memset(carry[:, :], 0.0))
        init_t = c.val("vE")
        st = {"xb": [None, None], "psP": [None, None],
              "psQs": None, "psRB": None, "kn": [None, None],
              "qs1": [None, None], "A": [None, None], "psA": None,
              "psY": None, "psSU": None, "qexp": None, "kexpf": None,
              "kexpb": None, "knat": None, "vnat": None, "qsT": None,
              "Sv": [init_t] * 16, "Sbv": [init_t] * 16, "yT": [None, None]}
        for cc in range(DEV.get("l1a_cc", NCC)):
            b = cc % 2
            yb = (cc // 4) % 2
            xs = f"lax{b}"
            c.dma("sync", xs,
                  xb[b][:, :].rearrange("p (dt c) -> p dt c", c=CC),
                  src_bf.rearrange("(dt p) t -> p dt t", p=128)[
                      :, :, cc * CC:(cc + 1) * CC],
                  w=(st["xb"][b] or []))
            xt = c.val(xs)
            for which in ("q", "k"):
                for et in range(NDT):
                    pb = et % 2
                    ecol = (0 if which == "q" else D) + et * 128
                    for dt in range(NDT):
                        w = []
                        if dt == 0:
                            w = [(xs, xt), ("law", wt)] + (st["psP"][pb] or [])
                        c.t(lambda e, dt=dt, pb=pb, ecol=ecol, b=b:
                            e.matmul(psP[pb][:, 0:CC],
                                     wTsb[dt][:, ecol:ecol + 128],
                                     xb[b][:, dt * CC:(dt + 1) * CC],
                                     start=(dt == 0), stop=(dt == NDT - 1)),
                            w=w, inc=("tE" if dt == NDT - 1 else None))
                    mt = c.val("tE")
                    dst = qexp if which == "q" else kexpf
                    wx = (st["qexp"] if which == "q" else st["kexpf"]) or []
                    c.a(lambda e, et=et, pb=pb, dst=dst: e.activation(
                        dst[:, et * CC:(et + 1) * CC], psP[pb][:, 0:CC],
                        AF.Exp, bias=cm.c(C_ZERO), scale=cm.c(C_QS)),
                        w=[("tE", mt)] + (wx if et < 2 else []))
                    at = c.val("aE")
                    st["psP"][pb] = [("aE", at)]
                    if which == "k":
                        c.v(lambda e, et=et: e.tensor_scalar(
                            kexpf[:, et * CC:(et + 1) * CC],
                            kexpf[:, et * CC:(et + 1) * CC], 1e-6, None,
                            ALU.add),
                            w=[("aE", at)] + ((st["kexpb"] or [])
                                              if et < 2 else []))
                        c.v(lambda e, et=et: e.tensor_copy(
                            kexpb[:, et * CC:(et + 1) * CC],
                            kexpf[:, et * CC:(et + 1) * CC]))
            kexp_done = c.val("vE")
            for which, col0 in (("k", D), ("v", 2 * D)):
                for ech in range(2):
                    pb = ech % 2
                    for dt in range(NDT):
                        w = []
                        if dt == 0:
                            w = [(xs, xt)] + (st["psP"][pb] or [])
                        c.t(lambda e, ech=ech, dt=dt, pb=pb, col0=col0, b=b:
                            e.matmul(psP[pb][:, :],
                                     xb[b][:, dt * CC:(dt + 1) * CC],
                                     wTsb[dt][:, col0 + ech * 512:
                                              col0 + (ech + 1) * 512],
                                     start=(dt == 0), stop=(dt == NDT - 1)),
                            w=w, inc=("tE" if dt == NDT - 1 else None))
                    mt = c.val("tE")
                    if which == "k":
                        c.a(lambda e, ech=ech, pb=pb: e.activation(
                            knat[:, ech * 512:(ech + 1) * 512], psP[pb][:, :],
                            AF.Exp, bias=cm.c(C_ZERO), scale=cm.c(C_QS)),
                            w=[("tE", mt)] + ((st["knat"] or [])
                                              if ech == 0 else []))
                    else:
                        c.a(lambda e, ech=ech, pb=pb: e.activation(
                            vnat[:, ech * 512:(ech + 1) * 512], psP[pb][:, :],
                            AF.Identity, bias=cm.c(C_ZERO),
                            scale=cm.c(C_ONE)),
                            w=[("tE", mt)] + ((st["vnat"] or [])
                                              if ech == 0 else []))
                    st["psP"][pb] = [("aE", c.val("aE"))]
            knv_done = c.val("aE")
            for et in range(NDT):
                kb = et % 2
                c.t(lambda e, et=et: e.matmul(psQs[:, :], cm.ones2[:, :],
                                              qexp[:, et * CC:(et + 1) * CC],
                                              start=True, stop=True),
                    w=[("aE", knv_done)] + (st["psQs"] or []), inc="tE")
                qst = c.val("tE")
                c.v(lambda e, et=et: e.reciprocal(
                    rsum2[:, et * CC:(et + 1) * CC], psQs[:, :]),
                    w=[("tE", qst)])
                st["psQs"] = [("vE", c.val("vE"))]
                rst = c.val("vE")
                c.t(lambda e, et=et: e.matmul(psRB[:, :], cm.sel2T[:, :],
                                              rsum2[:, et * CC:(et + 1) * CC],
                                              start=True, stop=True),
                    w=[("vE", rst)] + (st["psRB"] or []), inc="tE")
                rbt = c.val("tE")
                c.v(lambda e, et=et, kb=kb: e.tensor_tensor_scan(
                    knorm[kb][:, :], kexpf[:, et * CC:(et + 1) * CC],
                    cm.ones_t128[:, :], carry[:, et:et + 1], ALU.add,
                    ALU.mult),
                    w=[("vE", kexp_done)] + (st["kn"][kb] or []))
                c.v(lambda e, et=et, kb=kb: e.tensor_copy(
                    carry[:, et:et + 1], knorm[kb][:, CC - 1:CC]))
                c.v(lambda e, kb=kb: e.reciprocal(rkn[kb][:, :],
                                                  knorm[kb][:, :]))
                c.v(lambda e, et=et, kb=kb: e.tensor_tensor(
                    qs1[kb][:, :], qexp[:, et * CC:(et + 1) * CC],
                    psRB[:, :], ALU.mult),
                    w=[("tE", rbt)] + (st["qs1"][kb] or []))
                st["psRB"] = [("vE", c.val("vE"))]
                c.v(lambda e, et=et, kb=kb: e.tensor_tensor(
                    qsT[:, et * CC:(et + 1) * CC], qs1[kb][:, :],
                    rkn[kb][:, :], ALU.mult),
                    w=(st["qsT"] or []) if et == 0 else [])
                st["kn"][kb] = [("vE", c.val("vE"))]
                st["qs1"][kb] = st["kn"][kb]
            qs_done = c.val("vE")
            for h in range(16):
                et, hh = h // 2, h % 2
                ab = h % 2
                lo = hh * 64
                c.t(lambda e, et=et, lo=lo: e.matmul(
                    psA[:, :], kexpb[lo:lo + 64, et * CC:(et + 1) * CC],
                    qsT[lo:lo + 64, et * CC:(et + 1) * CC],
                    start=True, stop=True),
                    w=[("vE", qs_done)] + (st["psA"] or []), inc="tE")
                at_ = c.val("tE")
                c.v(lambda e, ab=ab: e.tensor_tensor(Abf[ab][:, :], psA[:, :],
                                                     cm.tril_b[:, :],
                                                     ALU.mult),
                    w=[("tE", at_)] + (st["A"][ab] or []))
                st["psA"] = [("vE", c.val("vE"))]
                amt = c.val("vE")
                c.t(lambda e, h=h, ab=ab: e.matmul(
                    psY[:, :], vnat[:, h * 64:(h + 1) * 64], Abf[ab][:, :],
                    start=True, stop=False),
                    w=[("vE", amt), ("aE", knv_done)] + (st["psY"] or []))
                c.t(lambda e, h=h, lo=lo: e.matmul(
                    psY[:, :],
                    Sb[lo:lo + 64, (h // 2) * 64:(h // 2) * 64 + 64],
                    qsT[lo:lo + 64, (h // 2) * CC:(h // 2) * CC + CC],
                    start=False, stop=True),
                    w=[("vE", st["Sbv"][h])], inc="tE")
                yt_ = c.val("tE")
                st["A"][ab] = [("tE", yt_)]
                c.a(lambda e, h=h, hh=hh, cc=cc, yb=yb: e.activation(
                    yT[yb][hh * 64:hh * 64 + 64,
                           (h // 2) * 512 + (cc % 4) * CC:
                           (h // 2) * 512 + (cc % 4) * CC + CC],
                    psY[:, :], AF.Identity,
                    bias=cm.consts[0:64, C_ZERO:C_ZERO + 1],
                    scale=cm.consts[0:64, C_ONE:C_ONE + 1]),
                    w=[("tE", yt_)] + (st["yT"][yb] or []))
                st["psY"] = [("aE", c.val("aE"))]
                c.t(lambda e, h=h: e.matmul(
                    psSU[:, :], knat[:, h * 64:(h + 1) * 64],
                    vnat[:, h * 64:(h + 1) * 64], start=True, stop=True),
                    w=(st["psSU"] or []) + [("aE", knv_done)], inc="tE")
                sut = c.val("tE")
                c.v(lambda e, h=h, lo=lo: e.tensor_tensor(
                    Sf[lo:lo + 64, (h // 2) * 64:(h // 2) * 64 + 64],
                    Sf[lo:lo + 64, (h // 2) * 64:(h // 2) * 64 + 64],
                    psSU[:, :], ALU.add),
                    w=[("tE", sut), ("vE", st["Sv"][h])])
                st["Sv"][h] = c.val("vE")
                c.v(lambda e, h=h, lo=lo: e.tensor_copy(
                    Sb[lo:lo + 64, (h // 2) * 64:(h // 2) * 64 + 64],
                    Sf[lo:lo + 64, (h // 2) * 64:(h // 2) * 64 + 64]))
                st["Sbv"][h] = c.val("vE")
                st["psSU"] = [("vE", st["Sv"][h])]
            end_t = c.val("tE")
            st["xb"][b] = [("tE", end_t)]
            st["qexp"] = [("tE", end_t), ("vE", qs_done)]
            st["kexpf"] = [("vE", qs_done)]
            st["kexpb"] = [("tE", end_t)]
            st["knat"] = [("tE", end_t)]
            st["vnat"] = [("tE", end_t)]
            st["qsT"] = [("tE", end_t)]
            if cc % 4 == 3:
                ys = f"lay{yb}"
                ya = c.val("aE")
                c.dma("sync", ys,
                      attO.rearrange("(et p) t -> p et t", p=128)[
                          :, :, (cc // 4) * 512:(cc // 4) * 512 + 512],
                      yT[yb][:, :].rearrange("p (et t) -> p et t", t=512),
                      w=[("aE", ya)])
                st["yT"][yb] = [(ys, c.val(ys))]
        c.bar()

    return logic




MPITCH = 4224
MSLAB = 17340672


def phase_l0attn(nc, cm, din, src_f32, src_bf, mscr, attO, E):
    RT = E(nc.sbuf_tensor("at_rt", [128, 8192], BF16))
    QT = E(nc.sbuf_tensor("at_qt", [128, T], BF16))
    KT = E(nc.sbuf_tensor("at_kt", [128, T], BF16))
    Vx = E(nc.sbuf_tensor("at_vx", [128, 64 * 66], BF16))
    wq = [E(nc.sbuf_tensor(f"at_wq{i}", [128, 8 * 128], BF16)) for i in range(2)]
    wk = [E(nc.sbuf_tensor(f"at_wk{i}", [128, 8 * 128], BF16)) for i in range(2)]
    wv = [E(nc.sbuf_tensor(f"at_wv{i}", [128, 8 * 128], BF16)) for i in range(2)]
    xc = [E(nc.sbuf_tensor(f"at_xc{i}", [128, NDT * 512], BF16)) for i in range(2)]
    bq = E(nc.sbuf_tensor("at_bq", [1, D], F32))
    bk = E(nc.sbuf_tensor("at_bk", [1, D], F32))
    bv = E(nc.sbuf_tensor("at_bv", [1, D], F32))
    Msb = [E(nc.sbuf_tensor(f"at_m{i}", [128, MPITCH], BF16)) for i in range(4)]
    bstrip = [E(nc.sbuf_tensor(f"at_bs{i}", [128, 4 * T], BF16))
              for i in range(2)]
    expS = [E(nc.sbuf_tensor(f"at_ex{i}", [128, 512], BF16)) for i in range(4)]
    rsum = E(nc.sbuf_tensor("at_rs", [1, 512], F32))
    rb = E(nc.sbuf_tensor("at_rb", [64, 512], F32))
    ao = [E(nc.sbuf_tensor(f"at_ao{i}", [64, 512], BF16)) for i in range(2)]
    psA = [E(nc.psum_tensor(f"at_psA{i}", [128, 512], F32)) for i in range(2)]
    psV = [E(nc.psum_tensor("at_psV0", [128, 128], F32))]
    psS = [E(nc.psum_tensor(f"at_psS{i}", [128, 512], F32)) for i in range(3)]
    psO = E(nc.psum_tensor("at_psO", [65, 512], F32))
    psR = E(nc.psum_tensor("at_psR", [64, 512], F32))

    def logic(c):
        c.dma("gpsimd", "arl", RT[0:64, 0:8191], din["RT"].ap())
        c.dma("gpsimd", "arl", RT[64:128, 0:8191], din["RT"].ap())
        c.dma("sync", "abl", bq[:, :], din["bqkvo"].ap()[0:1, :])
        c.dma("sync", "abl", bk[:, :], din["bqkvo"].ap()[1:2, :])
        c.dma("sync", "abl", bv[:, :], din["bqkvo"].ap()[2:3, :])
        rt_t = c.val("arl")
        bl_t = c.val("abl")
        c.v(lambda e: e.memset(RT[:, 8191:8192], 0.0))
        c.v(lambda e: e.memset(
            Vx[:, :].rearrange("p (g cc) -> p g cc", cc=66)[:, :, 64:65], 1.0))
        c.v(lambda e: e.memset(
            Vx[:, :].rearrange("p (g cc) -> p g cc", cc=66)[:, :, 65:66], 0.0))
        ones_t = c.val("vE")
        ringT = psA + psS
        st = {"w": [None, None], "xc": [None, None],
              "bank": [None] * 5, "psV": [None], "QT": None,
              "KT": None, "Vx": None, "Msb": [None] * 4,
              "bstrip": [None, None], "expS": [None] * 4,
              "psO": None, "psR": None, "rb": None, "ao": [None, None],
              "slab_free": [None, None, None, None]}
        for et in range(DEV.get("att_et", 8)):
            wb = et % 2
            for wbuf, wT in ((wq[wb], "WqT"), (wk[wb], "WkT"), (wv[wb], "WvT")):
                c.dma("gpsimd", f"aw{wb}",
                      wbuf[:, :].rearrange("p (dt e) -> p dt e", e=128),
                      din[wT].ap().rearrange("(dt p) e -> p dt e", p=128)[
                          :, :, et * 128:(et + 1) * 128],
                      w=(st["w"][wb] or []))
            w_t = c.val(f"aw{wb}")
            for tc in range(8):
                xb = tc % 2
                c.dma("sync", f"ax{xb}",
                      xc[xb][:, :].rearrange("p (dt cc) -> p dt cc", cc=512),
                      src_bf.rearrange("(dt p) t -> p dt t", p=128)[
                          :, :, tc * 512:(tc + 1) * 512],
                      w=(st["xc"][xb] or []))
                x_t = c.val(f"ax{xb}")
                for which, wbuf, brow, dst, scl in (
                        ("q", wq[wb], bq, QT, C_ONE), ("k", wk[wb], bk, KT, C_QS)):
                    pa = (tc * 2 + (0 if which == "q" else 1)) % 2
                    for dt in range(NDT):
                        w = []
                        if dt == 0:
                            w = [(f"ax{xb}", x_t), (f"aw{wb}", w_t),
                                 ("abl", bl_t)] + (st["bank"][pa] or [])
                        c.t(lambda e, dt=dt, pa=pa, xb=xb, wbuf=wbuf: e.matmul(
                            psA[pa][:, :], wbuf[:, dt * 128:(dt + 1) * 128],
                            xc[xb][:, dt * 512:(dt + 1) * 512],
                            start=(dt == 0), stop=False), w=w)
                    c.t(lambda e, pa=pa, brow=brow, et=et: e.matmul(
                        psA[pa][:, :], brow[:, et * 128:(et + 1) * 128],
                        cm.ones_row_f512[:, :], start=False, stop=True),
                        inc="tE")
                    mt = c.val("tE")
                    c.a(lambda e, pa=pa, tc=tc, dst=dst, scl=scl: e.activation(
                        dst[:, tc * 512:(tc + 1) * 512], psA[pa][:, :],
                        AF.Identity, bias=cm.c(C_ZERO), scale=cm.c(scl)),
                        w=[("tE", mt)] + ((st["QT"] or []) if tc == 0 and
                                          which == "q" else []))
                    st["bank"][pa] = [("aE", c.val("aE"))]
                for ttl in range(4):
                    for dt in range(NDT):
                        w = []
                        if dt == 0:
                            w = [(f"ax{xb}", x_t)] + (st["psV"][0] or [])
                            if tc == 0 and ttl == 0:
                                w += (st["Vx"] or []) + [("vE", ones_t)]
                        c.t(lambda e, dt=dt, xb=xb, ttl=ttl, wb=wb:
                            e.matmul(psV[0][:, :],
                                     xc[xb][:, dt * 512 + ttl * 128:
                                            dt * 512 + (ttl + 1) * 128],
                                     wv[wb][:, dt * 128:(dt + 1) * 128],
                                     start=(dt == 0), stop=False), w=w)
                    c.t(lambda e, et=et: e.matmul(
                        psV[0][:, :], cm.ones_row_f[:, :],
                        bv[:, et * 128:(et + 1) * 128], start=False,
                        stop=True), inc="tE")
                    vt = c.val("tE")
                    jt = tc * 4 + ttl
                    for hh in range(2):
                        c.a(lambda e, jt=jt, hh=hh: e.activation(
                            Vx[:, (jt * 2 + hh) * 66:(jt * 2 + hh) * 66 + 64],
                            psV[0][:, hh * 64:(hh + 1) * 64], AF.Identity,
                            bias=cm.consts[0:128, C_ZERO:C_ZERO + 1],
                            scale=cm.consts[0:128, C_ONE:C_ONE + 1]),
                            w=[("tE", vt)])
                    st["psV"][0] = [("aE", c.val("aE"))]
                st["xc"][xb] = [("tE", c.val("tE"))]
            proj_t = c.val("aE")
            st["w"][wb] = [("tE", c.val("tE"))]
            for hh in range(2):
                slab = (et % 2) * 2 + hh
                ms = f"ams{slab}"
                lo = hh * 64
                store_marks = []
                kcg = 0
                for it in range(32) if not DEV.get("att_skipM") else []:
                    mb = it % 4
                    last_v, last_a = None, None
                    for kc in range(9):
                        pa = kcg % 5
                        kcg += 1
                        ncol = 512 if kc < 8 else 128
                        cbase = 3968 - it * 128 + kc * 512
                        w = [("aE", proj_t), ("arl", rt_t)]
                        if kc == 0:
                            w += (st["Msb"][mb] or [])
                        w += (st["bank"][pa] or [])
                        c.t(lambda e, it=it, kc=kc, pa=pa, lo=lo, cbase=cbase,
                            ncol=ncol: e.matmul(
                                ringT[pa][:, 0:ncol],
                                QT[lo:lo + 64, it * 128:(it + 1) * 128],
                                RT[lo:lo + 64, cbase:cbase + ncol],
                                start=True, stop=True), w=w, inc="tE")
                        mt = c.val("tE")
                        if kc % 2 == 0:
                            c.v(lambda e, kc=kc, pa=pa, mb=mb, ncol=ncol:
                                e.tensor_copy(
                                    Msb[mb][:, kc * 512:kc * 512 + ncol],
                                    ringT[pa][:, 0:ncol]),
                                w=[("tE", mt)])
                            last_v = c.val("vE")
                            st["bank"][pa] = [("vE", last_v)]
                        else:
                            c.a(lambda e, kc=kc, pa=pa, mb=mb, ncol=ncol:
                                e.activation(
                                    Msb[mb][:, kc * 512:kc * 512 + ncol],
                                    ringT[pa][:, 0:ncol], AF.Identity,
                                    bias=cm.c(C_ZERO), scale=cm.c(C_ONE)),
                                w=[("tE", mt)])
                            last_a = c.val("aE")
                            st["bank"][pa] = [("aE", last_a)]
                    mv = ([("vE", last_v)] if last_v else []) + \
                        ([("aE", last_a)] if last_a else [])
                    c.dma("sync", ms,
                          AP(mscr, slab * MSLAB + it * 128 * (MPITCH - 1),
                             [[MPITCH, 128], [1, MPITCH]]),
                          Msb[mb][:, :],
                          w=mv + (st["slab_free"][slab] or []))
                    st["Msb"][mb] = [(ms, c.val(ms))]
                    if it % 4 == 3:
                        store_marks.append((ms, c.val(ms)))
                def load_strip(c, ic, slab=slab):
                    sb_ = ic % 2
                    ss_ = f"abs{sb_}"
                    for s in range(4):
                        it = ic * 4 + s
                        w = []
                        if s == 0:
                            w = ([store_marks[ic]]
                                 if not (DEV.get("att_nogate")
                                         or DEV.get("att_skipM"))
                                 else []) + (st["bstrip"][sb_] or [])
                        c.dma("sync", ss_,
                              bstrip[sb_][:, s * T:(s + 1) * T],
                              AP(mscr,
                                 slab * MSLAB + it * 128 * (MPITCH - 1) + 127,
                                 [[MPITCH - 1, 128], [1, T]]), w=w)
                    return c.val(ss_)

                strip_ts = {}
                if not DEV.get("att_skipA"):
                    strip_ts[0] = load_strip(c, 0)
                for ic in range(8) if not DEV.get("att_skipA") else []:
                    sb_ = ic % 2
                    ss_ = f"abs{sb_}"
                    if ic + 1 < 8:
                        strip_ts[ic + 1] = load_strip(c, ic + 1)
                    strip_t = strip_ts[ic]
                    pend = []

                    def emit_av(c, pjt, peb, pe_t, hh=hh):
                        w = [("aE", pe_t)]
                        if pjt == 0:
                            w += (st["psO"] or [])
                        c.t(lambda e, pjt=pjt, peb=peb, hh=hh: e.matmul(
                            psO[:, :],
                            Vx[:, (pjt * 2 + hh) * 66:
                               (pjt * 2 + hh) * 66 + 65],
                            expS[peb][:, :], start=(pjt == 0),
                            stop=(pjt == 31)), w=w, inc="tE")
                        st["expS"][peb] = [("tE", c.val("tE"))]

                    for jt in range(32):
                        ps = jt % 3
                        w = [(ss_, strip_t)] + (st["bank"][2 + ps] or [])
                        for s in range(4):
                            c.t(lambda e, s=s, ps=ps, sb_=sb_, jt=jt: e.matmul(
                                psS[ps][:, s * 128:(s + 1) * 128],
                                bstrip[sb_][:, s * T + jt * 128:
                                            s * T + (jt + 1) * 128],
                                cm.ident_b[:, :], start=(s == 0), stop=False),
                                w=(w if s == 0 else []))
                        c.t(lambda e, jt=jt, ic=ic, ps=ps, lo=lo: e.matmul(
                            psS[ps][:, :],
                            KT[lo:lo + 64, jt * 128:(jt + 1) * 128],
                            QT[lo:lo + 64, ic * 512:(ic + 1) * 512],
                            start=False, stop=True), inc="tE")
                        s_t = c.val("tE")
                        eb = jt % 4
                        c.a(lambda e, ps=ps, eb=eb: e.activation(
                            expS[eb][:, :], psS[ps][:, :], AF.Exp,
                            bias=cm.c(C_EXPB), scale=cm.c(C_ONE)),
                            w=[("tE", s_t)] + (st["expS"][eb] or []))
                        e_t = c.val("aE")
                        st["bank"][2 + ps] = [("aE", e_t)]
                        pend.append((jt, eb, e_t))
                        if len(pend) > 2:
                            emit_av(c, *pend.pop(0))
                    for p_ in pend:
                        emit_av(c, *p_)
                    o_t = c.val("tE")
                    st["bstrip"][sb_] = [("tE", o_t)]
                    c.v(lambda e: e.reciprocal(rsum[:, :], psO[64:65, :]),
                        w=[("tE", o_t)] + (st["rb"] or []))
                    r_t = c.val("vE")
                    c.t(lambda e: e.matmul(psR[:, :], cm.ones_row_f[0:1, 0:64],
                                           rsum[:, :], start=True, stop=True),
                        w=[("vE", r_t)] + (st["psR"] or []), inc="tE")
                    rb_t = c.val("tE")
                    c.v(lambda e: e.tensor_copy(rb[:, :], psR[:, :]),
                        w=[("tE", rb_t)])
                    rba_t = c.val("vE")
                    st["psR"] = [("vE", rba_t)]
                    ab = ic % 2
                    c.v(lambda e, ab=ab: e.tensor_tensor(
                        ao[ab][:, :], psO[0:64, :], rb[:, :], ALU.mult),
                        w=[("vE", rba_t)] + (st["ao"][ab] or []))
                    ao_t = c.val("vE")
                    st["psO"] = [("vE", ao_t)]
                    st["rb"] = [("vE", ao_t)]
                    c.dma("sync", f"aao{ab}",
                          attO[et * 128 + lo:et * 128 + lo + 64,
                               ic * 512:(ic + 1) * 512],
                          ao[ab][:, :], w=[("vE", ao_t)])
                    st["ao"][ab] = [(f"aao{ab}", c.val(f"aao{ab}"))]
                st["slab_free"][slab] = [("abs0", c.val("abs0")),
                                         ("abs1", c.val("abs1"))]
            st["QT"] = [("tE", c.val("tE"))]
            st["KT"] = st["QT"]
            st["Vx"] = st["QT"]
        c.bar()

    return logic



def phase_l1ffn_peer(nc, cm, din, src_f32, src_bf, tdu, expdT, E):
    xbt = E(nc.sbuf_tensor("pr_xbt", [128, NDT * 128], BF16))
    xft = [E(nc.sbuf_tensor(f"pr_xf{i}", [128, 128], F32)) for i in range(2)]
    y3n = E(nc.sbuf_tensor("pr_y3n", [128, 1024], BF16))
    Wq2 = E(nc.sbuf_tensor("pr_wq2", [128, NDT * 512], F32))
    keys = E(nc.sbuf_tensor("pr_keys", [128, 512], F32))
    qT = E(nc.sbuf_tensor("pr_qT", [128, 4 * 128], F32))
    sim = E(nc.sbuf_tensor("pr_sim", [128, 8 * 64], F32))
    sx = E(nc.sbuf_tensor("pr_sx", [128, 64], F32))
    ixf = E(nc.sbuf_tensor("pr_ixf", [128, 64], F32))
    idxu = E(nc.sbuf_tensor("pr_idxu", [128, 8], U32))
    allsc = E(nc.sbuf_tensor("pr_alls", [128, 64], F32))
    exy = E(nc.sbuf_tensor("pr_exy", [128, 64], F32))
    ix64 = E(nc.sbuf_tensor("pr_ix64", [128, 8], F32))
    s8f = E(nc.sbuf_tensor("pr_s8f", [128, 8], F32))
    pku = E(nc.sbuf_tensor("pr_pku", [128, 8], U32))
    pkf = E(nc.sbuf_tensor("pr_pkf", [128, 8], F32))
    oh64 = E(nc.sbuf_tensor("pr_oh64", [128, 64], F32))
    junk64 = E(nc.sbuf_tensor("pr_junk64", [128, 64], F32))
    ef8 = E(nc.sbuf_tensor("pr_ef8", [128, 8], F32))
    efi = E(nc.sbuf_tensor("pr_efi", [128, 8], I32))
    es8 = E(nc.sbuf_tensor("pr_es8", [128, 8], F32))
    p8 = E(nc.sbuf_tensor("pr_p8", [128, 8], F32))
    ssum = E(nc.sbuf_tensor("pr_ssum", [128, 1], F32))
    rs = E(nc.sbuf_tensor("pr_rs", [128, 1], F32))
    sd8 = E(nc.sbuf_tensor("pr_sd8", [128, 8], F32))
    w8 = E(nc.sbuf_tensor("pr_w8", [128, 8], F32))
    gjunk = E(nc.sbuf_tensor("pr_gjunk", [128, 1024], BF16))
    G = [E(nc.sbuf_tensor(f"pr_G{i}", [128, 2048], BF16)) for i in range(3)]
    expacc = E(nc.sbuf_tensor("pr_eacc", [128, 1024], F32))
    etile = [E(nc.sbuf_tensor(f"pr_et{i}", [128, 128], F32)) for i in range(2)]
    psB2 = [E(nc.psum_tensor(f"pr_psb{i}", [128, 128], BF16)) for i in range(2)]
    psF2 = [E(nc.psum_tensor(f"pr_psf{i}", [128, 128], F32)) for i in range(2)]

    def logic(c):
        c.dma("gpsimd", "prw",
              Wq2[:, :].rearrange("p (dt e) -> p dt e", e=512),
              din["Wq2T"].ap().rearrange("(dt p) e -> p dt e", p=128))
        c.dma("sync", "prk", keys[0:64, :].rearrange("p (g k) -> p g k", k=64),
              din["keysT"].ap().rearrange("two h n k -> n (two h) k"))
        c.dma("sync", "prk", keys[64:128, :].rearrange("p (g k) -> p g k", k=64),
              din["keysT"].ap().rearrange("two h n k -> n (two h) k"))
        wq_t = c.val("prw")
        k_t = c.val("prk")
        st = {"xbt": None, "xf": [None, None], "psB": [None, None],
              "psF": [None, None], "y3n": None, "qT": None, "sim": None,
              "G": [None, None, None], "et": [None, None], "eacc": None}
        for tt in range(DEV.get("peer_tt", 32)):
            t0 = tt * 128
            c.dma("sync", "prx",
                  xbt[:, :].rearrange("p (dt cc) -> p dt cc", cc=128),
                  src_bf.rearrange("(dt p) t -> p dt t", p=128)[
                      :, :, t0:t0 + 128], w=(st["xbt"] or []))
            xbt_t = c.val("prx")
            for dt in range(NDT):
                pb = dt % 2
                c.t(lambda e, dt=dt, pb=pb: e.transpose(
                    psB2[pb][:, :], xbt[:, dt * 128:(dt + 1) * 128],
                    cm.ident_b[:, :]),
                    w=[("prx", xbt_t)] + (st["psB"][pb] or []), inc="tE")
                trt = c.val("tE")
                c.a(lambda e, dt=dt, pb=pb: e.activation(
                    y3n[:, dt * 128:(dt + 1) * 128], psB2[pb][:, :],
                    AF.Identity, bias=cm.c(C_ZERO), scale=cm.c(C_ONE)),
                    w=[("tE", trt)] + ((st["y3n"] or []) if dt < 2 else []))
                st["psB"][pb] = [("aE", c.val("aE"))]
            y3_t = c.val("aE")
            for et in range(4):
                pb = et % 2
                for dt in range(NDT):
                    b = dt % 2
                    c.dma("sync", f"prf{b}", xft[b][:, :],
                          src_f32[dt * 128:(dt + 1) * 128, t0:t0 + 128],
                          w=(st["xf"][b] or []))
                    w = [(f"prf{b}", c.val(f"prf{b}")), ("prw", wq_t)]
                    if dt == 0:
                        w += (st["psF"][pb] or [])
                    c.t(lambda e, et=et, dt=dt, pb=pb, b=b: e.matmul(
                        psF2[pb][:, :],
                        Wq2[:, dt * 512 + et * 128:dt * 512 + (et + 1) * 128],
                        xft[b][:, :], start=(dt == 0), stop=(dt == NDT - 1)),
                        w=w, inc=("tE" if dt == NDT - 1 else None))
                    st["xf"][b] = [("tE", c.val("tE"))]
                qt_ = c.val("tE")
                c.v(lambda e, et=et, pb=pb: e.tensor_copy(
                    qT[:, et * 128:(et + 1) * 128], psF2[pb][:, :]),
                    w=[("tE", qt_)] + ((st["qT"] or []) if et < 2 else []))
                st["psF"][pb] = [("vE", c.val("vE"))]
            qT_t = c.val("vE")
            for ph in range(8):
                pb = ph % 2
                lo = (ph % 2) * 64
                etq = ph // 2
                c.t(lambda e, ph=ph, pb=pb, lo=lo, etq=etq: e.matmul(
                    psF2[pb][:, 0:64],
                    qT[lo:lo + 64, etq * 128:(etq + 1) * 128],
                    keys[lo:lo + 64, ph * 64:(ph + 1) * 64],
                    start=True, stop=True),
                    w=[("vE", qT_t), ("prk", k_t)] + (st["psF"][pb] or []),
                    inc="tE")
                smt = c.val("tE")
                c.v(lambda e, ph=ph, pb=pb: e.tensor_copy(
                    sim[:, ph * 64:(ph + 1) * 64], psF2[pb][:, 0:64]),
                    w=[("tE", smt)] + ((st["sim"] or []) if ph < 2 else []))
                st["psF"][pb] = [("vE", c.val("vE"))]
                c.v(lambda e, ph=ph: e.max(sx[:, ph * 8:(ph + 1) * 8],
                                           sim[:, ph * 64:(ph + 1) * 64]))
                c.v(lambda e, ph=ph: e.max_index(idxu[:, :],
                                                 sx[:, ph * 8:(ph + 1) * 8],
                                                 sim[:, ph * 64:(ph + 1) * 64]))
                c.v(lambda e, ph=ph: e.tensor_copy(ixf[:, ph * 8:(ph + 1) * 8],
                                                   idxu[:, :]))
            st["sim"] = None
            st["qT"] = None
            eacc_w = (st["eacc"] or [])
            c.v(lambda e: e.memset(expacc[:, :], 0.0), w=eacc_w)
            for h in range(4):
                for a in range(8):
                    c.v(lambda e, h=h, a=a: e.tensor_scalar(
                        allsc[:, a * 8:(a + 1) * 8],
                        sx[:, (4 + h) * 8:(5 + h) * 8],
                        sx[:, h * 8 + a:h * 8 + a + 1], None, ALU.add))
                c.v(lambda e, h=h: e.tensor_scalar(
                    ix64[:, :], ixf[:, h * 8:(h + 1) * 8], 64.0, None,
                    ALU.mult))
                for a in range(8):
                    c.v(lambda e, h=h, a=a: e.tensor_scalar(
                        exy[:, a * 8:(a + 1) * 8],
                        ixf[:, (4 + h) * 8:(5 + h) * 8],
                        ix64[:, a:a + 1], None, ALU.add))
                c.v(lambda e: e.max(s8f[:, :], allsc[:, :]))
                c.v(lambda e: e.max_index(pku[:, :], s8f[:, :], allsc[:, :]))
                c.v(lambda e: e.tensor_copy(pkf[:, :], pku[:, :]))
                c.v(lambda e: e.tensor_scalar(es8[:, :], s8f[:, :],
                                              s8f[:, 0:1], None, ALU.subtract))
                ev = c.val("vE")
                c.a(lambda e: e.activation(es8[:, :], es8[:, :], AF.Exp,
                                           bias=cm.c(C_ZERO),
                                           scale=cm.c(C_ONE)), w=[("vE", ev)])
                ea = c.val("aE")
                c.v(lambda e: e.reduce_sum(ssum[:, :], es8[:, :], axis=AXX),
                    w=[("aE", ea)])
                c.v(lambda e: e.reciprocal(rs[:, :], ssum[:, :]))
                c.v(lambda e: e.tensor_scalar(p8[:, :], es8[:, :], rs[:, 0:1],
                                              None, ALU.mult))
                for k in range(8):
                    c.v(lambda e, k=k: e.tensor_scalar(
                        oh64[:, :], cm.iota64[:, :], pkf[:, k:k + 1], None,
                        ALU.is_equal))
                    c.v(lambda e, k=k: e.scalar_tensor_tensor(
                        junk64[:, :], oh64[:, :], 1.0, exy[:, :],
                        ALU.mult, ALU.mult, accum_out=ef8[:, k:k + 1]))
                c.v(lambda e: e.tensor_copy(efi[:, :], ef8[:, :]))
                ef_t = c.val("vE")
                for s in range(8):
                    gb = s % 3
                    c.idma("prg", G[gb][:, :], tdu.ap(),
                           IndirectOffsetOnAxis(efi[:, s:s + 1], 0),
                           w=[("vE", ef_t)] + (st["G"][gb] or []))
                    g_t = c.val("prg")
                    c.v(lambda e, s=s, gb=gb: e.scalar_tensor_tensor(
                        gjunk[:, :], G[gb][:, 0:1024], 1.0, y3n[:, :],
                        ALU.mult, ALU.mult, accum_out=sd8[:, s:s + 1]),
                        w=[("prg", g_t), ("aE", y3_t)])
                    c.v(lambda e, s=s: e.tensor_tensor(
                        w8[:, s:s + 1], sd8[:, s:s + 1], p8[:, s:s + 1],
                        ALU.mult))
                    wv = c.val("vE")
                    c.a(lambda e, s=s: e.activation(w8[:, s:s + 1],
                                                    w8[:, s:s + 1], AF.Silu,
                                                    bias=cm.c(C_ZERO),
                                                    scale=cm.c(C_ONE)),
                        w=[("vE", wv)])
                    wa = c.val("aE")
                    c.v(lambda e, s=s, gb=gb: e.scalar_tensor_tensor(
                        expacc[:, :], G[gb][:, 1024:2048], w8[:, s:s + 1],
                        expacc[:, :], ALU.mult, ALU.add), w=[("aE", wa)])
                    st["G"][gb] = [("vE", c.val("vE"))]
            eacc_t = c.val("vE")
            for dt in range(NDT):
                pb = dt % 2
                eb = dt % 2
                c.t(lambda e, dt=dt, pb=pb: e.transpose(
                    psF2[pb][:, :], expacc[:, dt * 128:(dt + 1) * 128],
                    cm.ident_f[:, :]),
                    w=[("vE", eacc_t)] + (st["psF"][pb] or []), inc="tE")
                trt = c.val("tE")
                c.v(lambda e, pb=pb, eb=eb: e.tensor_copy(etile[eb][:, :],
                                                          psF2[pb][:, :]),
                    w=[("tE", trt)] + (st["et"][eb] or []))
                ev_ = c.val("vE")
                st["psF"][pb] = [("vE", ev_)]
                c.dma("sync", f"pre{eb}",
                      expdT[dt * 128:(dt + 1) * 128, t0:t0 + 128],
                      etile[eb][:, :], w=[("vE", ev_)])
                st["et"][eb] = [(f"pre{eb}", c.val(f"pre{eb}"))]
            st["eacc"] = [("tE", c.val("tE"))]
            st["xbt"] = [("tE", c.val("tE"))]
            st["y3n"] = [("vE", c.val("vE"))]
        c.bar()

    return logic



def phase_l1ffn_dense(nc, cm, din, src_f32, src_bf, expdT, dst_f32, dst_bf, E):
    ep = alloc_epilogue(nc, E, 512)
    Wdn = [E(nc.sbuf_tensor(f"du_wd{i}", [128, 4 * 1024], BF16))
           for i in range(8)]
    Wup = [E(nc.sbuf_tensor(f"du_wu{i}", [128, 8 * 1024], BF16))
           for i in range(2)]
    Usb = E(nc.sbuf_tensor("du_u", [128, 32 * 512], BF16))
    xb = E(nc.sbuf_tensor("du_xb", [128, NDT * 512], BF16))
    xft = [E(nc.sbuf_tensor(f"du_xf{i}", [128, 512], F32)) for i in range(2)]
    etl = [E(nc.sbuf_tensor(f"du_et{i}", [128, 512], F32)) for i in range(2)]
    psU = [E(nc.psum_tensor(f"du_psu{i}", [128, 512], F32)) for i in range(2)]
    psP = [E(nc.psum_tensor(f"du_psp{i}", [128, 512], F32)) for i in range(2)]

    def logic(c):
        for g in range(8):
            c.dma("gpsimd", "duwd",
                  Wdn[g][:, :].rearrange("p (it d) -> p it d", d=1024),
                  din["WdnT"].ap().rearrange("(it p) d -> p it d", p=128)[
                      :, 4 * g:4 * g + 4, :])
        wd_t = c.val("duwd")
        st = {"xb": None, "xf": [None, None], "et": [None, None],
              "psU": [None, None], "psP": [None, None], "U": None,
              "wu": [None, None]}
        state = {}
        for chk in range(T // 512):
            c0 = chk * 512
            c.dma("sync", "dux",
                  xb[:, :].rearrange("p (dt cc) -> p dt cc", cc=512),
                  src_bf.rearrange("(dt p) t -> p dt t", p=128)[
                      :, :, c0:c0 + 512], w=(st["xb"] or []))
            xb_t = c.val("dux")
            uw = (st["U"] or [])
            for q in range(4):
                wb = q % 2
                ws = f"duw{wb}"
                c.dma("gpsimd", ws,
                      Wup[wb][:, :].rearrange("p (dt i) -> p dt i", i=1024),
                      din["WupT"].ap().rearrange("(dt p) i -> p dt i", p=128)[
                          :, :, q * 1024:(q + 1) * 1024],
                      w=(st["wu"][wb] or []))
                w_t = c.val(ws)
                for itl in range(8):
                    ub = itl % 2
                    for dt in range(NDT):
                        w = []
                        if dt == 0:
                            w = [(ws, w_t), ("dux", xb_t),
                                 ("aE", c.val("aE"))] + \
                                (uw if q == 0 and itl == 0 else [])
                        c.t(lambda e, q=q, itl=itl, dt=dt, ub=ub, wb=wb:
                            e.matmul(psU[ub][:, :],
                                     Wup[wb][:, dt * 1024 + itl * 128:
                                             dt * 1024 + (itl + 1) * 128],
                                     xb[:, dt * 512:(dt + 1) * 512],
                                     start=(dt == 0), stop=(dt == NDT - 1)),
                            w=w, inc=("tE" if dt == NDT - 1 else None))
                    ut = c.val("tE")
                    c.a(lambda e, q=q, itl=itl, ub=ub: e.activation(
                        Usb[:, (8 * q + itl) * 512:(8 * q + itl + 1) * 512],
                        psU[ub][:, :], AF.Silu, bias=cm.c(C_ZERO),
                        scale=cm.c(C_ONE)), w=[("tE", ut)])
                st["wu"][wb] = [("tE", c.val("tE"))]
            u_all = c.val("aE")

            def make_z(c, dt, ztile, chk=chk, u_all=u_all):
                pb = dt % 2
                for it in range(32):
                    w = []
                    if it == 0:
                        w = [("aE", u_all), ("duwd", wd_t)] + \
                            (st["psP"][pb] or [])
                    c.t(lambda e, dt=dt, it=it, pb=pb: e.matmul(
                        psP[pb][:, :],
                        Wdn[it // 4][:, (it % 4) * 1024 + dt * 128:
                                     (it % 4) * 1024 + (dt + 1) * 128],
                        Usb[:, it * 512:(it + 1) * 512],
                        start=(it == 0), stop=(it == 31)), w=w,
                        inc=("tE" if it == 31 else None))
                pt = c.val("tE")
                b = dt % 2
                c.dma("sync", f"due{b}", etl[b][:, :],
                      expdT[dt * 128:(dt + 1) * 128, chk * 512:(chk + 1) * 512],
                      w=(st["et"][b] or []))
                c.dma("sync", f"duf{b}", xft[b][:, :],
                      src_f32[dt * 128:(dt + 1) * 128,
                              chk * 512:(chk + 1) * 512],
                      w=(st["xf"][b] or []))
                zf = state.get("z_free") if dt == 0 else None
                c.v(lambda e, dt=dt, pb=pb, b=b: e.tensor_tensor(
                    ztile[:, :], psP[pb][:, :], etl[b][:, :], ALU.add),
                    w=[("tE", pt), (f"due{b}", c.val(f"due{b}")),
                       (f"duf{b}", c.val(f"duf{b}"))] + (zf or []))
                c.v(lambda e, dt=dt, b=b: e.tensor_tensor(
                    ztile[:, :], ztile[:, :], xft[b][:, :], ALU.add))
                zt = c.val("vE")
                st["psP"][pb] = [("vE", zt)]
                st["et"][b] = [("vE", zt)]
                st["xf"][b] = [("vE", zt)]
                return [("vE", zt)]

            emit_epilogue_chunk(c, cm, ep, chk, make_z, dst_f32, dst_bf, state)
            st["U"] = [("tE", c.val("tE"))]
            st["xb"] = [("tE", c.val("tE"))]
        c.bar()

    return logic



def phase_projepi(nc, cm, din, wkey, brow, src_f32, actT, dst_f32, dst_bf, E):
    ep = alloc_epilogue(nc, E, 512)
    u = _EP_UNIQ[0]
    Wsb = [E(nc.sbuf_tensor(f"pe{u}_w{i}", [128, D], BF16)) for i in range(NDT)]
    ytl = [E(nc.sbuf_tensor(f"pe{u}_y{i}", [128, 512], BF16)) for i in range(8)]
    xft = [E(nc.sbuf_tensor(f"pe{u}_xf{i}", [128, 512], F32)) for i in range(2)]
    bo_sb = E(nc.sbuf_tensor(f"pe{u}_bo", [1, D], F32))
    psP = [E(nc.psum_tensor(f"pe{u}_psP{i}", [128, 512], F32)) for i in range(2)]

    def logic(c):
        for dt in range(NDT):
            c.dma("gpsimd", "pw", Wsb[dt][:, :],
                  din[wkey].ap()[dt * 128:(dt + 1) * 128, :])
        if brow is not None:
            c.dma("sync", "cs", bo_sb[:, :],
                  din["bqkvo"].ap()[brow:brow + 1, :])
        wt = c.val("pw")
        bt = c.val("cs")
        state = {}
        yfree = [None] * 8
        xfree = [None, None]
        pfree = [None, None]
        for chk in range(T // 512):
            ready = []
            for et in range(NDT):
                ys = f"pyl{et}"
                c.dma("sync", ys, ytl[et][:, :],
                      actT[et * 128:(et + 1) * 128,
                           chk * 512:(chk + 1) * 512],
                      w=(yfree[et] or []))
                ready.append((ys, c.val(ys)))

            def make_z(c, dt, ztile, chk=chk, ready=ready):
                pb = dt % 2
                for et in range(NDT):
                    w = [ready[et], ("pw", wt)]
                    if et == 0:
                        w += (pfree[pb] or [])
                    c.t(lambda e, dt=dt, et=et, pb=pb: e.matmul(
                        psP[pb][:, :], Wsb[et][:, dt * 128:(dt + 1) * 128],
                        ytl[et][:, :], start=(et == 0),
                        stop=(brow is None and et == NDT - 1)), w=w,
                        inc=("tE" if et == NDT - 1 and brow is None else None))
                if brow is not None:
                    c.t(lambda e, dt=dt, pb=pb: e.matmul(
                        psP[pb][:, :], bo_sb[:, dt * 128:(dt + 1) * 128],
                        cm.ones_row_f512[:, :], start=False, stop=True),
                        w=[("cs", bt)], inc="tE")
                pt = c.val("tE")
                xb_ = dt % 2
                xs = f"pxf{xb_}"
                c.dma("sync", xs, xft[xb_][:, :],
                      src_f32[dt * 128:(dt + 1) * 128,
                              chk * 512:(chk + 1) * 512],
                      w=(xfree[xb_] or []))
                zf = state.get("z_free") if dt == 0 else None
                c.v(lambda e, dt=dt, pb=pb, xb_=xb_: e.tensor_tensor(
                    ztile[:, :], psP[pb][:, :], xft[xb_][:, :], ALU.add),
                    w=[("tE", pt), (xs, c.val(xs))] + (zf or []))
                zt = c.val("vE")
                pfree[pb] = [("vE", zt)]
                xfree[xb_] = [("vE", zt)]
                return [("vE", zt)]

            emit_epilogue_chunk(c, cm, ep, chk, make_z, dst_f32, dst_bf, state)
            for et in range(NDT):
                yfree[et] = [("tE", c.val("tE"))]
        c.bar()

    return logic



PHASE_ORDER = ("l0attn", "l0ffn", "l1attn", "l1ffn")


def build_program(phases=PHASE_ORDER):
    nc = bass.Bass("TRN2", target_bir_lowering=False)
    din = declare_inputs(nc)
    zA = nc.dram_tensor("zA", [D, T], F32, kind="Internal")
    zB = nc.dram_tensor("zB", [D, T], F32, kind="Internal")
    zAb = nc.dram_tensor("zAb", [D, T], BF16, kind="Internal")
    zBb = nc.dram_tensor("zBb", [D, T], BF16, kind="Internal")
    x_bf = nc.dram_tensor("x_bf", [D, T], BF16, kind="Internal")
    ymoe = nc.dram_tensor("ymoe", [D, T], F32,
                          kind=("ExternalOutput" if DEV.get("dump_ymoe")
                                else "Internal"))
    selT_dram = nc.dram_tensor("selT", [8, T], F32,
                               kind=("ExternalOutput" if DEV.get("dump_sel")
                                     else "Internal"))
    tdu = nc.dram_tensor("tdu", [4096, 2048], BF16, kind="Internal")
    attO = nc.dram_tensor("attO", [D, T], BF16,
                          kind=("ExternalOutput" if DEV.get("dump_attO")
                                else "Internal"))
    expdT = nc.dram_tensor("expdT", [D, T], F32, kind="Internal")
    mscr = nc.dram_tensor("mscr", [4, 17340672], BF16, kind="Internal")
    outT = nc.dram_tensor("outT", [D, T], F32, kind="ExternalOutput")

    slots = [(zA.ap(), zAb.ap()), (zB.ap(), zBb.ap())]
    cnt = {}
    with ExitStack() as top:
        pool = SemPool(nc, top)
        cm = Common(nc, top.enter_context)
        block = top.enter_context(nc.Block())

        with ExitStack() as ph:
            plogic = phase_prep(nc, cm, din, din["xT"].ap(), x_bf.ap(), tdu,
                                ph.enter_context)

            def first(c):
                emit_common(c, cm, din)
                plogic(c)

            cnt = run_phase(block, pool, cnt, first)

        cur_f, cur_b = din["xT"].ap(), x_bf.ap()
        for pi, pname in enumerate(phases):
            last = pi == len(phases) - 1
            dst_f, dst_b = (outT.ap(), None) if last else slots[pi % 2]
            if pname == "l0attn":
                with ExitStack() as ph:
                    logic = phase_l0attn(nc, cm, din, cur_f, cur_b, mscr,
                                         attO.ap(), ph.enter_context)
                    cnt = run_phase(block, pool, cnt, logic)
                with ExitStack() as ph:
                    logic = phase_projepi(nc, cm, din, "WoT", 3, cur_f,
                                          attO.ap(), dst_f, dst_b,
                                          ph.enter_context)
                    cnt = run_phase(block, pool, cnt, logic)
            elif pname == "l0ffn":
                with ExitStack() as ph:
                    logic = phase_l0ffn_gate(nc, cm, din, cur_f, selT_dram,
                                             ph.enter_context)
                    cnt = run_phase(block, pool, cnt, logic)
                with ExitStack() as ph:
                    logic = phase_l0ffn_experts(nc, cm, din, cur_b, ymoe,
                                                selT_dram, ph.enter_context)
                    cnt = run_phase(block, pool, cnt, logic)
                with ExitStack() as ph:
                    logic = phase_l0ffn_epi(nc, cm, din, cur_f, ymoe, dst_f,
                                            dst_b, ph.enter_context)
                    cnt = run_phase(block, pool, cnt, logic)
            elif pname == "l1ffn":
                with ExitStack() as ph:
                    logic = phase_l1ffn_peer(nc, cm, din, cur_f, cur_b,
                                             tdu, expdT.ap(),
                                             ph.enter_context)
                    cnt = run_phase(block, pool, cnt, logic)
                with ExitStack() as ph:
                    logic = phase_l1ffn_dense(nc, cm, din, cur_f, cur_b,
                                              expdT.ap(), dst_f, dst_b,
                                              ph.enter_context)
                    cnt = run_phase(block, pool, cnt, logic)
            elif pname == "l1attn":
                with ExitStack() as ph:
                    logic = phase_l1attn_main(nc, cm, din, cur_b, attO.ap(),
                                              ph.enter_context)
                    cnt = run_phase(block, pool, cnt, logic)
                with ExitStack() as ph:
                    logic = phase_projepi(nc, cm, din, "w_out", None, cur_f,
                                          attO.ap(), dst_f, dst_b,
                                          ph.enter_context)
                    cnt = run_phase(block, pool, cnt, logic)
            else:
                raise NotImplementedError(pname)
            cur_f, cur_b = dst_f, dst_b

        def fin(c):
            c.bar()

        cnt = run_phase(block, pool, cnt, fin)
    return nc


def kernel(**inputs):
    sh = host_prep(inputs)
    x = np.asarray(inputs["L0_attn_x"])
    nc = build_program()
    in_maps = []
    for b in range(8):
        m = dict(sh)
        m["xT"] = np.ascontiguousarray(x[b].T.astype(np.float32))
        in_maps.append(m)
    res = run_bass_kernel_spmd(nc, in_maps, core_ids=list(range(8)))
    out = np.empty((8, T, D), np.float32)
    for b in range(8):
        out[b] = res.results[b]["outT"].T
    return out
